# version 12
# baseline (speedup 1.0000x reference)
import numpy as np
import ml_dtypes
import concourse.bass as bass
import concourse.mybir as mybir
from concourse.bass_utils import run_bass_kernel_spmd
from contextlib import ExitStack

F32 = mybir.dt.float32
BF16 = mybir.dt.bfloat16
U8 = mybir.dt.uint8
AF = mybir.ActivationFunctionType
ALU = mybir.AluOpType
AX = mybir.AxisListType.X
BF16NP = ml_dtypes.bfloat16

ENGS = ("sync", "tensor", "vector", "scalar", "gpsimd")

D = 1024
T = 2048
NH = 16
DH = 64
NE = 8
IE = 512
SHI = 1024
I1 = 2816
NCH = D // 128


def tap(x):
    return x.tensor if hasattr(x, "tensor") else x


class P:

    def __init__(self, nc):
        self.nc = nc
        self.q = {e: [] for e in ENGS}
        self.cnt = {}
        self.wm = {e: {} for e in ENGS}
        self.sems = {}
        self.semnames = []
        self.lastop = {}

    def sem(self, name):
        if name not in self.cnt:
            self.cnt[name] = 0
            self.semnames.append(name)
        return name

    BARRIER_SEMS = ("pe", "dv", "ac", "gp")

    def wait(self, eng, toks):
        for t in toks:
            if t is None:
                continue
            sname, val, peng = t
            if peng == eng and eng != "gpsimd":
                continue
            if sname in self.BARRIER_SEMS:
                val = self.cnt[sname]
                if val == 0:
                    continue
            if self.wm[eng].get(sname, 0) >= val:
                continue
            self.wm[eng][sname] = val
            self.q[eng].append(
                lambda E, s=sname, v=val: E.wait_ge(self.sems[s], v))

    SERIAL = {"vector": "dv", "scalar": "ac", "gpsimd": "gp"}

    def op(self, eng, fn, wait=(), inc=None, n=1, exact=False):
        ser = self.SERIAL.get(eng)
        if ser is not None:
            lt = self.lastop.get(eng)
            if lt is not None:
                self.wait(eng, [(lt[0], lt[1], "__self__")])
            inc = (ser + "x") if exact else ser
            self.sem(inc)
            n = 1
        self.wait(eng, wait)
        tok = None
        if inc is not None:
            self.sem(inc)
            self.cnt[inc] += n
            tok = (inc, self.cnt[inc], eng)
            self.q[eng].append(
                lambda E, f=fn, s=inc, v=n: f(E).then_inc(self.sems[s], v))
        else:
            self.q[eng].append(lambda E, f=fn: f(E))
        if ser is not None:
            self.lastop[eng] = tok
        return tok

    STREAM_ALIAS = {}

    def dma(self, eng, out, in_, wait=(), stream=None, **kw):
        stream = stream if stream is not None else eng
        s = "q_" + self.STREAM_ALIAS.get(stream, stream)
        self.sem(s)
        if self.cnt[s] > 0:
            self.wait(eng, [(s, self.cnt[s], "__dma__")])
        self.wait(eng, wait)
        self.cnt[s] += 16
        tok = (s, self.cnt[s], "__dma__")
        self.q[eng].append(
            lambda E, o=out, i=in_, sn=s, kw=kw:
            E.dma_start(o, i, **kw).then_inc(self.sems[sn], 16))
        return tok

    def barrier(self, toks):
        for eng in ENGS:
            self.wait(eng, toks)

    def finish(self, ctx):
        nc = self.nc
        for name in self.semnames:
            assert self.cnt[name] < 60000, (name, self.cnt[name])
            self.sems[name] = ctx.enter_context(nc.semaphore(name))
        block = ctx.enter_context(nc.Block())
        for eng in ENGS:
            ops = self.q[eng]
            if not ops:
                continue
            def replay(E, ops=ops):
                for f in ops:
                    f(E)
            getattr(block, eng)(replay)


def emit_consts(p, env):
    g = "gpsimd"
    p.op(g, lambda E: E.memset(env["ones_f1024"][:, :], 1.0 / 1024.0))
    p.op(g, lambda E: E.memset(env["ones_b1024"][:, :], 1.0 / 1024.0))
    p.op(g, lambda E: E.memset(env["ones_r"][:, :], 1.0))
    p.op(g, lambda E: E.memset(env["eps"][:, :], 1e-5))
    t = p.op(g, lambda E: E.memset(env["identf"][:, :], 1.0), inc="gp")
    t = p.op(g, lambda E: E.affine_select(
        env["identf"][:, :], env["identf"][:, :], [[1, 128]], ALU.is_equal, 0.0,
        channel_multiplier=-1), wait=[t], inc="gp")
    if "cidx8" in env:
        p.op(g, lambda E: E.iota(env["cidx8"][:, :], [[0, 16], [1, 8]],
                                 channel_multiplier=0,
                                 allow_small_or_imprecise_dtypes=True))
        p.op(g, lambda E: E.iota(env["cidx4"][:, :], [[0, 16], [1, 4]],
                                 channel_multiplier=0,
                                 allow_small_or_imprecise_dtypes=True))
        p.op(g, lambda E: E.iota(env["cidx8P"][:, :], [[0, 16], [1, 8]],
                                 base=90, channel_multiplier=0,
                                 allow_small_or_imprecise_dtypes=True))
        p.op(g, lambda E: E.iota(env["cidx4P"][:, :], [[0, 16], [1, 4]],
                                 base=90, channel_multiplier=0,
                                 allow_small_or_imprecise_dtypes=True))
    if "identb" in env:
        t2 = p.op(g, lambda E: E.memset(env["identb"][:, :], 1.0), inc="gp")
        p.op(g, lambda E: E.affine_select(
            env["identb"][:, :], env["identb"][:, :], [[1, 128]], ALU.is_equal,
            0.0, channel_multiplier=-1), wait=[t2], inc="gp")
    return p.op(g, lambda E: E.engine_nop(), inc="gp")


def emit_ln(p, env, src_sb, dst_sb, ntok, pbanks, wait=(), provider=None,
            dst_provider=None, provider_done=None):
    if provider is None:
        def provider(ch, ng, phase):
            c0 = ng * 512
            return src_sb[:, ch * ntok + c0: ch * ntok + c0 + 512], None
    row = env["row"]
    xsq = env["xsq"]
    mean_ps, msq_ps = pbanks["mean"], pbanks["msq"]
    bA, bB = pbanks["bA"], pbanks["bB"]
    NG = ntok // 512
    out_toks = {}
    mm_toks = {}
    row_free = {}
    for ng in range(NG):
        c0 = ng * 512
        par = 0
        mrow = row[0:1, 0:512]
        vrow = row[0:1, 512:1024]
        rrow = row[0:1, 1024:1536]
        nrow = row[0:1, 1536:2048]
        for ch in range(NCH):
            xs, xtok = provider(ch, ng, 0)
            xpar = (ng * NCH + ch) % 2
            xq = xsq[:, xpar * 512:(xpar + 1) * 512]
            sq = p.op("scalar", lambda E, i=xs, o=xq: E.activation(
                o, i, AF.Square),
                wait=list(wait) + [mm_toks.get(ng * NCH + ch - 2), xtok],
                inc="ac")
            mt = p.op("tensor", lambda E, i=xs, ch=ch: E.matmul(
                mean_ps[0:1, 0:512], env["ones_f1024"][:, 0:1], i,
                start=(ch == 0), stop=(ch == NCH - 1)),
                wait=(list(wait) + [row_free.get(("m", par)), xtok]
                      if ch == 0 else [xtok]), inc="pe")
            mm_toks[ng * NCH + ch] = p.op(
                "tensor", lambda E, xq=xq, ch=ch: E.matmul(
                    msq_ps[0:1, 0:512], env["ones_b1024"][:, 0:1], xq,
                    start=(ch == 0), stop=(ch == NCH - 1)),
                wait=([sq, row_free.get(("m", par))] if ch == 0 else [sq]),
                inc="pe")
            if provider_done is not None:
                provider_done(ch, ng, 0, [sq, mt, mm_toks[ng * NCH + ch]])
        stat_tok = mm_toks[ng * NCH + NCH - 1]
        p.op("vector", lambda E, mrow=mrow: E.tensor_copy(
            mrow, mean_ps[0:1, 0:512]),
            wait=[stat_tok, row_free.get(("bc", ng - 1))])
        p.op("vector", lambda E, mrow=mrow, vrow=vrow: E.tensor_mul(
            vrow, mrow, mrow))
        t3 = p.op("vector", lambda E, vrow=vrow: E.tensor_sub(
            vrow, msq_ps[0:1, 0:512], vrow), inc="dv")
        t4 = p.op("scalar", lambda E, vrow=vrow, rrow=rrow: E.activation(
            rrow, vrow, AF.Sqrt, bias=env["eps"][0:1, 0:1]),
            wait=[t3], inc="ac")
        t5 = p.op("vector", lambda E, rrow=rrow: E.reciprocal(rrow, rrow),
                  wait=[t4], inc="dv")
        t6 = p.op("vector", lambda E, mrow=mrow, rrow=rrow, nrow=nrow:
                  E.scalar_tensor_tensor(nrow, mrow, -1.0, rrow,
                                         ALU.mult, ALU.mult), inc="dv")
        row_free[("m", par)] = t6
        war = [out_toks.get((NCH - 1, ng - 1))]
        tb1 = p.op("tensor", lambda E, rrow=rrow: E.matmul(
            bA[:, 0:512], env["ones_r"][0:1, 0:128], rrow,
            start=True, stop=True), wait=[t5] + war, inc="pe")
        tb2 = p.op("tensor", lambda E, nrow=nrow: E.matmul(
            bB[:, 0:512], env["ones_r"][0:1, 0:128], nrow,
            start=True, stop=True), wait=[t6] + war, inc="pe")
        row_free[("bc", ng)] = tb2
        for ch in range(NCH):
            xs, xtok = provider(ch, ng, 1)
            if dst_provider is not None:
                ds, dwait, fin = dst_provider(ch, ng)
            else:
                ds = dst_sb[:, ch * ntok + c0: ch * ntok + c0 + 512]
                dwait, fin = [], None
            sc = env["lnsc"][:, 0:512]
            p.op("vector", lambda E, xs=xs, sc=sc: E.tensor_tensor(
                sc, xs, bA[:, 0:512], ALU.mult),
                wait=[tb1, tb2, xtok] + list(dwait))
            out_toks[(ch, ng)] = p.op(
                "vector", lambda E, sc=sc, ds=ds: E.tensor_tensor(
                    ds, sc, bB[:, 0:512], ALU.add), exact=True)
            if fin is not None:
                out_toks[(ch, ng)] = fin(out_toks[(ch, ng)])
            if provider_done is not None:
                provider_done(ch, ng, 1, [out_toks[(ch, ng)]])
    last = out_toks[(NCH - 1, NG - 1)]
    return (lambda ch, ng: out_toks[(ch, ng)]), last


def emit_l0_gating(p, env, xn_sb, xn_tok, pbL, pbT, dbg=None):
    cst = env.get("cst")
    """Routing weights -> wdram [NE*T] f32 (w-hat per expert, token order)."""
    sfcT = env["sfcT"]
    sg = env["sgate"]
    swT = env["swT"]
    sgate_t = tap(env["sgate"])
    W = env["sgate_w"]
    NG = T // 512
    stoks = []
    for ng in range(NG):
        c0 = ng * 512
        lt = None
        for ch in range(NCH):
            lt = p.op("tensor", lambda E, ch=ch, c0=c0: E.matmul(
                pbL[0:8, 0:512], env["sgw"][:, ch * 8:(ch + 1) * 8],
                xn_sb[:, ch * T + c0: ch * T + c0 + 512],
                start=(ch == 0), stop=(ch == NCH - 1)),
                wait=([xn_tok(ch, ng)] + ([stoks[-1]] if stoks else [])
                      if ch == 0 else [xn_tok(ch, ng)]),
                inc=("pe" if ch == NCH - 1 else None))
        stoks.append(p.op("scalar", lambda E, c0=c0: E.activation(
            sfcT[0:8, c0:c0 + 512], pbL[0:8, 0:512], AF.Sigmoid),
            wait=[lt], inc="ac"))
    tb = p.op("vector", lambda E: E.tensor_scalar_add(
        sfcT[0:8, :], sfcT[0:8, :], env["sgb"][0:8, 0:1]),
        wait=stoks, inc="dv")
    if dbg == "gate1":
        return [tb]
    ttoks = []
    for tt in range(16):
        ttoks.append(p.op("tensor", lambda E, tt=tt: E.transpose(
            pbT[:, tt * 8:(tt + 1) * 8], sfcT[0:8, tt * 128:(tt + 1) * 128],
            env["identf"][0:8, 0:8]), wait=[tb], inc="pe"))
    sfc = sg[:, 0:128]
    tcp = p.op("vector", lambda E: E.tensor_copy(sfc, pbT[:, 0:128]),
               wait=[ttoks[-1], cst])
    if dbg == "gate2":
        return [tcp]
    G_off = 128
    G = sg[:, G_off:G_off + 64]
    even = bass.AP(sgate_t, 0, [[W, 128], [8, 16], [2, 4]])
    odd = bass.AP(sgate_t, 1, [[W, 128], [8, 16], [2, 4]])
    p.op("vector", lambda E: E.tensor_tensor(G, even, odd, ALU.add))

    def top2(src_off, width, inner, cidx, cidxP, so):
        ng_ = width // inner
        m1 = sg[:, so:so + ng_]
        srcap = sg[:, src_off:src_off + width]
        src3 = bass.AP(sgate_t, src_off, [[W, 128], [inner, ng_], [1, inner]])
        m13 = bass.AP(sgate_t, so, [[W, 128], [1, ng_], [0, inner]])
        cand = sg[:, so + ng_:so + ng_ + width]
        cand3 = bass.AP(sgate_t, so + ng_, [[W, 128], [inner, ng_], [1, inner]])
        i1 = sg[:, so + ng_ + width:so + 2 * ng_ + width]
        i13 = bass.AP(sgate_t, so + ng_ + width, [[W, 128], [1, ng_], [0, inner]])
        f1_off = so + 2 * ng_ + width
        f1 = sg[:, f1_off:f1_off + width]
        t2_off = f1_off + width
        t2 = sg[:, t2_off:t2_off + width]
        t23 = bass.AP(sgate_t, t2_off, [[W, 128], [inner, ng_], [1, inner]])
        mf_off = t2_off + width
        mf = sg[:, mf_off:mf_off + width]
        p.op("vector", lambda E: E.tensor_reduce(m1, src3, AX, ALU.max))
        p.op("vector", lambda E: E.tensor_tensor(mf, srcap, m13, ALU.is_equal))
        p.op("vector", lambda E: E.scalar_tensor_tensor(
            cand, mf, -90.0, cidxP, ALU.mult, ALU.add))
        p.op("vector", lambda E: E.tensor_reduce(i1, cand3, AX, ALU.min))
        p.op("vector", lambda E: E.tensor_tensor(f1, cidx, i13, ALU.is_equal))
        p.op("vector", lambda E: E.scalar_tensor_tensor(
            t2, f1, -1e30, srcap, ALU.mult, ALU.add))
        p.op("vector", lambda E: E.tensor_reduce(m1, t23, AX, ALU.max))
        p.op("vector", lambda E: E.tensor_tensor(mf, t2, m13, ALU.is_equal))
        p.op("vector", lambda E: E.scalar_tensor_tensor(
            cand, mf, -90.0, cidxP, ALU.mult, ALU.add))
        p.op("vector", lambda E: E.tensor_reduce(i1, cand3, AX, ALU.min))
        p.op("vector", lambda E: E.tensor_tensor(t2, cidx, i13, ALU.is_equal))
        p.op("vector", lambda E: E.tensor_add(f1, f1, t2))
        return f1_off

    mG_off = top2(G_off, 64, 4, env["cidx4"][:, :], env["cidx4P"][:, :], 192)
    tmp_off = 512
    tmp = sg[:, tmp_off:tmp_off + 128]
    mGexp = bass.AP(sgate_t, mG_off, [[W, 128], [4, 16], [1, 4], [0, 2]])
    p.op("vector", lambda E: E.tensor_tensor(tmp, sfc, mGexp, ALU.mult))
    mE_off = top2(tmp_off, 128, 8, env["cidx8"][:, :], env["cidx8P"][:, :],
                  640)
    mE = sg[:, mE_off:mE_off + 128]
    ms_off = 1200
    msfc = sg[:, ms_off:ms_off + 128]
    sw_off = ms_off + 128
    sumw = sg[:, sw_off:sw_off + 16]
    sum3 = bass.AP(sgate_t, sw_off, [[W, 128], [1, 16], [0, 8]])
    msfc3 = bass.AP(sgate_t, ms_off, [[W, 128], [8, 16], [1, 8]])
    wh_off = sw_off + 16
    wh = sg[:, wh_off:wh_off + 128]
    p.op("vector", lambda E: E.tensor_mul(msfc, mE, sfc))
    p.op("vector", lambda E: E.tensor_reduce(sumw, msfc3, AX, ALU.add))
    p.op("vector", lambda E: E.tensor_scalar_add(sumw, sumw, 1e-20))
    p.op("vector", lambda E: E.reciprocal(sumw, sumw))
    twh = p.op("vector", lambda E: E.scalar_tensor_tensor(
        wh, msfc, 2.5, sum3, ALU.mult, ALU.mult), inc="dv")
    if dbg == "gate3":
        return [twh]
    dts = []
    tc_hist = {}
    whc_off = wh_off + 128
    for e in range(NE):
        whe = bass.AP(sgate_t, wh_off + e, [[W, 128], [8, 16]])
        whc = sg[:, whc_off + (e % 2) * 16: whc_off + (e % 2) * 16 + 16]
        tg2 = p.op("vector", lambda E, whe=whe, whc=whc: E.tensor_copy(
            whc, whe), wait=[twh])
        sl = e % 4
        tt = p.op("tensor", lambda E, sl=sl, whc=whc: E.transpose(
            pbT[0:16, sl * 128:(sl + 1) * 128], whc, env["identf"][:, :]),
            wait=[tg2, tc_hist.get(sl)], inc="pe")
        tc = p.op("vector", lambda E, e=e, sl=sl: E.tensor_copy(
            swT[0:16, e * 128:(e + 1) * 128],
            pbT[0:16, sl * 128:(sl + 1) * 128]), wait=[tt], inc="dv")
        tc_hist[sl] = tc
        dt = p.dma("gpsimd", env["wdram"][e * T:(e + 1) * T],
                   swT[0:16, e * 128:(e + 1) * 128],
                   wait=[tc, dts[e - 4] if e >= 4 else None],
                   stream="wdr%d" % (e % 4))
        dts.append(dt)
    return dts


def emit_l0ffn(p, env, x_sb, xn_sb, xn_tok, ln_done, first_call):
    NG = T // 512
    pb = env["pb"]
    gate_toks = emit_l0_gating(p, env, xn_sb, xn_tok, pb[2], pb[1])
    pbG = [pb[0], pb[1]]
    pbU = [pb[2], pb[3]]
    pbB = pb[4]
    pbY = [pb[5], pb[6]]
    swg, swu, swd = env["swg"], env["swu"], env["swd"]
    ssig = env["ssig"]
    shu = env["shu"]
    swrow = env["swrow"]

    ew_free = {}
    def load_w(kind, e, first_use):
        slot = e % 2
        buf = {"g": swg, "u": swu, "d": swd}[kind]
        width = {"g": 8 * 512, "u": 8 * 512, "d": 4 * 8 * 128}[kind]
        dst = buf[:, slot * width:(slot + 1) * width]
        qd = env["qdram_" + kind]
        wait = ew_free.get((kind, slot), [])
        if first_use:
            hsrc = {"g": env["hwg"], "u": env["hwu"], "d": env["hwd"]}[kind]
            if kind in ("g", "u"):
                hap = hsrc.rearrange("e (c p) i -> e p c i", p=128)[e]
                dstr = dst.rearrange("p (c i) -> p c i", c=NCH)
            else:
                hap = hsrc.rearrange("e (c p) (h m) -> e p c h m",
                                     p=128, m=128)[e]
                dstr = dst.rearrange("p (c h m) -> p c h m", c=4, h=NCH)
            t1 = p.dma("gpsimd", dstr, hap, wait=wait,
                       stream="hw%s%d" % (kind, slot))
            t2 = p.dma("gpsimd", qd[e], dst, wait=[t1],
                       stream="qs%s%d" % (kind, slot))
            env.setdefault("qstore", {})[(kind, e)] = t2
            return t1, slot
        return p.dma("sync", dst, qd[e], wait=wait,
                     stream="w%s%d" % (kind, slot)), slot

    x_upd = {}
    hu_free = {}
    acc_free = {}
    for ng in range(NG):
        c0 = ng * 512
        xn_ng = [xn_tok(ch, ng) for ch in range(NCH)]
        first_use = first_call and ng == 0
        wt = {}
        acc_toks = {}
        for e in range(10):
            slot = e % 2
            if e == 0:
                wt[0] = (load_w("g", 0, first_use), load_w("u", 0, first_use),
                         load_w("d", 0, first_use))
            if e + 1 < 10:
                wt[e + 1] = (load_w("g", e + 1, first_use),
                             load_w("u", e + 1, first_use),
                             load_w("d", e + 1, first_use))
            routed = e < NE
            if routed:
                rw = p.dma("sync", swrow[0:1, slot * 512:(slot + 1) * 512],
                           env["wdram"][e * T + c0: e * T + c0 + 512],
                           wait=[gate_toks[e], ew_free.get(("row", slot))],
                           stream="row%d" % slot)
                tB = p.op("tensor", lambda E, slot=slot: E.matmul(
                    pbB[:, 0:512], env["ones_r"][0:1, 0:128],
                    swrow[0:1, slot * 512:(slot + 1) * 512], start=True,
                    stop=True),
                    wait=[rw, ew_free.get(("B",)), ln_done,
                          gate_toks[-1] if e == 0 else None],
                    inc="pe")
                ew_free[("row", slot)] = tB
            hu_toks = {}
            tg = tu = None
            for ich in range(4):
                gb = pbG[ich % 2]
                ub = pbU[ich % 2]
                sgslot = ich % 2
                sig = ssig[:, sgslot * 512:(sgslot + 1) * 512]
                for ch in range(NCH):
                    o = slot * 8 * 512 + ch * 512 + ich * 128
                    rhs = xn_sb[:, ch * T + c0: ch * T + c0 + 512]
                    w0 = ([wt[e][0][0], wt[e][1][0], ln_done,
                           ew_free.get(("gu", ich % 2))] + xn_ng
                          if ch == 0 else [])
                    tg = p.op("tensor", lambda E, o=o, r=rhs, ch=ch, gb=gb:
                              E.matmul(gb[:, 0:512], swg[:, o:o + 128], r,
                                       start=(ch == 0), stop=(ch == NCH - 1)),
                              wait=w0, inc=("pw" if ch == NCH - 1 else None))
                    tu = p.op("tensor", lambda E, o=o, r=rhs, ch=ch, ub=ub:
                              E.matmul(ub[:, 0:512], swu[:, o:o + 128], r,
                                       start=(ch == 0), stop=(ch == NCH - 1)),
                              inc=("pw" if ch == NCH - 1 else None))
                ts = p.op("scalar", lambda E, gb=gb, sig=sig: E.activation(
                    sig, gb[:, 0:512], AF.Silu),
                    wait=[tg, ew_free.get(("sig", sgslot))], exact=True)
                hu = shu[:, ich * 512:(ich + 1) * 512]
                if routed:
                    tm = p.op("vector", lambda E, ub=ub, sig=sig:
                              E.tensor_tensor(env["stmp"][:, 0:512],
                                              ub[:, 0:512], sig, ALU.mult),
                              wait=[ts, tu, hu_free.get(ich)])
                    th = p.op("vector", lambda E, hu=hu: E.tensor_tensor(
                        hu, env["stmp"][:, 0:512], pbB[:, 0:512], ALU.mult),
                        wait=[tB], exact=True)
                else:
                    th = p.op("vector", lambda E, hu=hu, ub=ub, sig=sig:
                              E.tensor_tensor(hu, ub[:, 0:512], sig, ALU.mult),
                              wait=[ts, tu, hu_free.get(ich)], exact=True)
                hu_toks[ich] = th
                ew_free[("gu", ich % 2)] = th
                ew_free[("sig", sgslot)] = th
                if routed:
                    ew_free[("B",)] = th
            dslot = wt[e][2][1]
            tlast = None
            for H in range(NCH):
                yb = pbY[H % 2]
                for ich in range(4):
                    o = dslot * 4 * 8 * 128 + (ich * 8 + H) * 128
                    hu = shu[:, ich * 512:(ich + 1) * 512]
                    w0 = ([wt[e][2][0], acc_free.get(H % 2)] +
                          list(hu_toks.values()) if ich == 0 else [])
                    tlast = p.op(
                        "tensor",
                        lambda E, o=o, hu=hu, f=(ich == 0), s=(ich == 3),
                        yb=yb: E.matmul(yb[:, 0:512], swd[:, o:o + 128], hu,
                                        start=f, stop=s),
                        wait=w0, inc=("pw" if ich == 3 else None))
                xs = x_sb[:, H * T + c0: H * T + c0 + 512]
                tv = p.op("vector", lambda E, yb=yb, xs=xs:
                          E.scalar_tensor_tensor(
                              xs, yb[:, 0:512], 0.5, xs, ALU.mult, ALU.add),
                          wait=[tlast], exact=True)
                if e == 9:
                    x_upd[(H, ng)] = tv
                acc_free[H % 2] = tv
            for ich in range(4):
                hu_free[ich] = tlast
            ew_free[("g", slot)] = [tg, env.get("qstore", {}).get(("g", e))]
            ew_free[("u", slot)] = [tu, env.get("qstore", {}).get(("u", e))]
            ew_free[("d", dslot)] = [tlast,
                                     env.get("qstore", {}).get(("d", e))]
    return x_upd


def emit_l0attn(p, env, x_sb, xn_sb, xn_tok, xnp_dram, xnp_toks, ln_done,
                npairs=8, nphase=4):
    pb = env["pb"]
    pbS, pbA, pbO = pb[0], [pb[1], pb[2]], [pb[3], pb[4]]
    pbD, pbB, pbP = pb[5], pb[6], [pb[6], pb[7]]
    sQ, sKT = env["sQ"], env["sKT"]
    sKtok, sV = env["sKtok"], env["sV"]
    sxp = env["sxp"]
    sS = env["sS"]
    sAm = env["sAm"]
    srcp = env["srcp"]
    sOutE = env["sOutE"]
    sAtt = env["sAtt"]
    escr = env["escr"]
    sw = env["swpair"]
    WSLOT = 5 * 8 * 128
    NT = T // 128
    prev = {}
    x_att = {}

    def elu1_evac(psum_ap, dst_ap, par, wait, width, rows=128):
        s1 = escr[0:rows, par * 512:par * 512 + width]
        t1 = p.op("vector", lambda E: E.tensor_scalar(
            s1, psum_ap, 0.0, None, ALU.min), wait=wait)
        t2 = p.op("scalar", lambda E: E.activation(s1, s1, AF.Exp), wait=[t1])
        t3 = p.op("vector", lambda E: E.scalar_tensor_tensor(
            dst_ap, psum_ap, 0.0, s1, ALU.max, ALU.add), wait=[t2], inc="dv")
        return t3

    for pr in range(npairs):
        par = pr % 2
        wofs = par * WSLOT
        wtok = {}
        for wi, hname in enumerate(("hwqT", "hwkT", "hwvT")):
            dst = sw[:, wofs + wi * 1024: wofs + wi * 1024 + 1024]
            wtok[wi] = p.dma(
                "gpsimd", dst.rearrange("p (c m) -> p c m", c=NCH),
                env[hname].rearrange("(c p) o -> p c o", p=128)[
                    :, :, 128 * pr:128 * pr + 128],
                wait=[prev.get(("wfree", par))],
                stream="wat%s%d" % (hname[2], par))
        wo_dst = sw[:, wofs + 3 * 1024: wofs + 3 * 1024 + 1024]
        wtok[3] = p.dma(
            "gpsimd",
            bass.AP(tap(sw), wofs + 3 * 1024,
                    [[env["sw_w"], 64], [1024, 2], [1, 1024]]),
            env["hwoT"].rearrange("(r hh p) o -> p hh r o", p=64, hh=2)[
                :, :, pr], wait=[prev.get(("wfree", par))],
            stream="wao%d" % par)
        wq = lambda ch: sw[:, wofs + ch * 128: wofs + ch * 128 + 128]
        wk = lambda ch: sw[:, wofs + 1024 + ch * 128: wofs + 1024 + ch * 128 + 128]
        wv = lambda ch: sw[:, wofs + 2048 + ch * 128: wofs + 2048 + ch * 128 + 128]
        wo = lambda hh, H: sw[0:64, wofs + 3 * 1024 + hh * 1024 + H * 128:
                              wofs + 3 * 1024 + hh * 1024 + H * 128 + 128]

        sinit_done = None
        for tt in range(NT):
            xpar = tt % 2
            xp = sxp[:, xpar * 1024:(xpar + 1) * 1024]
            tld = p.dma("gpsimd", xp.rearrange("p (c m) -> p c m", c=NCH),
                        xnp_dram.rearrange("(c p) t -> p c t", p=128)[
                            :, :, tt * 128:(tt + 1) * 128],
                        wait=list(xnp_toks) + [prev.get(("xp", xpar))],
                        stream="xp%d" % xpar)
            tk = tv = None
            for ch in range(NCH):
                w0 = [tld, wtok[1], wtok[2], ln_done,
                      prev.get(("pp",))] if ch == 0 else [tld]
                tk = p.op("tensor", lambda E, ch=ch, xp=xp, w=wk(ch): E.matmul(
                    pbP[0][:, 0:128], xp[:, ch * 128:(ch + 1) * 128],
                    w, start=(ch == 0), stop=(ch == NCH - 1)),
                    wait=w0, inc=("pe" if ch == NCH - 1 else None))
                tv = p.op("tensor", lambda E, ch=ch, xp=xp, w=wv(ch): E.matmul(
                    pbP[1][:, 0:128], xp[:, ch * 128:(ch + 1) * 128],
                    w, start=(ch == 0), stop=(ch == NCH - 1)),
                    inc=("pe" if ch == NCH - 1 else None))
            prev[("xp", xpar)] = tv
            ek = elu1_evac(pbP[0][:, 0:128],
                           sKtok[:, tt * 128:(tt + 1) * 128], 0,
                           [tk, prev.get(("ktok", tt))], 128)
            vdst = bass.AP(tap(sV), tt * 130,
                           [[env["sV_w"], 128], [65, 2], [1, 64]])
            ev = p.op("vector", lambda E, vdst=vdst: E.tensor_copy(
                vdst, pbP[1][:, 0:128]),
                wait=[tv, prev.get(("vtok", tt))], inc="dv")
            prev[("pp",)] = ev
            kap = sKtok[:, tt * 128:(tt + 1) * 128]
            vap = sV[:, tt * 130:(tt + 1) * 130]
            sinit_done = p.op(
                "tensor", lambda E, kap=kap, vap=vap, tt=tt:
                E.matmul(pbS[:, 0:130], kap, vap, start=(tt == 0),
                         stop=(tt == NT - 1)),
                wait=([ek, ev, prev.get(("st",)), env.get("v1_tok")]
                      if tt == 0 else [ek, ev]),
                inc=("pe" if tt == NT - 1 else None))
        hm = p.op("vector", lambda E: E.tensor_scalar(
            env["sSf"][:, 0:130], pbS[:, 0:130], env["hmask"][:, 0:1], None,
            ALU.mult), wait=[sinit_done, prev.get(("sSread",))], inc="dv")
        hm = p.op("vector", lambda E: E.tensor_copy(
            sS[:, 0:130], env["sSf"][:, 0:130]), inc="dv")
        if nphase <= 1:
            x_att[(0, 0, pr)] = hm
            return {k: v for k, v in x_att.items() if k[2] == npairs - 1}

        qk_done = []
        for ng in range(4):
            c0 = ng * 512
            for hh in range(2):
                for dst, wfn, wt, pp in ((sQ, wq, wtok[0], 0),
                                         (sKT, wk, wtok[1], 1)):
                    tq = None
                    for ch in range(NCH):
                        w0 = ([wt, ln_done, hm, prev.get(("proj", pp))]
                              if ch == 0 else [])
                        w_ = wfn(ch)[:, hh * 64:(hh + 1) * 64]
                        tq = p.op("tensor", lambda E, ch=ch, c0=c0, pp=pp,
                                  w=w_: E.matmul(
                                      pbP[pp][0:64, 0:512], w,
                                      xn_sb[:, ch * T + c0:ch * T + c0 + 512],
                                      start=(ch == 0), stop=(ch == NCH - 1)),
                                  wait=w0,
                                  inc=("pe" if ch == NCH - 1 else None))
                    te = elu1_evac(
                        pbP[pp][0:64, 0:512],
                        dst[0:64, hh * T + c0: hh * T + c0 + 512], pp,
                        [tq], 512, rows=64)
                    prev[("proj", pp)] = te
                    qk_done.append(te)
        kv_done = []
        ek = ev = None
        for tt in range(NT):
            tk = tv = None
            for ch in range(NCH):
                xo = xn_sb[:, ch * T + tt * 128: ch * T + tt * 128 + 128]
                w0 = ([prev.get(("proj", 0)), prev.get(("proj", 1)), ek, ev]
                      if ch == 0 else [])
                tk = p.op("tensor", lambda E, ch=ch, xo=xo, w=wk(ch):
                          E.matmul(pbP[0][:, 0:128], xo, w, start=(ch == 0),
                                   stop=(ch == NCH - 1)), wait=w0,
                          inc=("pe" if ch == NCH - 1 else None))
                tv = p.op("tensor", lambda E, ch=ch, xo=xo, w=wv(ch):
                          E.matmul(pbP[1][:, 0:128], xo, w, start=(ch == 0),
                                   stop=(ch == NCH - 1)),
                          inc=("pe" if ch == NCH - 1 else None))
            ek = elu1_evac(pbP[0][:, 0:128],
                           sKtok[:, tt * 128:(tt + 1) * 128], 0, [tk], 128)
            vdst = bass.AP(tap(sV), tt * 130,
                           [[env["sV_w"], 128], [65, 2], [1, 64]])
            ev = p.op("vector", lambda E, vdst=vdst: E.tensor_copy(
                vdst, pbP[1][:, 0:128]), wait=[tv], inc="dv")
            prev[("ktok", tt)] = ek
            prev[("vtok", tt)] = ev
            kv_done.append(ek)
            kv_done.append(ev)

        if nphase <= 2:
            x_att[(0, 0, pr)] = kv_done[-1]
            return {k: v for k, v in x_att.items() if k[2] == npairs - 1}
        att_done = []
        st_cast = hm
        for c in range(NT):
            cpar = c % 2
            st_copy = st_cast
            ta = None
            for hh in range(2):
                ta = p.op("tensor", lambda E, hh=hh, c=c: E.matmul(
                    pbA[cpar][:, hh * 128:(hh + 1) * 128],
                    sKT[:, hh * T + c * 128: hh * T + (c + 1) * 128],
                    sQ[:, hh * T + c * 128: hh * T + (c + 1) * 128],
                    start=True, stop=True),
                    wait=([kv_done[-1], kv_done[-2], qk_done[-1],
                           prev.get(("Aread", cpar))] if hh == 0 else []),
                    inc=("pe" if hh == 1 else None))
            tm = p.op("vector", lambda E: E.tensor_tensor(
                sAm[:, 0:256], pbA[cpar][:, 0:256],
                bass.AP(tap(env["sUT"]), 0, [[128, 128], [0, 2], [1, 128]]),
                ALU.mult), wait=[ta], inc="dv")
            prev[("Aread", cpar)] = tm
            for hh in range(2):
                p.op("tensor", lambda E, hh=hh, c=c: E.matmul(
                    pbO[cpar][0:64, hh * 128:(hh + 1) * 128],
                    sS[:, hh * 65: hh * 65 + 64],
                    sQ[:, hh * T + c * 128: hh * T + (c + 1) * 128],
                    start=True, stop=False),
                    wait=([st_copy, tm, prev.get(("Oread", cpar))]
                          if hh == 0 else []),
                    inc="pe")
                p.op("tensor", lambda E, hh=hh, c=c: E.matmul(
                    pbO[cpar][0:64, hh * 128:(hh + 1) * 128],
                    sV[:, c * 130 + hh * 65: c * 130 + hh * 65 + 64],
                    sAm[:, hh * 128:(hh + 1) * 128],
                    start=False, stop=True), inc="pe")
            to = None
            for hh in range(2):
                p.op("tensor", lambda E, hh=hh, c=c: E.matmul(
                    pbD[0:1, hh * 128:(hh + 1) * 128],
                    sS[:, hh * 65 + 64: hh * 65 + 65],
                    sQ[:, hh * T + c * 128: hh * T + (c + 1) * 128],
                    start=True, stop=False),
                    wait=([prev.get(("Dread",))] if hh == 0 else []),
                    inc="pe")
                to = p.op("tensor", lambda E, hh=hh, c=c: E.matmul(
                    pbD[0:1, hh * 128:(hh + 1) * 128],
                    sV[:, c * 130 + hh * 65 + 64: c * 130 + hh * 65 + 65],
                    sAm[:, hh * 128:(hh + 1) * 128],
                    start=False, stop=True), inc="pe")
            stm = p.op("tensor", lambda E, c=c: E.matmul(
                pbS[:, 0:130], sKtok[:, c * 128:(c + 1) * 128],
                sV[:, c * 130:(c + 1) * 130], start=True, stop=True),
                wait=[prev.get(("sadd",)), hm], inc="pe")
            tadd = p.op("vector", lambda E: E.tensor_add(
                env["sSf"][:, 0:130], pbS[:, 0:130], env["sSf"][:, 0:130]),
                wait=[stm], inc="dv")
            prev[("sadd",)] = tadd
            st_cast = p.op("vector", lambda E: E.tensor_copy(
                sS[:, 0:130], env["sSf"][:, 0:130]), wait=[to], inc="dv")
            prev[("stmm", c)] = stm
            trm = p.op("vector", lambda E: E.tensor_scalar_max(
                srcp[0:1, 0:256], pbD[0:1, 0:256], 1e-6),
                wait=[to], inc="dv")
            prev[("Dread",)] = trm
            tr = p.op("vector", lambda E: E.reciprocal(
                srcp[0:1, 0:256], srcp[0:1, 0:256]), inc="dv")
            tbb = None
            for hh in range(2):
                tbb = p.op("tensor", lambda E, hh=hh: E.matmul(
                    pbB[0:64, hh * 128:(hh + 1) * 128],
                    env["ones_r"][0:1, 0:64],
                    srcp[0:1, hh * 128:(hh + 1) * 128],
                    start=True, stop=True),
                    wait=([tr, prev.get(("Bread",))] if hh == 0 else []),
                    inc=("pe" if hh == 1 else None))
            tev = p.op("scalar", lambda E: E.activation(
                sOutE[0:64, 0:256], pbO[cpar][0:64, 0:256], AF.Identity),
                wait=[to], inc="ac")
            prev[("Oread", cpar)] = tev
            tdv = None
            for hh in range(2):
                tdv = p.op("vector", lambda E, hh=hh, c=c: E.tensor_tensor(
                    sAtt[0:64, hh * T + c * 128: hh * T + (c + 1) * 128],
                    sOutE[0:64, hh * 128:(hh + 1) * 128],
                    pbB[0:64, hh * 128:(hh + 1) * 128], ALU.mult),
                    wait=[tbb, tev, prev.get(("attw",))], inc="dv")
            prev[("Bread",)] = tdv
            att_done.append(tdv)
        prev[("sSread",)] = att_done[-1]
        prev[("sadd",)] = prev.get(("sadd",))

        if nphase <= 3:
            x_att[(0, 0, pr)] = att_done[-1]
            return {k: v for k, v in x_att.items() if k[2] == npairs - 1}
        for ng in range(4):
            c0 = ng * 512
            for H in range(NCH):
                ypp = pbP[H % 2]
                tw_ = None
                for hh in range(2):
                    tw_ = p.op("tensor", lambda E, hh=hh, H=H, c0=c0,
                               ypp=ypp, w=wo(hh, H): E.matmul(
                                   ypp[:, 0:512], w,
                                   sAtt[0:64, hh * T + c0: hh * T + c0 + 512],
                                   start=(hh == 0), stop=(hh == 1)),
                               wait=([att_done[-1], wtok[3],
                                      prev.get(("yp", H % 2))]
                                     if hh == 0 else []),
                               inc=("pe" if hh == 1 else None))
                xs = x_sb[:, H * T + c0: H * T + c0 + 512]
                tx_ = p.op("vector", lambda E, xs=xs, ypp=ypp: E.tensor_add(
                    xs, ypp[:, 0:512], xs), wait=[tw_], inc="dv")
                prev[("yp", H % 2)] = tx_
                x_att[(H, ng, pr)] = tx_
        prev[("attw",)] = tx_
        prev[("wfree", par)] = tx_
        prev[("st",)] = tx_
        prev[("pp",)] = tx_
    return {k: v for k, v in x_att.items() if k[2] == npairs - 1}


MAGIC = 12582912.0

def emit_quant(p, env, src, dst, W, wait=(), src_f32=True, dst_wait=(),
               eng="vector", scr="", exact=False):
    nb = W // 16
    z, qa = env["qz" + scr][:, 0:W], env["qa" + scr][:, 0:W]
    qs = env["qs" + scr][:, 0:W]
    bm, sc = env["qbm" + scr][:, 0:nb], env["qsc" + scr][:, 0:nb]
    st = tap(src)
    src3 = bass.AP(st, src.offset, list(src.ap[:-1]) + [[16, nb], [1, 16]])
    bmt = tap(env["qbm" + scr])
    bm3 = bass.AP(bmt, 0, [[env["qbm_w" + scr], 128], [1, nb], [0, 16]])
    sct = tap(env["qsc" + scr])
    sc3 = bass.AP(sct, 0, [[env["qsc_w" + scr], 128], [1, nb], [0, 16]])
    qk = ("qprev", eng, scr)
    tr_ = p.op("vector", lambda E: E.tensor_reduce(
        bm, src3, AX, ALU.max, apply_absolute_value=True),
        wait=list(wait) + [env.get(("qbmw", eng, scr))], inc="dv")
    tsc = p.op(eng, lambda E: E.tensor_scalar(
        sc, bm, 1.0 / 6.0, 1e-12, ALU.mult, ALU.max), wait=[tr_])
    trc = p.op("vector", lambda E: E.reciprocal(bm, sc), wait=[tsc],
               inc="dv")
    t0 = p.op("scalar", lambda E: E.activation(qs, src, AF.Sign),
              wait=list(wait) + [env.get(qk)], inc="ac")
    t1 = p.op("scalar", lambda E: E.activation(z, src, AF.Abs),
              inc="ac")
    tzm = p.op(eng, lambda E: E.tensor_tensor(z, z, bm3, ALU.mult),
               wait=[t1, trc])
    env[("qbmw", eng, scr)] = tzm if eng != "vector" else None
    p.op(eng, lambda E: E.tensor_scalar(
        qa, z, 2.0, MAGIC, ALU.mult, ALU.add))
    p.op(eng, lambda E: E.tensor_scalar(
        qa, qa, MAGIC + 4.0, None, ALU.min))
    p.op(eng, lambda E: E.tensor_scalar(
        qa, qa, -MAGIC, 0.5, ALU.add, ALU.mult))
    p.op(eng, lambda E: E.tensor_tensor(qs, qs, sc3, ALU.mult),
         wait=[t0])
    qb = env["qb" + scr][:, 0:W]
    p.op(eng, lambda E: E.tensor_scalar(
        qb, z, MAGIC, MAGIC + 4.0, ALU.add, ALU.min))
    p.op(eng, lambda E: E.tensor_scalar(
        qb, qb, MAGIC + 2.0, -(MAGIC + 2.0), ALU.max, ALU.add))
    p.op(eng, lambda E: E.tensor_add(qa, qa, qb))
    p.op(eng, lambda E: E.tensor_scalar(
        qb, z, 5.0, 2.0, ALU.is_gt, ALU.mult))
    p.op(eng, lambda E: E.tensor_add(qa, qa, qb))
    tf = p.op(eng, lambda E: E.tensor_tensor(dst, qa, qs, ALU.mult),
              wait=list(dst_wait), exact=exact)
    env[qk] = tf
    return tf


QU_G0, QU_U0, QU_D0 = 0, 176, 352
QU_PER_CORE = 66


def _q_unit(kind, it, sub):
    if kind == "g":
        return QU_G0 + it * 8 + sub
    if kind == "u":
        return QU_U0 + it * 8 + sub
    return QU_D0 + it * 22 + sub


def emit_wquant_dist(p, env):
    qwbuf, qwout, souT = env["qwbuf"], env["qwout"], env["souT"]
    pbq = env["pbq"]
    wqsrc = env["wqsrc"]
    qown = env["qown"]
    strips = [(k * 4, 512) for k in range(16)] + [(64, 256)]
    tf = {}
    trp = {}
    cp = {}
    st = {}
    for k, (u0, W) in enumerate(strips):
        tl = p.dma("sync", qwbuf[:, 0:W], wqsrc[:, u0 * 128: u0 * 128 + W],
                   wait=[tf.get(k - 1)], stream="qwl0")
        tf[k] = emit_quant(p, env, qwbuf[:, 0:W], qwout[:, 0:W], W,
                           wait=[tl], dst_wait=[trp.get(k - 1)],
                           eng="vector")
        t = None
        for j in range(W // 128):
            t = p.op("tensor", lambda E, j=j: E.transpose(
                pbq[:, j * 128: j * 128 + 128],
                qwout[:, j * 128: j * 128 + 128],
                env["identb"][:, :]),
                wait=([tf[k], cp.get(k - 1), env["cst"]] if j == 0 else []),
                inc=("pe" if j == W // 128 - 1 else None))
        trp[k] = t
        cp[k] = p.op("vector", lambda E, W=W: E.tensor_copy(
            souT[:, 0:W], pbq[:, 0:W]),
            wait=[t, st.get(k - 1)], inc="dv")
        dst = bass.AP(tap(qown), u0 * 16384,
                      [[128, 128], [16384, W // 128], [1, 128]])
        src = souT[:, 0:W].rearrange("p (n m) -> p n m", m=128)
        st[k] = p.dma("sync", dst, src, wait=[cp[k]], stream="qws0")
    return list(st.values())


def emit_l1ffn(p, env, x_sb, xn_sb, xn_tok, ln_done, first_call, wq_toks):
    pb = env["pb"]
    pbt = env["pbt"]
    pbG = [pb[0], pb[1]]
    pbU = [pb[2], pb[3]]
    pbY = [pb[4], pb[5]]
    sxq, shq = env["sxq"], env["shq"]
    sxt = env["sxt"]
    ssil = env["ssil"]
    shts = env["shts"]
    shtq = env["shtq"]
    swg1, swu1, swd1 = env["swg1"], env["swu1"], env["swd1"]
    igroups = [(i * 512, 512) for i in range(5)] + [(2560, 256)]
    prev = {}
    x_upd = {}
    last_gu = {}
    sfx = "A" if first_call else "B"
    qdone = {}

    def xfwd(ng):
        c0 = ng * 512
        xn_ng = [xn_tok(ch, ng) for ch in range(NCH)]
        for tt in range(4):
            ttr = None
            for ch in range(NCH):
                ttr = p.op("tensor", lambda E, ch=ch, tt=tt, c0=c0: E.transpose(
                    pbt[:, ch * 128:(ch + 1) * 128],
                    xn_sb[:, ch * T + c0 + tt * 128: ch * T + c0 + tt * 128 + 128],
                    env["identb"][:, :]),
                    wait=(xn_ng + [ln_done, prev.get(("pbt",))]
                          if ch == 0 else []),
                    inc=("pe" if ch == NCH - 1 else None))
            sl = sxt[:, tt * 1024:(tt + 1) * 1024]
            tc = p.op("vector", lambda E, sl=sl: E.tensor_copy(
                sl, pbt[:, 0:1024]),
                wait=[ttr, prev.get(("sxt", tt))], exact=True)
            prev[("pbt",)] = tc
            q1 = emit_quant(p, env, sl[:, 0:512], sl[:, 0:512], 512,
                            wait=[tc], src_f32=False, exact=True)
            q2 = emit_quant(p, env, sl[:, 512:1024], sl[:, 512:1024], 512,
                            wait=[tc], src_f32=False, exact=True)
            qdone[tt] = (q1, q2)

    def xback(ng):
        for tt in range(4):
            sl = sxt[:, tt * 1024:(tt + 1) * 1024]
            ttr = None
            for ch in range(NCH):
                ttr = p.op("tensor", lambda E, sl=sl, ch=ch: E.transpose(
                    pbt[:, ch * 128:(ch + 1) * 128],
                    sl[:, ch * 128:(ch + 1) * 128], env["identb"][:, :]),
                    wait=(list(qdone[tt]) + [prev.get(("pbt",))]
                          if ch == 0 else []),
                    inc=("pe" if ch == NCH - 1 else None))
            prev[("sxt", tt)] = ttr
            sxq_dst = bass.AP(tap(sxq), tt * 128,
                              [[8 * 512, 128], [512, 8], [1, 128]])
            tc = p.op("vector", lambda E, d=sxq_dst: E.tensor_copy(
                d, pbt[:, 0:1024]),
                wait=[ttr, prev.get(("sxq", tt))], exact=True)
            prev[("pbt",)] = tc
            prev[("xq", tt)] = tc

    xfwd(0)
    xback(0)
    for ng in range(4):
        c0 = ng * 512
        for gi, (i0, w) in enumerate(igroups):
            wslot = gi % 2
            wg_tok = {}
            for kind, buf in (("g", swg1), ("u", swu1)):
                for ch in range(NCH):
                    t0 = i0 // 128
                    nt = w // 128
                    col0 = wslot * 8 * 512 + ch * 512
                    src = bass.AP(tap(env["qall"]),
                                  _q_unit(kind, t0, ch) * 16384,
                                  [[128, 128], [8 * 16384, nt], [1, 128]])
                    wg_tok[(kind, ch)] = p.dma(
                        "sync",
                        buf[:, col0:col0 + w].rearrange(
                            "p (n m) -> p n m", m=128),
                        src,
                        wait=[prev.get(("gw", kind, wslot, ch)),
                              env.get("qall_tok")],
                        stream="l1w%s%d%d" % (kind, wslot, ch))
            for tt in range(4):
                gb = pbG[tt % 2]
                ub = pbU[tt % 2]
                tg = tu = None
                for ch in range(NCH):
                    lh = bass.AP(tap(sxq), ch * 512 + tt * 128,
                                 [[8 * 512, 128], [1, 128]])
                    w0 = ([prev.get(("xq", tt)), wg_tok[("g", ch)],
                           wg_tok[("u", ch)], prev.get(("gub", tt % 2))]
                          if ch == 0
                          else ([wg_tok[("g", ch)], wg_tok[("u", ch)]]
                                if tt == 0 else []))
                    tg = p.op("tensor", lambda E, lh=lh, ch=ch, w=w, gb=gb,
                              o=wslot * 8 * 512 + ch * 512: E.matmul(
                                  gb[:, 0:w], lh, swg1[:, o:o + w],
                                  start=(ch == 0), stop=(ch == NCH - 1)),
                              wait=w0, inc=("pw" if ch == NCH - 1 else None))
                    tu = p.op("tensor", lambda E, lh=lh, ch=ch, w=w, ub=ub,
                              o=wslot * 8 * 512 + ch * 512: E.matmul(
                                  ub[:, 0:w], lh, swu1[:, o:o + w],
                                  start=(ch == 0), stop=(ch == NCH - 1)),
                              inc=("pw" if ch == NCH - 1 else None))
                if gi == 5:
                    last_gu[tt] = tu
                ts_ = p.op("scalar", lambda E, gb=gb, w=w: E.activation(
                    ssil[:, 0:w], gb[:, 0:w], AF.Silu), wait=[tg],
                    exact=True)
                tm = p.op("vector", lambda E, ub=ub, w=w: E.tensor_tensor(
                    shts[:, 0:w], ub[:, 0:w], ssil[:, 0:w], ALU.mult),
                    wait=[ts_, tu], exact=True)
                prev[("gub", tt % 2)] = tm
                for ch in range(NCH):
                    prev[("gw", "g", wslot, ch)] = tg
                    prev[("gw", "u", wslot, ch)] = tu
                qt = emit_quant(p, env, shts[:, 0:w], shtq[:, 0:w], w,
                                wait=[tm], exact=True)
                ttr = None
                for j in range(w // 128):
                    ttr = p.op("tensor", lambda E, j=j: E.transpose(
                        pbt[:, j * 128:(j + 1) * 128],
                        shtq[:, j * 128:(j + 1) * 128], env["identb"][:, :]),
                        wait=([qt, prev.get(("pbt",))] if j == 0 else []),
                        inc=("pe" if j == w // 128 - 1 else None))
                hq_dst = bass.AP(tap(shq), (i0 // 128) * 512 + tt * 128,
                                 [[22 * 512, 128], [512, w // 128], [1, 128]])
                tc = p.op("vector", lambda E, d=hq_dst, w=w: E.tensor_copy(
                    d, pbt[:, 0:w]),
                    wait=[ttr, prev.get(("hq", gi, tt))], exact=True)
                prev[("pbt",)] = tc
                prev[("hq", gi, tt)] = tc
        cur_hq = {gi: [prev[("hq", gi, tt)] for tt in range(4)]
                  for gi in range(6)}
        if ng + 1 < 4:
            xfwd(ng + 1)
        for H in range(NCH):
            yb = pbY[H % 2]
            wslot = H % 2
            tlast = None
            for ich in range(22):
                wt = p.dma("sync",
                           swd1[:, wslot * 256 + (ich % 2) * 128:
                                wslot * 256 + (ich % 2) * 128 + 128],
                           bass.AP(tap(env["qall"]),
                                   _q_unit("d", H, ich) * 16384,
                                   [[128, 128], [1, 128]]),
                           wait=[prev.get(("dw", wslot, ich % 2)),
                                 env.get("qall_tok")],
                           stream="l1wd%d%d" % (wslot, ich % 2))
                w0 = [wt, prev.get(("yb", H % 2))] if ich == 0 else [wt]
                if ich % 4 == 0:
                    w0 += cur_hq[ich // 4]
                tlast = p.op("tensor", lambda E, ich=ich, H=H, yb=yb,
                             o=wslot * 256 + (ich % 2) * 128: E.matmul(
                                 yb[:, 0:512], swd1[:, o:o + 128],
                                 shq[:, ich * 512:(ich + 1) * 512],
                                 start=(ich == 0), stop=(ich == 21)),
                             wait=w0, inc="pw")
                prev[("dw", wslot, ich % 2)] = tlast
            xs = x_sb[:, H * T + c0: H * T + c0 + 512]
            tv = p.op("vector", lambda E, xs=xs, yb=yb:
                      E.scalar_tensor_tensor(xs, yb[:, 0:512], 0.5, xs,
                                             ALU.mult, ALU.add),
                      wait=[tlast], exact=True)
            prev[("yb", H % 2)] = tv
            x_upd[(H, ng)] = tv
        for gi in range(6):
            for tt in range(4):
                prev[("hq", gi, tt)] = tlast
        for tt in range(4):
            prev[("sxq", tt)] = last_gu[tt]
        if ng + 1 < 4:
            xback(ng + 1)
    return x_upd


KT = 4096


def emit_l1attn_projA(p, env, dbg=None):
    pb = env["pb"]
    sxn, sxg = env["sxn"], env["sxg"]
    swq, swk, swv = env["swq"], env["swk"], env["swv"]
    qsb, ksb, sv = env["qsb"], env["ksb"], env["sv"]
    sq2, rnorm = env["sq2"], env["rnorm"]
    negrow, mrow = env["negrow"], env["mrow"]
    ln_last = env["lnlast"]
    prev = {}
    t_v1 = p.op("gpsimd", lambda E: E.memset(
        bass.AP(tap(sv), 64, [[2080, 128], [65, 32]]), 1.0), inc="gp")
    tub = p.dma("gpsimd", env["subf"][0:16, 0:6144], env["ub_in"],
                stream="ub")
    tue = p.op("scalar", lambda E: E.activation(
        env["subb"][0:16, 0:6144], env["subf"][0:16, 0:6144], AF.Exp),
        wait=[tub], inc="ac")
    tus = p.dma("gpsimd", env["ubexp"], env["subb"][0:16, 0:6144],
                wait=[tue], stream="ub")
    tbs = p.op("vector", lambda E: E.tensor_scalar(
        env["sbqs"][0:64, 0:16], env["sbq"][0:64, 0:16], 0.125, None,
        ALU.mult), wait=[env["bq_tok"]], inc="dv")
    sstore = {}
    for h in range(16):
        sl = h % 2
        tq = p.dma("gpsimd",
                   swq[:, sl * 512:(sl + 1) * 512].rearrange(
                       "p (c m) -> p c m", c=8),
                   env["hwqT"][:, h * 64:(h + 1) * 64].rearrange(
                       "(c p) m -> p c m", p=128),
                   wait=[prev.get(("wq", sl)), env["cst"]],
                   stream="wq%d" % sl)
        tk = p.dma("gpsimd",
                   swk[:, sl * 512:(sl + 1) * 512].rearrange(
                       "p (c m) -> p c m", c=8),
                   env["hwkT"][:, h * 64:(h + 1) * 64].rearrange(
                       "(c p) m -> p c m", p=128),
                   wait=[prev.get(("wk", sl))], stream="wk%d" % sl)
        tv = p.dma("gpsimd",
                   swv[:, sl * 512:(sl + 1) * 512].rearrange(
                       "p (c m) -> p c m", c=8),
                   env["hwvT"][:, h * 64:(h + 1) * 64].rearrange(
                       "(c p) m -> p c m", p=128),
                   wait=[prev.get(("wv", sl))], stream="wv%d" % sl)
        nrcol = 0
        tnr = None
        for qg in range(4):
            bank = pb[qg % 2]
            tm = None
            for ch in range(NCH):
                tm = p.op("tensor", lambda E, ch=ch, qg=qg, sl=sl, bank=bank:
                          E.matmul(bank[0:64, 0:512],
                                   swq[:, sl * 512 + ch * 64:
                                       sl * 512 + (ch + 1) * 64],
                                   sxn[:, ch * T + qg * 512:
                                       ch * T + qg * 512 + 512],
                                   start=(ch == 0), stop=(ch == NCH - 1)),
                          wait=([tq, prev.get(("pj", qg % 2)), ln_last, tbs]
                                if ch == 0 else []),
                          inc=("pe" if ch == NCH - 1 else None))
            te = p.op("scalar", lambda E, qg=qg, h=h, bank=bank: E.activation(
                qsb[0:64, qg * 512:(qg + 1) * 512], bank[0:64, 0:512],
                AF.Identity, bias=env["sbqs"][0:64, h:h + 1], scale=0.125),
                wait=[tm, sstore.get(("q", h - 1))], inc="ac")
            prev[("pj", qg % 2)] = te
            tsq = p.op("scalar", lambda E, qg=qg: E.activation(
                sq2[0:64, 0:512], qsb[0:64, qg * 512:(qg + 1) * 512],
                AF.Square), wait=[prev.get(("sqw",))], inc="ac")
            tnm = p.op("tensor", lambda E: E.matmul(
                pb[6][0:1, 0:512], env["onesc"][0:64, 0:1],
                sq2[0:64, 0:512], start=True, stop=True),
                wait=[tsq, prev.get(("nr",))], inc="pe")
            prev[("sqw",)] = tnm
            tnr = p.op("vector", lambda E, col=nrcol: E.tensor_reduce(
                rnorm[0:1, col:col + 1], pb[6][0:1, 0:512], AX, ALU.max),
                wait=[tnm], inc="dv")
            prev[("nr",)] = tnr
            nrcol += 1
        prev[("wq", sl)] = tm
        for kg in range(8):
            bank = pb[kg % 2]
            tm = None
            for ch in range(NCH):
                tm = p.op("tensor", lambda E, ch=ch, kg=kg, sl=sl, bank=bank:
                          E.matmul(bank[0:64, 0:512],
                                   swk[:, sl * 512 + ch * 64:
                                       sl * 512 + (ch + 1) * 64],
                                   sxg[:, ch * KT + kg * 512:
                                       ch * KT + kg * 512 + 512],
                                   start=(ch == 0), stop=(ch == NCH - 1)),
                          wait=([tk, prev.get(("pj", kg % 2))]
                                if ch == 0 else []),
                          inc=("pe" if ch == NCH - 1 else None))
            te = p.op("scalar", lambda E, kg=kg, h=h, bank=bank: E.activation(
                ksb[0:64, kg * 512:(kg + 1) * 512], bank[0:64, 0:512],
                AF.Identity, bias=env["sbk"][0:64, h:h + 1]),
                wait=[tm, env["bk_tok"], sstore.get(("k", h - 1))], inc="ac")
            prev[("pj", kg % 2)] = te
            tsq = p.op("scalar", lambda E, kg=kg: E.activation(
                sq2[0:64, 0:512], ksb[0:64, kg * 512:(kg + 1) * 512],
                AF.Square), wait=[prev.get(("sqw",))], inc="ac")
            tnm = p.op("tensor", lambda E: E.matmul(
                pb[6][0:1, 0:512], env["onesc"][0:64, 0:1],
                sq2[0:64, 0:512], start=True, stop=True),
                wait=[tsq, prev.get(("nr",))], inc="pe")
            prev[("sqw",)] = tnm
            tnr = p.op("vector", lambda E, col=nrcol: E.tensor_reduce(
                rnorm[0:1, col:col + 1], pb[6][0:1, 0:512], AX, ALU.max),
                wait=[tnm], inc="dv")
            prev[("nr",)] = tnr
            nrcol += 1
        prev[("wk", sl)] = tm
        for vb in range(8):
            bank = pb[vb % 2]
            tm4 = None
            for s4 in range(4):
                kc = vb * 4 + s4
                tm = None
                for ch in range(NCH):
                    tm = p.op("tensor",
                              lambda E, ch=ch, kc=kc, s4=s4, sl=sl, bank=bank:
                              E.matmul(bank[:, s4 * 64:(s4 + 1) * 64],
                                       sxg[:, ch * KT + kc * 128:
                                           ch * KT + kc * 128 + 128],
                                       swv[:, sl * 512 + ch * 64:
                                           sl * 512 + (ch + 1) * 64],
                                       start=(ch == 0), stop=(ch == NCH - 1)),
                              wait=([tv, prev.get(("pj", vb % 2)), t_v1]
                                    if (ch == 0 and s4 == 0) else []),
                              inc=("pe" if (ch == NCH - 1 and s4 == 3)
                                   else None))
                tm4 = tm
            te = p.op("vector", lambda E, vb=vb, bank=bank: E.tensor_copy(
                bass.AP(tap(sv), vb * 4 * 65, [[2080, 128], [65, 4], [1, 64]]),
                bass.AP(tap(bank), 0, [[512, 128], [64, 4], [1, 64]])),
                wait=[tm4, sstore.get(("v", h - 1))], inc="dv")
            prev[("pj", vb % 2)] = te
        prev[("wv", sl)] = tm4
        t1 = p.op("vector", lambda E: E.tensor_reduce(
            mrow[0:1, 0:1], rnorm[0:1, 0:4], AX, ALU.max), wait=[tnr],
            inc="dv")
        p.op("vector", lambda E: E.tensor_reduce(
            mrow[0:1, 1:2], rnorm[0:1, 4:12], AX, ALU.max))
        t3 = p.op("vector", lambda E: E.tensor_mul(
            mrow[0:1, 2:3], mrow[0:1, 0:1], mrow[0:1, 1:2]), inc="dv")
        t4 = p.op("scalar", lambda E: E.activation(
            mrow[0:1, 3:4], mrow[0:1, 2:3], AF.Sqrt,
            bias=env["szero"][0:1, 0:1]), wait=[t3, env["cst"]], inc="ac")
        tneg = p.op("vector", lambda E, h=h: E.tensor_scalar(
            negrow[0:1, h:h + 1], mrow[0:1, 3:4], -1.0, None, ALU.mult),
            wait=[t4], inc="dv")
        sstore[("q", h)] = p.dma(
            "gpsimd", bass.AP(tap(env["qsid"]), h * 64 * 2048,
                              [[2048, 64], [1, 2048]]),
            qsb[0:64, 0:2048],
            wait=[prev[("pj", 0)], prev[("pj", 1)], sstore.get(("q", h - 1))],
            stream="qs")
        sstore[("k", h)] = p.dma(
            "gpsimd", bass.AP(tap(env["ksid"]), h * 64 * 4096,
                              [[4096, 64], [1, 4096]]),
            ksb[0:64, 0:4096],
            wait=[prev[("pj", 0)], prev[("pj", 1)], sstore.get(("k", h - 1))],
            stream="ks")
        sstore[("v", h)] = p.dma(
            "gpsimd", bass.AP(tap(env["vsid"]), h * 128 * 2080,
                              [[2080, 128], [1, 2080]]),
            sv[:, 0:2080],
            wait=[prev[("pj", 0)], prev[("pj", 1)], sstore.get(("v", h - 1))],
            stream="vs")
    tbm = p.op("tensor", lambda E: E.matmul(
        pb[7][:, 0:16], env["ones_r"][0:1, 0:128], negrow[0:1, 0:16],
        start=True, stop=True), wait=[tneg], inc="pe")
    tbc = p.op("vector", lambda E: E.tensor_copy(
        env["snegm"][:, 0:16], pb[7][:, 0:16]), wait=[tbm], inc="dv")
    return [sstore[("q", 15)], sstore[("k", 15)], sstore[("v", 15)],
            tbc, tus]


def emit_l1attn_scores(p, env, pa_done, dbg=None):
    pb = env["pb"]
    qsb2, ksb2, sv2 = env["qsb2"], env["ksb2"], env["sv2"]
    sub, Eb, escr = env["sub"], env["Eb"], env["escr"]
    onum, of32, osb, srcp = env["onum"], env["of32"], env["osb"], env["srcp"]
    prev = {}
    ost = {}
    for h in range(16):
        w0 = pa_done if h == 0 else []
        tql = p.dma("sync", qsb2[0:64, 0:2048],
                    bass.AP(tap(env["qsid"]), h * 64 * 2048,
                            [[2048, 64], [1, 2048]]),
                    wait=[prev.get(("qfree",))] + w0, stream="qL")
        tkl = p.dma("sync", ksb2[0:64, 0:4096],
                    bass.AP(tap(env["ksid"]), h * 64 * 4096,
                            [[4096, 64], [1, 4096]]),
                    wait=[prev.get(("kfree",))] + w0, stream="kL")
        tvl = p.dma("sync", sv2[:, 0:2080],
                    bass.AP(tap(env["vsid"]), h * 128 * 2080,
                            [[2080, 128], [1, 2080]]),
                    wait=[prev.get(("vfree",))] + w0, stream="vL")
        tsl = p.dma("sync", sub[:, 0:6016],
                    bass.AP(tap(env["ubexp"]), h * 6144,
                            [[1, 128], [1, 6016]]),
                    wait=[prev.get(("subfree",))] + w0, stream="sL")
        for qg in range(4):
            etoks = []
            for c in range(32):
                bank = pb[c % 2]
                tm = p.op("tensor", lambda E, c=c, qg=qg, bank=bank: E.matmul(
                    bank[:, 0:512], ksb2[0:64, c * 128:(c + 1) * 128],
                    qsb2[0:64, qg * 512:(qg + 1) * 512],
                    start=True, stop=True),
                    wait=([tql, tkl, prev.get(("sc", c % 2))]
                          if c < 2 else [prev.get(("sc", c % 2))]),
                    inc="pe")
                tex = p.op("scalar", lambda E, bank=bank, h=h, sl=c % 2:
                           E.activation(escr[:, sl * 512:(sl + 1) * 512],
                                        bank[:, 0:512], AF.Exp,
                                        bias=env["snegm"][:, h:h + 1]),
                           wait=[tm] + ([etoks[c - 2]] if c >= 2
                                        else [prev.get(("esc", c % 2))]),
                           inc="ac")
                prev[("sc", c % 2)] = tex
                tmu = p.op("vector", lambda E, c=c, qg=qg, sl=c % 2:
                           E.tensor_tensor(
                               Eb[:, c * 512:(c + 1) * 512],
                               escr[:, sl * 512:(sl + 1) * 512],
                               bass.AP(tap(sub), c * 128 - qg * 512 + 2047,
                                       [[6016, 128], [-1, 512]]),
                               ALU.mult),
                           wait=([tex, tsl, prev.get(("Ewar",))]
                                 if c == 0 else [tex]), exact=True)
                etoks.append(tmu)
            avb = pb[2 + (qg % 2)]
            tav = None
            for c in range(32):
                tav = p.op("tensor", lambda E, c=c, avb=avb: E.matmul(
                    avb[0:65, 0:512], sv2[:, c * 65:(c + 1) * 65],
                    Eb[:, c * 512:(c + 1) * 512],
                    start=(c == 0), stop=(c == 31)),
                    wait=([tvl, etoks[c], prev.get(("av", qg % 2))]
                          if c == 0 else [etoks[c]]),
                    inc=("pw" if c == 31 else None))
            prev[("Ewar",)] = tav
            prev[("esc", 0)] = etoks[-2]
            prev[("esc", 1)] = etoks[-1]
            trc = p.op("vector", lambda E, avb=avb: E.tensor_scalar_max(
                srcp[0:1, 0:512], avb[64:65, 0:512], 1e-37),
                wait=[tav], inc="dv")
            trc2 = p.op("vector", lambda E: E.reciprocal(
                srcp[0:1, 0:512], srcp[0:1, 0:512]), inc="dv")
            tbc2 = p.op("tensor", lambda E: E.matmul(
                pb[4][0:64, 0:512], env["ones_r"][0:1, 0:64],
                srcp[0:1, 0:512], start=True, stop=True),
                wait=[trc2, prev.get(("bb",))], inc="pe")
            tnum = p.op("scalar", lambda E, avb=avb: E.activation(
                onum[0:64, 0:512], avb[0:64, 0:512], AF.Identity),
                wait=[tav], inc="ac")
            tdv = p.op("vector", lambda E: E.tensor_tensor(
                of32[0:64, 0:512], onum[0:64, 0:512], pb[4][0:64, 0:512],
                ALU.mult), wait=[tbc2, tnum], exact=True)
            prev[("bb",)] = tdv
            prev[("av", qg % 2)] = tdv
            tos = p.op("scalar", lambda E, h=h: E.activation(
                osb[0:64, 0:512], of32[0:64, 0:512], AF.Identity,
                bias=env["sbv"][0:64, h:h + 1]),
                wait=[tdv, env["bv_tok"], ost.get(("w",))], exact=True)
            ost[("w",)] = p.dma(
                "gpsimd", bass.AP(tap(env["osid"]),
                                  h * 64 * 2048 + qg * 512,
                                  [[2048, 64], [1, 512]]),
                osb[0:64, 0:512], wait=[tos, ost.get(("w",))], stream="os")
        prev[("qfree",)] = tav
        prev[("kfree",)] = tav
        prev[("vfree",)] = tav
        prev[("subfree",)] = etoks[-1]
    return [ost[("w",)]]


def emit_l1attn_wo(p, env, ps_done):
    pb = env["pb"]
    swo, so_r = env["swo"], env["so_r"]
    sxi, sxo = env["sxi"], env["sxo"]
    prev = {}
    wo_toks = {}
    for h in range(16):
        wo_toks[h] = p.dma(
            "gpsimd", swo[0:64, h * 1024:(h + 1) * 1024],
            env["hwoT"][h * 64:(h + 1) * 64, 0:1024],
            wait=(ps_done if h < 2 else [wo_toks[h - 2]]),
            stream="wo%d" % (h % 2))
    xst = []
    for qg in range(4):
        so_toks = {}
        for h in range(16):
            so_toks[h] = p.dma(
                "sync", so_r[0:64, h * 512:(h + 1) * 512],
                bass.AP(tap(env["osid"]), h * 64 * 2048 + qg * 512,
                        [[2048, 64], [1, 512]]),
                wait=[prev.get(("sor", h % 4))] +
                     (ps_done if (qg == 0 and h < 4) else []),
                stream="sor%d" % (h % 4))
        for H in range(NCH):
            bank = pb[5 + (H % 2)]
            sl = H % 2
            txl = p.dma("sync", sxi[:, sl * 512:(sl + 1) * 512],
                        env["x_in"][H * 128:(H + 1) * 128,
                                    qg * 512:(qg + 1) * 512],
                        wait=[prev.get(("xi", sl))], stream="xi%d" % sl)
            tmm = None
            for hh in range(16):
                tmm = p.op("tensor", lambda E, hh=hh, H=H, bank=bank:
                           E.matmul(bank[:, 0:512],
                                    swo[0:64, hh * 1024 + H * 128:
                                        hh * 1024 + H * 128 + 128],
                                    so_r[0:64, hh * 512:(hh + 1) * 512],
                                    start=(hh == 0), stop=(hh == 15)),
                           wait=([wo_toks[14], wo_toks[15], so_toks[12],
                                  so_toks[13], so_toks[14], so_toks[15],
                                  prev.get(("wb", H % 2))]
                                 if hh == 0 else []),
                           inc=("pe" if hh == 15 else None))
            tev = p.op("scalar", lambda E, bank=bank, H=H, sl=sl:
                       E.activation(sxo[:, sl * 512:(sl + 1) * 512],
                                    bank[:, 0:512], AF.Identity,
                                    bias=env["sbo"][:, H:H + 1]),
                       wait=[tmm, env["bo_tok"], prev.get(("xo", sl))],
                       inc="ac")
            tad = p.op("vector", lambda E, sl=sl: E.tensor_add(
                sxo[:, sl * 512:(sl + 1) * 512],
                sxo[:, sl * 512:(sl + 1) * 512],
                sxi[:, sl * 512:(sl + 1) * 512]),
                wait=[tev, txl], inc="dv")
            tst = p.dma("gpsimd", env["xo"][H * 128:(H + 1) * 128,
                                            qg * 512:(qg + 1) * 512],
                        sxo[:, sl * 512:(sl + 1) * 512],
                        wait=[tad], stream="xst%d" % sl)
            prev[("xo", sl)] = tst
            prev[("xi", sl)] = tad
            prev[("wb", sl)] = tev
            xst.append(tst)
        for hh in range(4):
            prev[("sor", hh)] = tmm
    return xst


def emit_l1attn_projA_fused(p, env):
    pb = env["pb"]
    sxn = env["sxn"]
    swq, swk, swv = env["swq"], env["swk"], env["swv"]
    qsb, ksb, sv = env["qsb"], env["ksb"], env["sv"]
    sq2, rnorm = env["sq2"], env["rnorm"]
    prev = {}
    t_v1 = p.op("gpsimd", lambda E: E.memset(
        bass.AP(tap(sv), 64, [[1040, 128], [65, 16]]), 1.0), inc="gp")
    tus = []
    for s in range(2):
        tub = p.dma("gpsimd", env["subf"][0:16, 0:3072],
                    env["ub_in"][0:16, s * 3072:(s + 1) * 3072],
                    wait=[prev.get(("ubb",))], stream="ub")
        tue = p.op("scalar", lambda E: E.activation(
            env["subb"][0:16, 0:3072], env["subf"][0:16, 0:3072], AF.Exp),
            wait=[tub, prev.get(("ubs",))], inc="ac")
        prev[("ubb",)] = tue
        t = p.dma("gpsimd", env["ubexp"][0:16, s * 3072:(s + 1) * 3072],
                  env["subb"][0:16, 0:3072], wait=[tue], stream="ub")
        prev[("ubs",)] = t
        tus.append(t)
    tbs = p.op("vector", lambda E: E.tensor_scalar(
        env["sbqs"][0:64, 0:16], env["sbq"][0:64, 0:16], 0.125, None,
        ALU.mult), wait=[env["bq_tok"]], inc="dv")
    ln_last = env["lnlast"]
    sstore = {}
    tqn = None
    for h in range(16):
        sl = h % 2
        tq = p.dma("gpsimd",
                   swq[:, sl * 512:(sl + 1) * 512].rearrange(
                       "p (c m) -> p c m", c=8),
                   env["hwq1"][:, h * 64:(h + 1) * 64].rearrange(
                       "(c p) m -> p c m", p=128),
                   wait=[prev.get(("wq", sl)), env["cst"]],
                   stream="wq%d" % sl)
        tk = p.dma("gpsimd",
                   swk[:, sl * 512:(sl + 1) * 512].rearrange(
                       "p (c m) -> p c m", c=8),
                   env["hwk1"][:, h * 64:(h + 1) * 64].rearrange(
                       "(c p) m -> p c m", p=128),
                   wait=[prev.get(("wk", sl))], stream="wk%d" % sl)
        tv = p.dma("gpsimd",
                   swv[:, sl * 512:(sl + 1) * 512].rearrange(
                       "p (c m) -> p c m", c=8),
                   env["hwv1"][:, h * 64:(h + 1) * 64].rearrange(
                       "(c p) m -> p c m", p=128),
                   wait=[prev.get(("wv", sl))], stream="wv%d" % sl)
        tnr = None
        for qg in range(4):
            bank = pb[qg % 2]
            tm = None
            for ch in range(NCH):
                tm = p.op("tensor", lambda E, ch=ch, qg=qg, sl=sl, bank=bank:
                          E.matmul(bank[0:64, 0:512],
                                   swq[:, sl * 512 + ch * 64:
                                       sl * 512 + (ch + 1) * 64],
                                   sxn[:, ch * T + qg * 512:
                                       ch * T + qg * 512 + 512],
                                   start=(ch == 0), stop=(ch == NCH - 1)),
                          wait=([tq, prev.get(("pj", qg % 2)), ln_last, tbs]
                                if ch == 0 else []),
                          inc=("pe" if ch == NCH - 1 else None))
            te = p.op("scalar", lambda E, qg=qg, h=h, bank=bank: E.activation(
                qsb[0:64, qg * 512:(qg + 1) * 512], bank[0:64, 0:512],
                AF.Identity, bias=env["sbqs"][0:64, h:h + 1], scale=0.125),
                wait=[tm, sstore.get(("q", h - 1))], inc="ac")
            prev[("pj", qg % 2)] = te
            tsq = p.op("scalar", lambda E, qg=qg: E.activation(
                sq2[0:64, 0:512], qsb[0:64, qg * 512:(qg + 1) * 512],
                AF.Square), wait=[prev.get(("sqw",))], inc="ac")
            tnm = p.op("tensor", lambda E: E.matmul(
                pb[6][0:1, 0:512], env["onesc"][0:64, 0:1],
                sq2[0:64, 0:512], start=True, stop=True),
                wait=[tsq, prev.get(("nr",))], inc="pe")
            prev[("sqw",)] = tnm
            tnr = p.op("vector", lambda E, col=qg: E.tensor_reduce(
                rnorm[0:1, col:col + 1], pb[6][0:1, 0:512], AX, ALU.max),
                wait=[tnm], inc="dv")
            prev[("nr",)] = tnr
        prev[("wq", sl)] = tm
        tqn = p.op("vector", lambda E, h=h: E.tensor_reduce(
            env["qnorm"][0:1, h:h + 1], rnorm[0:1, 0:4], AX, ALU.max),
            wait=[tnr], inc="dv")
        for kg in range(4):
            bank = pb[kg % 2]
            tm = None
            for ch in range(NCH):
                tm = p.op("tensor", lambda E, ch=ch, kg=kg, sl=sl, bank=bank:
                          E.matmul(bank[0:64, 0:512],
                                   swk[:, sl * 512 + ch * 64:
                                       sl * 512 + (ch + 1) * 64],
                                   sxn[:, ch * T + kg * 512:
                                       ch * T + kg * 512 + 512],
                                   start=(ch == 0), stop=(ch == NCH - 1)),
                          wait=([tk, prev.get(("pj", kg % 2))]
                                if ch == 0 else []),
                          inc=("pe" if ch == NCH - 1 else None))
            te = p.op("scalar", lambda E, kg=kg, h=h, bank=bank: E.activation(
                ksb[0:64, kg * 512:(kg + 1) * 512], bank[0:64, 0:512],
                AF.Identity, bias=env["sbk"][0:64, h:h + 1]),
                wait=[tm, env["bk_tok"], sstore.get(("k", h - 1))], inc="ac")
            prev[("pj", kg % 2)] = te
        prev[("wk", sl)] = tm
        for vb in range(4):
            bank = pb[vb % 2]
            tm4 = None
            for s4 in range(4):
                kc = vb * 4 + s4
                tm = None
                for ch in range(NCH):
                    tm = p.op("tensor",
                              lambda E, ch=ch, kc=kc, s4=s4, sl=sl, bank=bank:
                              E.matmul(bank[:, s4 * 64:(s4 + 1) * 64],
                                       sxn[:, ch * T + kc * 128:
                                           ch * T + kc * 128 + 128],
                                       swv[:, sl * 512 + ch * 64:
                                           sl * 512 + (ch + 1) * 64],
                                       start=(ch == 0), stop=(ch == NCH - 1)),
                              wait=([tv, prev.get(("pj", vb % 2)), t_v1]
                                    if (ch == 0 and s4 == 0) else []),
                              inc=("pe" if (ch == NCH - 1 and s4 == 3)
                                   else None))
                tm4 = tm
            te = p.op("vector", lambda E, vb=vb, bank=bank: E.tensor_copy(
                bass.AP(tap(sv), vb * 4 * 65, [[1040, 128], [65, 4], [1, 64]]),
                bass.AP(tap(bank), 0, [[512, 128], [64, 4], [1, 64]])),
                wait=[tm4, sstore.get(("v", h - 1))], inc="dv")
            prev[("pj", vb % 2)] = te
        prev[("wv", sl)] = tm4
        sstore[("q", h)] = p.dma(
            "gpsimd", bass.AP(tap(env["qsid"]), h * 64 * 2048,
                              [[2048, 64], [1, 2048]]),
            qsb[0:64, 0:2048],
            wait=[prev[("pj", 0)], prev[("pj", 1)]], stream="qs")
        sstore[("k", h)] = p.dma(
            "gpsimd", bass.AP(tap(env["ksid_own"]), h * 64 * 2048,
                              [[2048, 64], [1, 2048]]),
            ksb[0:64, 0:2048],
            wait=[prev[("pj", 0)], prev[("pj", 1)]], stream="ks")
        sstore[("v", h)] = p.dma(
            "gpsimd", bass.AP(tap(env["vsid_own"]), h * 128 * 1040,
                              [[1040, 128], [1, 1040]]),
            sv[:, 0:1040],
            wait=[prev[("pj", 0)], prev[("pj", 1)]], stream="vs")
    return [sstore[("q", 15)], sstore[("k", 15)], sstore[("v", 15)],
            tqn] + tus


def emit_l1attn_scores_fused(p, env, pa_done, ag_toks):
    pb = env["pb"]
    qsb2, ksb2, sv2 = env["qsb2"], env["ksb2"], env["sv2"]
    sub, Eb, escr = env["sub"], env["Eb"], env["escr"]
    onum, of32, osb, srcp = env["onum"], env["of32"], env["osb"], env["srcp"]
    sq2b, rnk, mr2 = env["sq2b"], env["rnk"], env["mr2"]
    snegm = env["snegm"]
    prev = {}
    ost = {}
    for h in range(16):
        w0 = (pa_done + ag_toks) if h == 0 else []
        tql = p.dma("sync", qsb2[0:64, 0:2048],
                    bass.AP(tap(env["qsid"]), h * 64 * 2048,
                            [[2048, 64], [1, 2048]]),
                    wait=[prev.get(("qfree",))] + w0, stream="qL")
        tkl = None
        for hf in range(2):
            tkl = p.dma("sync", ksb2[0:64, hf * 2048:(hf + 1) * 2048],
                        bass.AP(tap(env["ksid_all"]),
                                (hf * 16 + h) * 64 * 2048,
                                [[2048, 64], [1, 2048]]),
                        wait=[prev.get(("kfree",))] + w0 if hf == 0 else [],
                        stream="kL")
        tvl = None
        for hf in range(2):
            tvl = p.dma("sync", sv2[:, hf * 1040:(hf + 1) * 1040],
                        bass.AP(tap(env["vsid_all"]),
                                (hf * 16 + h) * 128 * 1040,
                                [[1040, 128], [1, 1040]]),
                        wait=[prev.get(("vfree",))] + w0 if hf == 0 else [],
                        stream="vL")
        tsl = p.dma("sync", sub[:, 0:6016],
                    bass.AP(tap(env["ubexp"]), h * 6144,
                            [[1, 128], [1, 6016]]),
                    wait=[prev.get(("subfree",))] + w0, stream="sL")
        tnr = None
        for kg in range(8):
            tsq = p.op("scalar", lambda E, kg=kg: E.activation(
                sq2b[0:64, 0:512], ksb2[0:64, kg * 512:(kg + 1) * 512],
                AF.Square),
                wait=([tkl] if kg == 0 else []) + [prev.get(("sqw",))],
                inc="ac")
            tnm = p.op("tensor", lambda E: E.matmul(
                pb[6][0:1, 0:512], env["onesc"][0:64, 0:1],
                sq2b[0:64, 0:512], start=True, stop=True),
                wait=[tsq, prev.get(("nr",))], inc="pe")
            prev[("sqw",)] = tnm
            tnr = p.op("vector", lambda E, col=kg: E.tensor_reduce(
                rnk[0:1, col:col + 1], pb[6][0:1, 0:512], AX, ALU.max),
                wait=[tnm], inc="dv")
            prev[("nr",)] = tnr
        p.op("vector", lambda E: E.tensor_reduce(
            mr2[0:1, 0:1], rnk[0:1, 0:8], AX, ALU.max), wait=[tnr])
        t2 = p.op("vector", lambda E, h=h: E.tensor_mul(
            mr2[0:1, 1:2], mr2[0:1, 0:1], env["qnorm"][0:1, h:h + 1]),
            inc="dv")
        t3 = p.op("scalar", lambda E: E.activation(
            mr2[0:1, 2:3], mr2[0:1, 1:2], AF.Sqrt,
            bias=env["szero"][0:1, 0:1]), wait=[t2], inc="ac")
        t4 = p.op("vector", lambda E: E.tensor_scalar(
            mr2[0:1, 3:4], mr2[0:1, 2:3], -1.0, None, ALU.mult),
            wait=[t3], inc="dv")
        t5 = p.op("tensor", lambda E: E.matmul(
            pb[5][:, 0:1], env["ones_r"][0:1, 0:128], mr2[0:1, 3:4],
            start=True, stop=True), wait=[t4, prev.get(("negb",))], inc="pe")
        t6 = p.op("vector", lambda E: E.tensor_copy(
            snegm[:, 0:1], pb[5][:, 0:1]),
            wait=[t5, prev.get(("negw",))], exact=True)
        prev[("negb",)] = t6
        for qg in range(4):
            etoks = []
            for c in range(32):
                bank = pb[c % 2]
                tm = p.op("tensor", lambda E, c=c, qg=qg, bank=bank: E.matmul(
                    bank[:, 0:512], ksb2[0:64, c * 128:(c + 1) * 128],
                    qsb2[0:64, qg * 512:(qg + 1) * 512],
                    start=True, stop=True),
                    wait=([tql, tkl, prev.get(("sc", c % 2))]
                          if c < 2 else [prev.get(("sc", c % 2))]),
                    inc="pe")
                tex = p.op("scalar", lambda E, bank=bank, sl=c % 2:
                           E.activation(escr[:, sl * 512:(sl + 1) * 512],
                                        bank[:, 0:512], AF.Exp,
                                        bias=snegm[:, 0:1]),
                           wait=[tm, t6] + ([etoks[c - 2]] if c >= 2
                                            else [prev.get(("esc", c % 2))]),
                           exact=True)
                prev[("sc", c % 2)] = tex
                tmu = p.op("vector", lambda E, c=c, qg=qg, sl=c % 2:
                           E.tensor_tensor(
                               Eb[:, c * 512:(c + 1) * 512],
                               escr[:, sl * 512:(sl + 1) * 512],
                               bass.AP(tap(sub), c * 128 - qg * 512 + 2047,
                                       [[6016, 128], [-1, 512]]),
                               ALU.mult),
                           wait=([tex, tsl, prev.get(("Ewar",))]
                                 if c == 0 else [tex]), exact=True)
                etoks.append(tmu)
            avb = pb[2 + (qg % 2)]
            tav = None
            for c in range(32):
                tav = p.op("tensor", lambda E, c=c, avb=avb: E.matmul(
                    avb[0:65, 0:512], sv2[:, c * 65:(c + 1) * 65],
                    Eb[:, c * 512:(c + 1) * 512],
                    start=(c == 0), stop=(c == 31)),
                    wait=([tvl, etoks[c], prev.get(("av", qg % 2))]
                          if c == 0 else [etoks[c]]),
                    inc=("pw" if c == 31 else None))
            prev[("Ewar",)] = tav
            prev[("esc", 0)] = etoks[-2]
            prev[("esc", 1)] = etoks[-1]
            trc = p.op("vector", lambda E, avb=avb: E.tensor_scalar_max(
                srcp[0:1, 0:512], avb[64:65, 0:512], 1e-37),
                wait=[tav], inc="dv")
            trc2 = p.op("vector", lambda E: E.reciprocal(
                srcp[0:1, 0:512], srcp[0:1, 0:512]), inc="dv")
            tbc2 = p.op("tensor", lambda E: E.matmul(
                pb[4][0:64, 0:512], env["ones_r"][0:1, 0:64],
                srcp[0:1, 0:512], start=True, stop=True),
                wait=[trc2, prev.get(("bb",))], inc="pe")
            tnum = p.op("scalar", lambda E, avb=avb: E.activation(
                onum[0:64, 0:512], avb[0:64, 0:512], AF.Identity),
                wait=[tav], inc="ac")
            tdv = p.op("vector", lambda E: E.tensor_tensor(
                of32[0:64, 0:512], onum[0:64, 0:512], pb[4][0:64, 0:512],
                ALU.mult), wait=[tbc2, tnum], exact=True)
            prev[("bb",)] = tdv
            prev[("av", qg % 2)] = tdv
            tos = p.op("scalar", lambda E, h=h: E.activation(
                osb[0:64, 0:512], of32[0:64, 0:512], AF.Identity,
                bias=env["sbv"][0:64, h:h + 1]),
                wait=[tdv, env["bv_tok"], ost.get(("w",))], exact=True)
            ost[("w",)] = p.dma(
                "gpsimd", bass.AP(tap(env["osid"]),
                                  h * 64 * 2048 + qg * 512,
                                  [[2048, 64], [1, 512]]),
                osb[0:64, 0:512], wait=[tos], stream="os")
        prev[("qfree",)] = tav
        prev[("kfree",)] = tav
        prev[("vfree",)] = tav
        prev[("subfree",)] = etoks[-1]
        prev[("negw",)] = prev[("sc", 1)]
    return [ost[("w",)]]


def emit_l1attn_wo_fused(p, env, ps_done, x_sb):
    pb = env["pb"]
    swo, so_r, sxo = env["swo"], env["so_r"], env["sxo"]
    prev = {}
    wo_toks = {}
    x_upd = {}
    for h in range(16):
        wo_toks[h] = p.dma(
            "gpsimd", swo[0:64, h * 1024:(h + 1) * 1024],
            env["hwo1"][h * 64:(h + 1) * 64, 0:1024],
            wait=(ps_done if h < 2 else []), stream="wo%d" % (h % 2))
    for qg in range(4):
        so_toks = {}
        for h in range(16):
            so_toks[h] = p.dma(
                "sync", so_r[0:64, h * 512:(h + 1) * 512],
                bass.AP(tap(env["osid"]), h * 64 * 2048 + qg * 512,
                        [[2048, 64], [1, 512]]),
                wait=[prev.get(("sor", h % 4))] +
                     (ps_done if (qg == 0 and h < 4) else []),
                stream="sor%d" % (h % 4))
        for H in range(NCH):
            bank = pb[5 + (H % 2)]
            sl = H % 2
            tmm = None
            for hh in range(16):
                tmm = p.op("tensor", lambda E, hh=hh, H=H, bank=bank:
                           E.matmul(bank[:, 0:512],
                                    swo[0:64, hh * 1024 + H * 128:
                                        hh * 1024 + H * 128 + 128],
                                    so_r[0:64, hh * 512:(hh + 1) * 512],
                                    start=(hh == 0), stop=(hh == 15)),
                           wait=([wo_toks[14], wo_toks[15], so_toks[12],
                                  so_toks[13], so_toks[14], so_toks[15],
                                  prev.get(("wb", H % 2))]
                                 if hh == 0 else []),
                           inc=("pe" if hh == 15 else None))
            tev = p.op("scalar", lambda E, bank=bank, H=H, sl=sl:
                       E.activation(sxo[:, sl * 512:(sl + 1) * 512],
                                    bank[:, 0:512], AF.Identity,
                                    bias=env["sbo"][:, H:H + 1]),
                       wait=[tmm, env["bo_tok"], prev.get(("xo", sl))],
                       inc="ac")
            xs = x_sb[:, H * T + qg * 512: H * T + qg * 512 + 512]
            tad = p.op("vector", lambda E, xs=xs, sl=sl: E.tensor_add(
                xs, sxo[:, sl * 512:(sl + 1) * 512], xs),
                wait=[tev], inc="dv")
            prev[("xo", sl)] = tad
            prev[("wb", sl)] = tev
            x_upd[(H, qg)] = tad
        for hh in range(4):
            prev[("sor", hh)] = tmm
    return x_upd


def build_fused(dbg=None, nstages=6):
    P.STREAM_ALIAS = {}
    for kind in "gu":
        for sl in range(2):
            for ch in range(8):
                P.STREAM_ALIAS["l1w%s%d%d" % (kind, sl, ch)] = \
                    "l1w%s%d" % (kind, sl)
    for xh, yh in (("watq", "wq"), ("watk", "wk"), ("watv", "wv"),
                   ("wao", "wo")):
        for i in range(2):
            P.STREAM_ALIAS["%s%d" % (xh, i)] = "%s%d" % (yh, i)
    nc = bass.Bass("TRN2", target_bir_lowering=False)
    x_in = nc.dram_tensor("x", [D, T], F32, kind="ExternalInput").ap()
    x_out = nc.dram_tensor("xo", [D, T], F32, kind="ExternalOutput").ap()
    hgw = nc.dram_tensor("gw", [D, NE], F32, kind="ExternalInput").ap()
    hgb = nc.dram_tensor("gb", [NE], F32, kind="ExternalInput").ap()
    hwg = nc.dram_tensor("wg", [10, D, IE], F32, kind="ExternalInput").ap()
    hwu = nc.dram_tensor("wu", [10, D, IE], F32, kind="ExternalInput").ap()
    hwd = nc.dram_tensor("wd", [10, IE, D], F32, kind="ExternalInput").ap()
    hwqT = nc.dram_tensor("wqT", [D, D], F32, kind="ExternalInput").ap()
    hwkT = nc.dram_tensor("wkT", [D, D], F32, kind="ExternalInput").ap()
    hwvT = nc.dram_tensor("wvT", [D, D], F32, kind="ExternalInput").ap()
    hwoT = nc.dram_tensor("woT", [D, D], F32, kind="ExternalInput").ap()
    hmask_in = nc.dram_tensor("hmask", [128, 1], F32,
                              kind="ExternalInput").ap()
    wqsrc = nc.dram_tensor("wqsrc", [128, QU_PER_CORE * 128], F32,
                           kind="ExternalInput").ap()
    hwq1 = nc.dram_tensor("aqT", [D, D], F32, kind="ExternalInput").ap()
    hwk1 = nc.dram_tensor("akT", [D, D], F32, kind="ExternalInput").ap()
    hwv1 = nc.dram_tensor("avT", [D, D], F32, kind="ExternalInput").ap()
    hwo1 = nc.dram_tensor("aoT", [D, D], F32, kind="ExternalInput").ap()
    ub_in = nc.dram_tensor("ub", [16, 6144], F32, kind="ExternalInput").ap()
    bq_in = nc.dram_tensor("bq", [64, 16], F32, kind="ExternalInput").ap()
    bk_in = nc.dram_tensor("bk", [64, 16], F32, kind="ExternalInput").ap()
    bv_in = nc.dram_tensor("bv", [64, 16], F32, kind="ExternalInput").ap()
    bo_in = nc.dram_tensor("bo", [128, 8], F32, kind="ExternalInput").ap()
    wdram = nc.dram_tensor("wdram", [NE * T], F32, kind="Internal").ap()
    qg = nc.dram_tensor("qg", [10, 128, 8 * 512], BF16, kind="Internal").ap()
    qu = nc.dram_tensor("qu", [10, 128, 8 * 512], BF16, kind="Internal").ap()
    qd = nc.dram_tensor("qd", [10, 128, 4 * 8 * 128], BF16,
                        kind="Internal").ap()
    cc1_in = nc.dram_tensor("ccxi", [D, T], BF16, kind="Internal").ap()
    cc1_out = nc.dram_tensor("ccxo", [2 * D, T], BF16, kind="Internal").ap()
    qown = nc.dram_tensor("qown", [QU_PER_CORE * 16384], BF16,
                          kind="Internal").ap()
    qall = nc.dram_tensor("qall", [8 * QU_PER_CORE * 16384], BF16,
                          kind="Internal", addr_space="Shared").ap()
    ubexp = nc.dram_tensor("ubexp", [16, 6144], BF16, kind="Internal").ap()
    qsid = nc.dram_tensor("qsid", [16, 64, 2048], BF16, kind="Internal").ap()
    ksid_own = nc.dram_tensor("ksid_own", [16, 64, 2048], BF16,
                              kind="Internal").ap()
    vsid_own = nc.dram_tensor("vsid_own", [16, 128, 1040], BF16,
                              kind="Internal").ap()
    ksid_all = nc.dram_tensor("ksid_all", [2, 16, 64, 2048], BF16,
                              kind="Internal").ap()
    vsid_all = nc.dram_tensor("vsid_all", [2, 16, 128, 1040], BF16,
                              kind="Internal").ap()
    osid = nc.dram_tensor("osid", [16, 64, 2048], BF16, kind="Internal").ap()

    env = {"hwg": hwg, "hwu": hwu, "hwd": hwd, "wdram": wdram,
           "qdram_g": qg, "qdram_u": qu, "qdram_d": qd,
           "hwqT": hwqT, "hwkT": hwkT, "hwvT": hwvT, "hwoT": hwoT,
           "wqsrc": wqsrc, "qown": qown, "qall": qall,
           "hwq1": hwq1, "hwk1": hwk1, "hwv1": hwv1, "hwo1": hwo1,
           "ub_in": ub_in, "ubexp": ubexp, "qsid": qsid,
           "ksid_own": ksid_own, "vsid_own": vsid_own,
           "ksid_all": ksid_all, "vsid_all": vsid_all, "osid": osid}
    RG = [[0, 1], [2, 3], [4, 5], [6, 7]]
    with ExitStack() as ctx:
        e = ctx.enter_context
        env["ones_f1024"] = e(nc.sbuf_tensor("ones_f1024", [128, 1], F32))
        env["ones_b1024"] = e(nc.sbuf_tensor("ones_b1024", [128, 1], BF16))
        env["ones_r"] = e(nc.sbuf_tensor("ones_r", [1, 128], F32))
        env["eps"] = e(nc.sbuf_tensor("eps", [1, 1], F32))
        env["identf"] = e(nc.sbuf_tensor("identf", [128, 128], F32))
        env["identb"] = e(nc.sbuf_tensor("identb", [128, 128], BF16))
        env["cidx8"] = e(nc.sbuf_tensor("cidx8", [128, 128], F32))
        env["cidx4"] = e(nc.sbuf_tensor("cidx4", [128, 64], F32))
        env["cidx8P"] = e(nc.sbuf_tensor("cidx8P", [128, 128], F32))
        env["cidx4P"] = e(nc.sbuf_tensor("cidx4P", [128, 64], F32))
        env["ones_r64"] = e(nc.sbuf_tensor("ones_r64", [128, 64], F32))
        env["sUT"] = e(nc.sbuf_tensor("sUT", [128, 128], BF16))
        env["hmask"] = e(nc.sbuf_tensor("hmaskb", [128, 1], F32))
        env["onesc"] = e(nc.sbuf_tensor("onesc", [64, 1], BF16))
        env["szero"] = e(nc.sbuf_tensor("szero", [1, 1], F32))
        env["sbq"] = e(nc.sbuf_tensor("sbq", [64, 16], F32))
        env["sbqs"] = e(nc.sbuf_tensor("sbqs", [64, 16], F32))
        env["sbk"] = e(nc.sbuf_tensor("sbk", [64, 16], F32))
        env["sbv"] = e(nc.sbuf_tensor("sbv", [64, 16], F32))
        env["sbo"] = e(nc.sbuf_tensor("sbo", [128, 8], F32))
        env["qnorm"] = e(nc.sbuf_tensor("qnorm", [1, 16], F32))
        env["row"] = e(nc.sbuf_tensor("row", [1, 2048], F32))
        env["xsq"] = e(nc.sbuf_tensor("xsq", [128, 1024], BF16))
        env["lnsc"] = e(nc.sbuf_tensor("lnsc", [128, 512], F32))
        env["qz"] = e(nc.sbuf_tensor("qz", [128, 512], F32))
        env["qa"] = e(nc.sbuf_tensor("qa", [128, 512], F32))
        env["qb"] = e(nc.sbuf_tensor("qb", [128, 512], F32))
        env["qs"] = e(nc.sbuf_tensor("qs", [128, 512], F32))
        env["qbm_w"] = 32
        env["qbm"] = e(nc.sbuf_tensor("qbm", [128, 32], F32))
        env["qsc_w"] = 32
        env["qsc"] = e(nc.sbuf_tensor("qsc", [128, 32], F32))
        sx = e(nc.sbuf_tensor("sx", [128, NCH * T], F32))
        sxn = e(nc.sbuf_tensor("sxn", [128, NCH * T], BF16))
        env["sxn"] = sxn
        pbb = [e(nc.psum_tensor(f"pb{i}", [128, 512], F32)) for i in range(7)]

        p = P(nc)
        for s in ("q_sync", "q_gpsimd", "pe", "dv", "ac", "gp"):
            p.sem(s)
        env["pb"] = pbb + pbb[0:1]
        benv = env
        cst = emit_consts(p, env)
        p.op("gpsimd", lambda E: E.memset(env["ones_r64"][:, :], 1.0))
        t_ut = p.op("gpsimd", lambda E: E.memset(env["sUT"][:, :], 1.0),
                    inc="gp")
        t_ut = p.op("gpsimd", lambda E: E.affine_select(
            env["sUT"][:, :], env["sUT"][:, :], [[1, 128]], ALU.is_ge, 0.0,
            channel_multiplier=-1), wait=[t_ut], inc="gp")
        p.op("gpsimd", lambda E: E.memset(env["onesc"][:, :], 1.0))
        cst = p.op("gpsimd", lambda E: E.memset(env["szero"][:, :], 0.0),
                   inc="gp")
        env["cst"] = cst
        thm = p.dma("gpsimd", env["hmask"][:, :], hmask_in, stream="hm")
        env["bq_tok"] = p.dma("gpsimd", env["sbq"][:, :], bq_in, stream="bq")
        env["bk_tok"] = p.dma("gpsimd", env["sbk"][:, :], bk_in, stream="bk")
        env["bv_tok"] = p.dma("gpsimd", env["sbv"][:, :], bv_in, stream="bv")
        env["bo_tok"] = p.dma("gpsimd", env["sbo"][:, :], bo_in, stream="bo")
        tx = p.dma("gpsimd", sx[:, :].rearrange("p (c t) -> p c t", c=NCH),
                   x_in.rearrange("(c p) t -> p c t", p=128), stream="x")

        sQ0 = ExitStack()
        q0 = sQ0.enter_context
        env["qwbuf"] = q0(nc.sbuf_tensor("qwbuf", [128, 512], F32))
        env["qwout"] = q0(nc.sbuf_tensor("qwout", [128, 512], BF16))
        env["souT"] = q0(nc.sbuf_tensor("souT", [128, 512], BF16))
        env["pbq"] = q0(nc.psum_tensor("pbq", [128, 1024], BF16))
        wq_store = emit_wquant_dist(p, env)

        lnb_ff0 = {"mean": pbb[6], "msq": pbb[5], "bA": pbb[4], "bB": pbb[3]}
        lnb_ff1 = {"mean": pbb[2], "msq": pbb[3], "bA": pbb[4], "bB": pbb[5]}

        def l0ffn_stage(first, gate_w, sfx):
            env = dict(benv)
            sA = ExitStack()
            a = sA.enter_context
            env["sgw"] = a(nc.sbuf_tensor("sgw" + sfx, [128, 64], BF16))
            env["sgb"] = a(nc.sbuf_tensor("sgb" + sfx, [8, 1], F32))
            env["sfcT"] = a(nc.sbuf_tensor("sfcT" + sfx, [8, T], F32))
            env["sgate_w"] = 1536
            env["sgate"] = a(nc.sbuf_tensor("sgate" + sfx, [128, 1536], F32))
            env["swT"] = a(nc.sbuf_tensor("swT" + sfx, [16, NE * 128], F32))
            env["swrow"] = a(nc.sbuf_tensor("swrow" + sfx, [1, 1024], F32))
            env["swg"] = a(nc.sbuf_tensor("swgb" + sfx,
                                          [128, 2 * 8 * 512], BF16))
            env["swu"] = a(nc.sbuf_tensor("swub" + sfx,
                                          [128, 2 * 8 * 512], BF16))
            env["swd"] = a(nc.sbuf_tensor("swdb" + sfx,
                                          [128, 2 * 4 * 8 * 128], BF16))
            env["ssig"] = a(nc.sbuf_tensor("ssig" + sfx, [128, 1024], BF16))
            env["stmp"] = a(nc.sbuf_tensor("stmp" + sfx, [128, 512], BF16))
            env["shu"] = a(nc.sbuf_tensor("shu" + sfx, [128, 4 * 512], BF16))
            env["pb"] = pbb + pbb[0:1]
            tgw = p.dma("gpsimd",
                        env["sgw"][:, :].rearrange("p (c e) -> p c e", c=NCH),
                        hgw.rearrange("(c p) e -> p c e", p=128), stream="gw")
            tgb = p.dma("gpsimd", env["sgb"][:, :], hgb[:, None], stream="gb")
            xn_tok, ln_done = emit_ln(p, env, sx, sxn, T, lnb_ff0,
                                      wait=gate_w + [tx, cst, tgw, tgb])
            x_upd = emit_l0ffn(p, env, sx, sxn, xn_tok, ln_done, first)
            fin = list(x_upd.values())
            if first:
                fin += list(env["qstore"].values())
            sA.close()
            p.barrier(fin)
            return fin

        fin = fin1 = l0ffn_stage(True, [], "")

        qall_tok = p.op("gpsimd", lambda E: E.collective_compute(
            "AllGather", ALU.bypass, replica_groups=[list(range(8))],
            ins=[env["qown"]], outs=[env["qall"]]),
            wait=wq_store, inc="gp")
        benv["qall_tok"] = qall_tok
        p.barrier(fin1 + [qall_tok])
        sQ0.close()

        env = dict(benv)
        sA = ExitStack()
        a = sA.enter_context
        env["sQ"] = a(nc.sbuf_tensor("sQ", [128, 2 * T], BF16))
        env["sKT"] = a(nc.sbuf_tensor("sKT", [128, 2 * T], BF16))
        env["sKtok"] = a(nc.sbuf_tensor("sKtok", [128, 16 * 128], BF16))
        env["sV_w"] = 16 * 130
        env["sV"] = a(nc.sbuf_tensor("sV", [128, 16 * 130], BF16))
        env["sxp"] = a(nc.sbuf_tensor("sxp", [128, 2 * 1024], BF16))
        env["sS"] = a(nc.sbuf_tensor("sS", [128, 130], BF16))
        env["sSf"] = a(nc.sbuf_tensor("sSf", [128, 130], F32))
        env["sAm"] = a(nc.sbuf_tensor("sAm", [128, 256], BF16))
        env["srcp"] = a(nc.sbuf_tensor("srcp", [1, 256], F32))
        env["sOutE"] = a(nc.sbuf_tensor("sOutE", [64, 256], BF16))
        env["sAtt"] = a(nc.sbuf_tensor("sAtt", [64, 2 * T], BF16))
        env["escr"] = a(nc.sbuf_tensor("escr", [128, 1024], F32))
        env["sw_w"] = 2 * 5 * 8 * 128
        env["swpair"] = a(nc.sbuf_tensor("swpair", [128, env["sw_w"]], BF16))
        pb7a = a(nc.psum_tensor("pb7a", [128, 512], F32))
        env["pb"] = pbb + [pb7a]
        t_v1 = p.op("gpsimd", lambda E, t=env["sV"]: E.memset(t[:, :], 1.0),
                    wait=fin1, inc="gp")
        p.op("gpsimd", lambda E, t=env["sQ"]: E.memset(t[64:128, :], 0.0))
        t_v1 = p.op("gpsimd", lambda E, t=env["sKT"]: E.memset(
            t[64:128, :], 0.0), inc="gp")
        env["v1_tok"] = t_v1
        xn_tok2, ln_done2 = emit_ln(p, env, sx, sxn, T, lnb_ff0,
                                    wait=fin1 + [cst, t_ut, thm])
        tccs = p.dma("gpsimd", cc1_in.rearrange("(c p) t -> p c t", p=128),
                     sxn[:, :].rearrange("p (c t) -> p c t", c=NCH),
                     wait=[ln_done2], stream="ccs")
        tcc = p.op("gpsimd", lambda E: E.collective_compute(
            "AllGather", ALU.bypass, replica_groups=RG,
            ins=[cc1_in], outs=[cc1_out]), wait=[tccs], inc="gp")
        if nstages >= 2:
            x_att = emit_l0attn(p, env, sx, sxn, xn_tok2, cc1_out[0:D, :],
                                [tcc], ln_done2, npairs=8, nphase=4)
            fin = fin2 = list(x_att.values())
        sA.close()
        p.barrier(fin)

        if nstages >= 3:
            fin = fin3 = l0ffn_stage(False, fin2, "b")

        def l1ffn_stage(first, gate_w, sfx):
            env = dict(benv)
            sA = ExitStack()
            a = sA.enter_context
            env["pbt"] = a(nc.psum_tensor("pbt" + sfx, [128, 1024], BF16))
            env["pb"] = pbb
            env["sxq"] = a(nc.sbuf_tensor("sxq" + sfx, [128, 8 * 512], BF16))
            env["shq"] = a(nc.sbuf_tensor("shq" + sfx, [128, 22 * 512], BF16))
            env["sxt"] = a(nc.sbuf_tensor("sxt" + sfx, [128, 4096], BF16))
            env["ssil"] = a(nc.sbuf_tensor("ssil" + sfx, [128, 512], F32))
            env["shts"] = a(nc.sbuf_tensor("shts" + sfx, [128, 512], F32))
            env["shtq"] = a(nc.sbuf_tensor("shtq" + sfx, [128, 512], BF16))
            env["swg1"] = a(nc.sbuf_tensor("swg1" + sfx,
                                           [128, 2 * 8 * 512], BF16))
            env["swu1"] = a(nc.sbuf_tensor("swu1" + sfx,
                                           [128, 2 * 8 * 512], BF16))
            env["swd1"] = a(nc.sbuf_tensor("swd1" + sfx, [128, 512], BF16))
            env["qzP"] = a(nc.sbuf_tensor("qzP" + sfx, [128, 512], F32))
            env["qaP"] = a(nc.sbuf_tensor("qaP" + sfx, [128, 512], F32))
            env["qbP"] = a(nc.sbuf_tensor("qbP" + sfx, [128, 512], F32))
            env["qsP"] = a(nc.sbuf_tensor("qsP" + sfx, [128, 512], F32))
            env["qbm_wP"] = 32
            env["qbmP"] = a(nc.sbuf_tensor("qbmP" + sfx, [128, 32], F32))
            env["qsc_wP"] = 32
            env["qscP"] = a(nc.sbuf_tensor("qscP" + sfx, [128, 32], F32))
            xn_tok, ln_done = emit_ln(p, env, sx, sxn, T, lnb_ff1,
                                      wait=gate_w + [cst])
            x_upd = emit_l1ffn(p, env, sx, sxn, xn_tok, ln_done, first, {})
            fin = list(x_upd.values())
            sA.close()
            p.barrier(fin)
            return fin

        if nstages >= 4:
            fin = fin4 = l1ffn_stage(True, fin3, "")

        if nstages < 5:
            st = p.dma("gpsimd",
                       x_out.rearrange("(c p) t -> p c t", p=128),
                       sx[:, :].rearrange("p (c t) -> p c t", c=NCH),
                       wait=fin, stream="st")
            p.op("gpsimd", lambda E: E.engine_nop(), wait=[st])
            p.finish(ctx)
            P.STREAM_ALIAS = {}
            return nc

        env = dict(benv)
        sA = ExitStack()
        a = sA.enter_context
        env["swq"] = a(nc.sbuf_tensor("aswq", [128, 1024], BF16))
        env["swk"] = a(nc.sbuf_tensor("aswk", [128, 1024], BF16))
        env["swv"] = a(nc.sbuf_tensor("aswv", [128, 1024], BF16))
        env["qsb"] = a(nc.sbuf_tensor("qsb", [64, 2048], BF16))
        env["ksb"] = a(nc.sbuf_tensor("ksb", [64, 2048], BF16))
        env["sv"] = a(nc.sbuf_tensor("asv", [128, 1040], BF16))
        env["sq2"] = a(nc.sbuf_tensor("sq2", [64, 512], BF16))
        env["rnorm"] = a(nc.sbuf_tensor("rnorm", [1, 4], F32))
        env["subf"] = a(nc.sbuf_tensor("subf", [16, 3072], F32))
        env["subb"] = a(nc.sbuf_tensor("subb", [16, 3072], BF16))
        env["pb"] = pbb + pbb[0:1]
        xn_tok5, ln_done5 = emit_ln(p, env, sx, sxn, T, lnb_ff1,
                                    wait=fin4 + [cst])
        env["lnlast"] = ln_done5
        pa = emit_l1attn_projA_fused(p, env)
        tagk = p.op("gpsimd", lambda E: E.collective_compute(
            "AllGather", ALU.bypass, replica_groups=RG,
            ins=[ksid_own], outs=[ksid_all]), wait=pa, inc="gp")
        tagv = p.op("gpsimd", lambda E: E.collective_compute(
            "AllGather", ALU.bypass, replica_groups=RG,
            ins=[vsid_own], outs=[vsid_all]), wait=[tagk], inc="gp")
        sA.close()
        p.barrier(pa + [tagk, tagv])
        env = dict(env)
        sB = ExitStack()
        b_ = sB.enter_context
        env["qsb2"] = b_(nc.sbuf_tensor("qsb2", [64, 2048], BF16))
        env["ksb2"] = b_(nc.sbuf_tensor("ksb2", [64, 4096], BF16))
        env["sv2"] = b_(nc.sbuf_tensor("sv2", [128, 2080], BF16))
        env["sub"] = b_(nc.sbuf_tensor("sub", [128, 6016], BF16))
        env["Eb"] = b_(nc.sbuf_tensor("Eb", [128, 32 * 512], BF16))
        env["escr"] = b_(nc.sbuf_tensor("escr2", [128, 1024], BF16))
        env["onum"] = b_(nc.sbuf_tensor("onum", [64, 512], F32))
        env["of32"] = b_(nc.sbuf_tensor("of32", [64, 512], F32))
        env["osb"] = b_(nc.sbuf_tensor("osb", [64, 512], BF16))
        env["srcp"] = b_(nc.sbuf_tensor("srcp2", [1, 512], F32))
        env["sq2b"] = b_(nc.sbuf_tensor("sq2b", [64, 512], BF16))
        env["rnk"] = b_(nc.sbuf_tensor("rnk", [1, 8], F32))
        env["mr2"] = b_(nc.sbuf_tensor("mr2", [1, 4], F32))
        env["snegm"] = b_(nc.sbuf_tensor("snegm", [128, 1], F32))
        ps = emit_l1attn_scores_fused(p, env, pa, [tagk, tagv])
        sB.close()
        p.barrier(ps)
        env = dict(env)
        sC = ExitStack()
        c_ = sC.enter_context
        env["swo"] = c_(nc.sbuf_tensor("swo", [64, 16 * 1024], BF16))
        env["so_r"] = c_(nc.sbuf_tensor("so_r", [64, 16 * 512], BF16))
        env["sxo"] = c_(nc.sbuf_tensor("sxo", [128, 1024], F32))
        x_upd5 = emit_l1attn_wo_fused(p, env, ps, sx)
        fin = fin5 = list(x_upd5.values())
        sC.close()
        p.barrier(fin5)

        if nstages >= 6:
            fin = fin6 = l1ffn_stage(False, fin5, "z6")

        st = p.dma("gpsimd", x_out.rearrange("(c p) t -> p c t", p=128),
                   sx[:, :].rearrange("p (c t) -> p c t", c=NCH),
                   wait=fin, stream="st")
        p.op("gpsimd", lambda E: E.engine_nop(), wait=[st])
        p.finish(ctx)
    P.STREAM_ALIAS = {}
    return nc


def _env_common(nc, ctx, env):
    e = ctx.enter_context
    env["ones_f1024"] = e(nc.sbuf_tensor("ones_f1024", [128, 1], F32))
    env["ones_b1024"] = e(nc.sbuf_tensor("ones_b1024", [128, 1], BF16))
    env["ones_r"] = e(nc.sbuf_tensor("ones_r", [1, 128], F32))
    env["eps"] = e(nc.sbuf_tensor("eps", [1, 1], F32))
    env["identf"] = e(nc.sbuf_tensor("identf", [128, 128], F32))
    env["row"] = e(nc.sbuf_tensor("row", [1, 2048], F32))
    env["xsq"] = e(nc.sbuf_tensor("xsq", [128, 1024], BF16))
    env["lnsc"] = e(nc.sbuf_tensor("lnsc", [128, 512], F32))
    env["pb"] = [e(nc.psum_tensor(f"pb{i}", [128, 512], F32)) for i in range(8)]


def build_l0ffn(dbg=None):
    nc = bass.Bass("TRN2", target_bir_lowering=False)
    x_in = nc.dram_tensor("x", [D, T], F32, kind="ExternalInput").ap()
    x_out = nc.dram_tensor("xo", [D, T], F32, kind="ExternalOutput").ap()
    hgw = nc.dram_tensor("gw", [D, NE], F32, kind="ExternalInput").ap()
    hgb = nc.dram_tensor("gb", [NE], F32, kind="ExternalInput").ap()
    hwg = nc.dram_tensor("wg", [10, D, IE], F32, kind="ExternalInput").ap()
    hwu = nc.dram_tensor("wu", [10, D, IE], F32, kind="ExternalInput").ap()
    hwd = nc.dram_tensor("wd", [10, IE, D], F32, kind="ExternalInput").ap()
    wdram = nc.dram_tensor("wdram", [NE * T], F32, kind="Internal").ap()
    qg = nc.dram_tensor("qg", [10, 128, 8 * 512], BF16, kind="Internal").ap()
    qu = nc.dram_tensor("qu", [10, 128, 8 * 512], BF16, kind="Internal").ap()
    qd = nc.dram_tensor("qd", [10, 128, 4 * 8 * 128], BF16, kind="Internal").ap()

    env = {"hwg": hwg, "hwu": hwu, "hwd": hwd, "wdram": wdram,
           "qdram_g": qg, "qdram_u": qu, "qdram_d": qd}
    with ExitStack() as ctx:
        e = ctx.enter_context
        _env_common(nc, ctx, env)
        env["cidx8"] = e(nc.sbuf_tensor("cidx8", [128, 128], F32))
        env["cidx4"] = e(nc.sbuf_tensor("cidx4", [128, 64], F32))
        env["sgw"] = e(nc.sbuf_tensor("sgw", [128, 64], BF16))
        env["sgb"] = e(nc.sbuf_tensor("sgb", [8, 1], F32))
        env["sfcT"] = e(nc.sbuf_tensor("sfcT", [8, T], F32))
        env["sgate_w"] = 1536
        env["sgate"] = e(nc.sbuf_tensor("sgate", [128, 1536], F32))
        env["cidx8P"] = e(nc.sbuf_tensor("cidx8P", [128, 128], F32))
        env["cidx4P"] = e(nc.sbuf_tensor("cidx4P", [128, 64], F32))
        env["swT"] = e(nc.sbuf_tensor("swT", [16, NE * 128], F32))
        env["swrow"] = e(nc.sbuf_tensor("swrow", [1, 1024], F32))
        sx = e(nc.sbuf_tensor("sx", [128, NCH * T], F32))
        sxn = e(nc.sbuf_tensor("sxn", [128, NCH * T], BF16))
        env["swg"] = e(nc.sbuf_tensor("swgb", [128, 2 * 8 * 512], BF16))
        env["swu"] = e(nc.sbuf_tensor("swub", [128, 2 * 8 * 512], BF16))
        env["swd"] = e(nc.sbuf_tensor("swdb", [128, 2 * 4 * 8 * 128], BF16))
        env["ssig"] = e(nc.sbuf_tensor("ssig", [128, 1024], BF16))
        env["stmp"] = e(nc.sbuf_tensor("stmp", [128, 512], BF16))
        env["shu"] = e(nc.sbuf_tensor("shu", [128, 4 * 512], BF16))

        p = P(nc)
        for s in ("q_sync", "q_gpsimd", "pe", "dv", "ac", "gp"):
            p.sem(s)
        cst = emit_consts(p, env)
        env["cst"] = cst
        tx = p.dma("gpsimd", sx[:, :].rearrange("p (c t) -> p c t", c=NCH),
                   x_in.rearrange("(c p) t -> p c t", p=128), stream="x")
        tgw = p.dma("gpsimd",
                    env["sgw"][:, :].rearrange("p (c e) -> p c e", c=NCH),
                    hgw.rearrange("(c p) e -> p c e", p=128), stream="gw")
        tgb = p.dma("gpsimd", env["sgb"][:, :], hgb[:, None], stream="gb")
        xn_tok, ln_done = emit_ln(
            p, env, sx, sxn, T,
            {"mean": env["pb"][6], "msq": env["pb"][5],
             "bA": env["pb"][4], "bB": env["pb"][3]},
            wait=[tx, cst, tgw, tgb])
        if dbg == "ln":
            st = p.dma("gpsimd", x_out.rearrange("(c p) t -> p c t", p=128),
                       sxn[:, :].rearrange("p (c t) -> p c t", c=NCH),
                       wait=[ln_done], stream="st")
        elif dbg in ("gate", "gate1", "gate2", "gate3"):
            gts = emit_l0_gating(p, env, sxn, xn_tok, env["pb"][2],
                                 env["pb"][1], dbg=dbg)
            if dbg == "gate1":
                st = p.dma("gpsimd", x_out[0:8, :], env["sfcT"][0:8, :],
                           wait=gts, stream="st")
                p.op("gpsimd", lambda E: E.engine_nop(), wait=[st])
                p.finish(ctx)
                return nc
            if dbg in ("gate2", "gate3"):
                st = p.dma("gpsimd", x_out[0:128, 0:1536],
                           env["sgate"][:, :], wait=gts, stream="st")
                p.op("gpsimd", lambda E: E.engine_nop(), wait=[st])
                p.finish(ctx)
                return nc
            st = None
            for e_ in range(8):
                tld = p.dma("gpsimd", env["sfcT"][0:1, :],
                            env["wdram"][e_ * T:(e_ + 1) * T],
                            wait=gts + [st], stream="dbg")
                st = p.dma("gpsimd", x_out[e_:e_ + 1, :], env["sfcT"][0:1, :],
                           wait=[tld], stream="st")
        else:
            x_upd = emit_l0ffn(p, env, sx, sxn, xn_tok, ln_done, True)
            st = p.dma("gpsimd", x_out.rearrange("(c p) t -> p c t", p=128),
                       sx[:, :].rearrange("p (c t) -> p c t", c=NCH),
                       wait=list(x_upd.values()), stream="st")
        p.op("gpsimd", lambda E: E.engine_nop(), wait=[st])
        p.finish(ctx)
    return nc


def build_l0attn(dbg=None):
    nc = bass.Bass("TRN2", target_bir_lowering=False)
    x_in = nc.dram_tensor("x", [D, T], F32, kind="ExternalInput").ap()
    xp_in = nc.dram_tensor("xp", [D, T], F32, kind="ExternalInput").ap()
    x_out = nc.dram_tensor("xo", [D, T], F32, kind="ExternalOutput").ap()
    hwqT = nc.dram_tensor("wqT", [D, D], F32, kind="ExternalInput").ap()
    hwkT = nc.dram_tensor("wkT", [D, D], F32, kind="ExternalInput").ap()
    hwvT = nc.dram_tensor("wvT", [D, D], F32, kind="ExternalInput").ap()
    hwoT = nc.dram_tensor("woT", [D, D], F32, kind="ExternalInput").ap()
    hmask_in = nc.dram_tensor("hmask", [128, 1], F32, kind="ExternalInput").ap()
    xnp_dram = nc.dram_tensor("xnp", [D, T], BF16, kind="Internal").ap()

    env = {"hwqT": hwqT, "hwkT": hwkT, "hwvT": hwvT, "hwoT": hwoT}
    with ExitStack() as ctx:
        e = ctx.enter_context
        _env_common(nc, ctx, env)
        env["ones_r64"] = e(nc.sbuf_tensor("ones_r64", [128, 64], F32))
        env["sUT"] = e(nc.sbuf_tensor("sUT", [128, 128], BF16))
        env["hmask"] = e(nc.sbuf_tensor("hmaskb", [128, 1], F32))
        sx = e(nc.sbuf_tensor("sx", [128, NCH * T], F32))
        sxn = e(nc.sbuf_tensor("sxn", [128, NCH * T], BF16))
        sxpf = e(nc.sbuf_tensor("sxpf", [128, 4 * 512], F32))
        sstg = e(nc.sbuf_tensor("sstg", [128, 2 * 512], BF16))
        env["sQ"] = e(nc.sbuf_tensor("sQ", [128, 2 * T], BF16))
        env["sKT"] = e(nc.sbuf_tensor("sKT", [128, 2 * T], BF16))
        env["sKtok"] = e(nc.sbuf_tensor("sKtok", [128, 16 * 128], BF16))
        env["sV_w"] = 16 * 130
        env["sV"] = e(nc.sbuf_tensor("sV", [128, 16 * 130], BF16))
        env["sxp"] = e(nc.sbuf_tensor("sxp", [128, 2 * 1024], BF16))
        env["sS"] = e(nc.sbuf_tensor("sS", [128, 130], BF16))
        env["sSf"] = e(nc.sbuf_tensor("sSf", [128, 130], F32))
        env["sAm"] = e(nc.sbuf_tensor("sAm", [128, 256], BF16))
        env["srcp"] = e(nc.sbuf_tensor("srcp", [1, 256], F32))
        env["sOutE"] = e(nc.sbuf_tensor("sOutE", [64, 256], BF16))
        env["sAtt"] = e(nc.sbuf_tensor("sAtt", [64, 2 * T], BF16))
        env["escr"] = e(nc.sbuf_tensor("escr", [128, 1024], F32))
        env["sw_w"] = 2 * 5 * 8 * 128
        env["swpair"] = e(nc.sbuf_tensor("swpair", [128, env["sw_w"]], BF16))

        p = P(nc)
        for s in ("q_sync", "q_gpsimd", "pe", "dv", "ac", "gp"):
            p.sem(s)
        cst = emit_consts(p, env)
        env["cst"] = cst
        if dbg != "lnpA":
            p.op("gpsimd", lambda E: E.memset(env["ones_r64"][:, :], 1.0))
            t_ut = p.op("gpsimd", lambda E: E.memset(env["sUT"][:, :], 1.0),
                        inc="gp")
            t_ut = p.op("gpsimd", lambda E: E.affine_select(
                env["sUT"][:, :], env["sUT"][:, :], [[1, 128]], ALU.is_ge, 0.0,
                channel_multiplier=-1), wait=[t_ut], inc="gp")
            t_v1 = p.op("gpsimd", lambda E: E.memset(
                env["sV"][:, :], 1.0), inc="gp")
            p.op("gpsimd", lambda E: E.memset(env["sQ"][64:128, :], 0.0))
            t_v1 = p.op("gpsimd", lambda E: E.memset(
                env["sKT"][64:128, :], 0.0), inc="gp")
            thm = p.dma("gpsimd", env["hmask"][:, :], hmask_in, stream="hm")
        else:
            t_ut = t_v1 = thm = None
        tx = p.dma("gpsimd", sx[:, :].rearrange("p (c t) -> p c t", c=NCH),
                   x_in.rearrange("(c p) t -> p c t", p=128), stream="x")
        xn_tok, ln_done = emit_ln(
            p, env, sx, sxn, T,
            {"mean": env["pb"][6], "msq": env["pb"][5],
             "bA": env["pb"][4], "bB": env["pb"][3]},
            wait=[tx, cst, t_ut, thm])
        slot_busy = {}
        slot_seq = [0]
        slot_of = {}
        slot_last = {}
        def part_provider(ch, ng, phase):
            slot = slot_seq[0] % 4
            slot_seq[0] += 1
            slot_of[(ch, ng, phase)] = slot
            buf = sxpf[:, slot * 512:(slot + 1) * 512]
            tok = p.dma("sync", buf,
                        xp_in[ch * 128:(ch + 1) * 128,
                              ng * 512:(ng + 1) * 512],
                        wait=list(slot_busy.get(slot, [])) +
                        [slot_last.get(slot)], stream="xpf%d" % slot)
            slot_last[slot] = tok
            return buf, tok
        def part_done(ch, ng, phase, tok):
            slot_busy[slot_of[(ch, ng, phase)]] = tok
        dst_dma = {}
        def part_dst(ch, ng):
            slot = ch % 2
            ds = sstg[:, slot * 512:(slot + 1) * 512]
            dwait = [dst_dma.get(slot)]
            def fin(tok):
                dt = p.dma("gpsimd",
                           xnp_dram[ch * 128:(ch + 1) * 128,
                                    ng * 512:(ng + 1) * 512], ds,
                           wait=[tok], stream="xnp%d" % slot)
                dst_dma[slot] = dt
                return dt
            return ds, dwait, fin
        _, lnp_done = emit_ln(
            p, env, None, None, T,
            {"mean": env["pb"][6], "msq": env["pb"][5],
             "bA": env["pb"][4], "bB": env["pb"][3]},
            wait=[cst, ln_done], provider=part_provider,
            dst_provider=part_dst, provider_done=part_done)
        env["v1_tok"] = t_v1
        if dbg == "lnpA":
            st = p.dma("gpsimd", x_out.rearrange("(c p) t -> p c t", p=128),
                       sxn[:, :].rearrange("p (c t) -> p c t", c=NCH),
                       wait=[ln_done], stream="st")
            p.op("gpsimd", lambda E: E.engine_nop(), wait=[st])
            p.finish(ctx)
            return nc
        if dbg == "lnp0":
            tw9 = p.dma("gpsimd", xnp_dram.rearrange("(c p) t -> p c t", p=128),
                        sxn[:, :].rearrange("p (c t) -> p c t", c=NCH),
                        wait=[ln_done], stream="xnp0")
            st = None
            for ch in range(NCH):
                tl = p.dma("gpsimd", sstg[:, 0:256],
                           xnp_dram[ch * 128:(ch + 1) * 128, 0:256],
                           wait=[tw9, st], stream="dbg")
                st = p.dma("gpsimd",
                           x_out[ch * 128:(ch + 1) * 128, 0:256],
                           sstg[:, 0:256], wait=[tl], stream="st")
            p.op("gpsimd", lambda E: E.engine_nop(), wait=[st])
            p.finish(ctx)
            return nc
        if dbg == "lnp":
            st = None
            for ch in range(NCH):
                tl = p.dma("gpsimd", sstg[:, 0:256],
                           xnp_dram[ch * 128:(ch + 1) * 128, 0:256],
                           wait=[dst_dma.get(0), dst_dma.get(1), lnp_done, st],
                           stream="dbg")
                st = p.dma("gpsimd",
                           x_out[ch * 128:(ch + 1) * 128, 0:256],
                           sstg[:, 0:256], wait=[tl], stream="st")
            p.op("gpsimd", lambda E: E.engine_nop(), wait=[st])
            p.finish(ctx)
            return nc
        nph = {"p1a": 1, "p1b": 2, "p1c": 3}.get(dbg, 4)
        x_att = emit_l0attn(p, env, sx, sxn, xn_tok, xnp_dram,
                            [dst_dma.get(0), dst_dma.get(1), lnp_done],
                            ln_done,
                            npairs=(1 if (dbg or "").startswith("p1") else 8),
                            nphase=nph)
        st = p.dma("gpsimd", x_out.rearrange("(c p) t -> p c t", p=128),
                   sx[:, :].rearrange("p (c t) -> p c t", c=NCH),
                   wait=list(x_att.values()), stream="st")
        p.op("gpsimd", lambda E: E.engine_nop(), wait=[st])
        p.finish(ctx)
    return nc


def build_l1ffn(first_call=True):
    nc = bass.Bass("TRN2", target_bir_lowering=False)
    x_in = nc.dram_tensor("x", [D, T], F32, kind="ExternalInput").ap()
    x_out = nc.dram_tensor("xo", [D, T], F32, kind="ExternalOutput").ap()
    hl1g = nc.dram_tensor("l1g", [I1, D], F32, kind="ExternalInput").ap()
    hl1u = nc.dram_tensor("l1u", [I1, D], F32, kind="ExternalInput").ap()
    hl1d = nc.dram_tensor("l1d", [D, I1], F32, kind="ExternalInput").ap()
    ql1g = nc.dram_tensor("ql1g", [I1, D], BF16, kind="Internal").ap()
    ql1u = nc.dram_tensor("ql1u", [I1, D], BF16, kind="Internal").ap()
    ql1d = nc.dram_tensor("ql1d", [D, I1], BF16, kind="Internal").ap()
    env = {"hl1g": hl1g, "hl1u": hl1u, "hl1d": hl1d,
           "ql1g": ql1g, "ql1u": ql1u, "ql1d": ql1d}
    with ExitStack() as ctx:
        e = ctx.enter_context
        env["ones_f1024"] = e(nc.sbuf_tensor("ones_f1024", [128, 1], F32))
        env["ones_b1024"] = e(nc.sbuf_tensor("ones_b1024", [128, 1], BF16))
        env["ones_r"] = e(nc.sbuf_tensor("ones_r", [1, 128], F32))
        env["eps"] = e(nc.sbuf_tensor("eps", [1, 1], F32))
        env["identf"] = e(nc.sbuf_tensor("identf", [128, 128], F32))
        env["identb"] = e(nc.sbuf_tensor("identb", [128, 128], BF16))
        env["row"] = e(nc.sbuf_tensor("row", [1, 2048], F32))
        env["xsq"] = e(nc.sbuf_tensor("xsq", [128, 1024], BF16))
        env["lnsc"] = e(nc.sbuf_tensor("lnsc", [128, 512], F32))
        env["pb"] = [e(nc.psum_tensor(f"pb{i}", [128, 512], F32))
                     for i in range(7)]
        env["pbt"] = e(nc.psum_tensor("pbt", [128, 1024], BF16))
        sx = e(nc.sbuf_tensor("sx", [128, NCH * T], F32))
        sxn = e(nc.sbuf_tensor("sxn", [128, NCH * T], BF16))
        env["sxq"] = e(nc.sbuf_tensor("sxq", [128, 8 * 512], BF16))
        env["shq"] = e(nc.sbuf_tensor("shq", [128, 22 * 512], BF16))
        env["sxt"] = e(nc.sbuf_tensor("sxt", [128, 1024], BF16))
        env["sxtq"] = e(nc.sbuf_tensor("sxtq", [128, 1024], BF16))
        env["ssil"] = e(nc.sbuf_tensor("ssil", [128, 512], F32))
        env["shts"] = e(nc.sbuf_tensor("shts", [128, 512], F32))
        env["shtq"] = e(nc.sbuf_tensor("shtq", [128, 512], BF16))
        env["swg1"] = e(nc.sbuf_tensor("swg1", [128, 2 * 8 * 512], BF16))
        env["swu1"] = e(nc.sbuf_tensor("swu1", [128, 2 * 8 * 512], BF16))
        env["swd1"] = e(nc.sbuf_tensor("swd1", [128, 512], BF16))
        env["qwbuf"] = e(nc.sbuf_tensor("qwbuf", [128, 1024], F32))
        env["qwout"] = e(nc.sbuf_tensor("qwout", [128, 1024], BF16))
        env["qz"] = e(nc.sbuf_tensor("qz", [128, 512], F32))
        env["qa"] = e(nc.sbuf_tensor("qa", [128, 512], F32))
        env["qb"] = e(nc.sbuf_tensor("qb", [128, 512], F32))
        env["qs"] = e(nc.sbuf_tensor("qs", [128, 512], F32))
        env["qbm_w"] = 32
        env["qbm"] = e(nc.sbuf_tensor("qbm", [128, 32], F32))
        env["qsc_w"] = 32
        env["qsc"] = e(nc.sbuf_tensor("qsc", [128, 32], F32))

        p = P(nc)
        for s in ("q_sync", "q_gpsimd", "pe", "dv", "ac", "gp"):
            p.sem(s)
        cst = emit_consts(p, env)
        env["cst"] = cst
        tx = p.dma("gpsimd", sx[:, :].rearrange("p (c t) -> p c t", c=NCH),
                   x_in.rearrange("(c p) t -> p c t", p=128), stream="x")
        wq_toks = emit_l1_wquant(p, env, None)
        xn_tok, ln_done = emit_ln(
            p, env, sx, sxn, T,
            {"mean": env["pb"][2], "msq": env["pb"][3],
             "bA": env["pb"][4], "bB": env["pb"][5]},
            wait=[tx, cst])
        x_upd = emit_l1ffn(p, env, sx, sxn, xn_tok, ln_done, True, wq_toks)
        st = p.dma("gpsimd", x_out.rearrange("(c p) t -> p c t", p=128),
                   sx[:, :].rearrange("p (c t) -> p c t", c=NCH),
                   wait=list(x_upd.values()), stream="st")
        p.op("gpsimd", lambda E: E.engine_nop(), wait=[st])
        p.finish(ctx)
    return nc


def build_l1attn(dbg=None):
    nc = bass.Bass("TRN2", target_bir_lowering=False)
    x_in = nc.dram_tensor("x", [D, T], F32, kind="ExternalInput").ap()
    xg0_in = nc.dram_tensor("xg0", [D, T], F32, kind="ExternalInput").ap()
    xg1_in = nc.dram_tensor("xg1", [D, T], F32, kind="ExternalInput").ap()
    x_out = nc.dram_tensor("xo", [D, T], F32, kind="ExternalOutput").ap()
    hwqT = nc.dram_tensor("wqT", [D, D], F32, kind="ExternalInput").ap()
    hwkT = nc.dram_tensor("wkT", [D, D], F32, kind="ExternalInput").ap()
    hwvT = nc.dram_tensor("wvT", [D, D], F32, kind="ExternalInput").ap()
    hwoT = nc.dram_tensor("woT", [D, D], F32, kind="ExternalInput").ap()
    ub_in = nc.dram_tensor("ub", [16, 6144], F32, kind="ExternalInput").ap()
    bq_in = nc.dram_tensor("bq", [64, 16], F32, kind="ExternalInput").ap()
    bk_in = nc.dram_tensor("bk", [64, 16], F32, kind="ExternalInput").ap()
    bv_in = nc.dram_tensor("bv", [64, 16], F32, kind="ExternalInput").ap()
    bo_in = nc.dram_tensor("bo", [128, 8], F32, kind="ExternalInput").ap()
    ubexp = nc.dram_tensor("ubexp", [16, 6144], BF16, kind="Internal").ap()
    qsid = nc.dram_tensor("qsid", [16, 64, 2048], BF16, kind="Internal").ap()
    ksid = nc.dram_tensor("ksid", [16, 64, 4096], BF16, kind="Internal").ap()
    vsid = nc.dram_tensor("vsid", [16, 128, 2080], BF16, kind="Internal").ap()
    osid = nc.dram_tensor("osid", [16, 64, 2048], BF16, kind="Internal").ap()
    env = {"hwqT": hwqT, "hwkT": hwkT, "hwvT": hwvT, "hwoT": hwoT,
           "ub_in": ub_in, "ubexp": ubexp, "qsid": qsid, "ksid": ksid,
           "vsid": vsid, "osid": osid, "x_in": x_in, "xo": x_out}
    with ExitStack() as ctx:
        e = ctx.enter_context
        env["ones_f1024"] = e(nc.sbuf_tensor("ones_f1024", [128, 1], F32))
        env["ones_b1024"] = e(nc.sbuf_tensor("ones_b1024", [128, 1], BF16))
        env["ones_r"] = e(nc.sbuf_tensor("ones_r", [1, 128], F32))
        env["eps"] = e(nc.sbuf_tensor("eps", [1, 1], F32))
        env["identf"] = e(nc.sbuf_tensor("identf", [128, 128], F32))
        env["onesc"] = e(nc.sbuf_tensor("onesc", [64, 1], BF16))
        env["szero"] = e(nc.sbuf_tensor("szero", [1, 1], F32))
        env["snegm"] = e(nc.sbuf_tensor("snegm", [128, 16], F32))
        env["sbq"] = e(nc.sbuf_tensor("sbq", [64, 16], F32))
        env["sbqs"] = e(nc.sbuf_tensor("sbqs", [64, 16], F32))
        env["sbk"] = e(nc.sbuf_tensor("sbk", [64, 16], F32))
        env["sbv"] = e(nc.sbuf_tensor("sbv", [64, 16], F32))
        env["sbo"] = e(nc.sbuf_tensor("sbo", [128, 8], F32))
        env["pb"] = [e(nc.psum_tensor(f"pb{i}", [128, 512], F32))
                     for i in range(8)]
        p = P(nc)
        for s in ("q_sync", "q_gpsimd", "pe", "dv", "ac", "gp"):
            p.sem(s)

        stackA = ExitStack()
        eA = stackA.enter_context
        env["row"] = eA(nc.sbuf_tensor("row", [1, 2048], F32))
        env["xsq"] = eA(nc.sbuf_tensor("xsq", [128, 1024], BF16))
        env["lnsc"] = eA(nc.sbuf_tensor("lnsc", [128, 512], F32))
        sxpf = eA(nc.sbuf_tensor("sxpf", [128, 4 * 512], F32))
        env["sxn"] = eA(nc.sbuf_tensor("sxn", [128, NCH * T], BF16))
        env["sxg"] = eA(nc.sbuf_tensor("sxg", [128, NCH * KT], BF16))
        env["swq"] = eA(nc.sbuf_tensor("swq", [128, 1024], BF16))
        env["swk"] = eA(nc.sbuf_tensor("swk", [128, 1024], BF16))
        env["swv"] = eA(nc.sbuf_tensor("swv", [128, 1024], BF16))
        env["qsb"] = eA(nc.sbuf_tensor("qsb", [64, 2048], BF16))
        env["ksb"] = eA(nc.sbuf_tensor("ksb", [64, 4096], BF16))
        env["sv"] = eA(nc.sbuf_tensor("sv", [128, 2080], BF16))
        env["sq2"] = eA(nc.sbuf_tensor("sq2", [64, 512], BF16))
        env["rnorm"] = eA(nc.sbuf_tensor("rnorm", [1, 12], F32))
        env["negrow"] = eA(nc.sbuf_tensor("negrow", [1, 16], F32))
        env["mrow"] = eA(nc.sbuf_tensor("mrow", [1, 4], F32))
        env["subf"] = eA(nc.sbuf_tensor("subf", [16, 6144], F32))
        env["subb"] = eA(nc.sbuf_tensor("subb", [16, 6144], BF16))

        cst = emit_consts(p, env)
        p.op("gpsimd", lambda E: E.memset(env["onesc"][:, :], 1.0))
        cst = p.op("gpsimd", lambda E: E.memset(env["szero"][:, :], 0.0),
                   inc="gp")
        env["cst"] = cst
        env["bq_tok"] = p.dma("gpsimd", env["sbq"][:, :], bq_in, stream="bq")
        env["bk_tok"] = p.dma("gpsimd", env["sbk"][:, :], bk_in, stream="bk")
        env["bv_tok"] = p.dma("gpsimd", env["sbv"][:, :], bv_in, stream="bv")
        env["bo_tok"] = p.dma("gpsimd", env["sbo"][:, :], bo_in, stream="bo")

        slot_busy, slot_seq, slot_of, slot_last = {}, [0], {}, {}

        def mk_prov(srcd, tag):
            def prov(ch, ng, phase):
                slot = slot_seq[0] % 4
                slot_seq[0] += 1
                slot_of[(tag, ch, ng, phase)] = slot
                buf = sxpf[:, slot * 512:(slot + 1) * 512]
                tok = p.dma("sync", buf,
                            srcd[ch * 128:(ch + 1) * 128,
                                 ng * 512:(ng + 1) * 512],
                            wait=list(slot_busy.get(slot, [])) +
                            [slot_last.get(slot)], stream="xpf%d" % slot)
                slot_last[slot] = tok
                return buf, tok

            def done(ch, ng, phase, toks):
                slot_busy[slot_of[(tag, ch, ng, phase)]] = toks
            return prov, done

        def mk_dst(off):
            def dstp(ch, ng):
                c0 = ch * KT + off + ng * 512
                return env["sxg"][:, c0:c0 + 512], [], None
            return dstp

        lnb = {"mean": env["pb"][2], "msq": env["pb"][3],
               "bA": env["pb"][4], "bB": env["pb"][5]}
        pv, pd = mk_prov(x_in, 0)
        _, ln0 = emit_ln(p, env, None, env["sxn"], T, lnb,
                         wait=[cst], provider=pv, provider_done=pd)
        pv, pd = mk_prov(xg0_in, 1)
        _, ln1 = emit_ln(p, env, None, None, T, lnb, wait=[cst, ln0],
                         provider=pv, dst_provider=mk_dst(0),
                         provider_done=pd)
        pv, pd = mk_prov(xg1_in, 2)
        _, ln2 = emit_ln(p, env, None, None, T, lnb, wait=[cst, ln1],
                         provider=pv, dst_provider=mk_dst(2048),
                         provider_done=pd)
        env["lnlast"] = ln2
        if dbg == "ln":
            st = p.dma("gpsimd", x_out.rearrange("(c p) t -> p c t", p=128),
                       env["sxn"][:, :].rearrange("p (c t) -> p c t", c=NCH),
                       wait=[ln2], stream="st")
            p.op("gpsimd", lambda E: E.engine_nop(), wait=[st])
            p.finish(ctx)
            stackA.close()
            return nc
        pa = emit_l1attn_projA(p, env)
        if dbg in ("projq", "projk"):
            srcb = env["qsb"] if dbg == "projq" else env["ksb"]
            w = 2048 if dbg == "projq" else 4096
            st = p.dma("gpsimd", bass.AP(tap(x_out), 0, [[w, 64], [1, w]]),
                       srcb[0:64, 0:w], wait=pa, stream="st")
            p.op("gpsimd", lambda E: E.engine_nop(), wait=[st])
            p.finish(ctx)
            stackA.close()
            return nc
        stackA.close()

        stackB = ExitStack()
        eB = stackB.enter_context
        env["qsb2"] = eB(nc.sbuf_tensor("qsb2", [64, 2048], BF16))
        env["ksb2"] = eB(nc.sbuf_tensor("ksb2", [64, 4096], BF16))
        env["sv2"] = eB(nc.sbuf_tensor("sv2", [128, 2080], BF16))
        env["sub"] = eB(nc.sbuf_tensor("sub", [128, 6016], BF16))
        env["Eb"] = eB(nc.sbuf_tensor("Eb", [128, 32 * 512], BF16))
        env["escr"] = eB(nc.sbuf_tensor("escr", [128, 1024], BF16))
        env["onum"] = eB(nc.sbuf_tensor("onum", [64, 512], F32))
        env["of32"] = eB(nc.sbuf_tensor("of32", [64, 512], F32))
        env["osb"] = eB(nc.sbuf_tensor("osb", [64, 512], BF16))
        env["srcp"] = eB(nc.sbuf_tensor("srcp", [1, 512], F32))
        ps = emit_l1attn_scores(p, env, pa)
        stackB.close()
        if dbg == "o":
            st = p.dma("gpsimd", bass.AP(tap(x_out), 0,
                                         [[2048, 1024], [1, 2048]]),
                       bass.AP(tap(osid), 0, [[2048, 1024], [1, 2048]]),
                       wait=ps, stream="st")
            p.op("gpsimd", lambda E: E.engine_nop(), wait=[st])
            p.finish(ctx)
            return nc

        env["swo"] = e(nc.sbuf_tensor("swo", [64, 16 * 1024], BF16))
        env["so_r"] = e(nc.sbuf_tensor("so_r", [64, 16 * 512], BF16))
        env["sxi"] = e(nc.sbuf_tensor("sxi", [128, 1024], F32))
        env["sxo"] = e(nc.sbuf_tensor("sxo", [128, 1024], F32))
        xst = emit_l1attn_wo(p, env, ps)
        p.op("gpsimd", lambda E: E.engine_nop(), wait=xst)
        p.finish(ctx)
    return nc





def _prep_l0ffn_inputs(inputs):
    swg = np.ascontiguousarray(inputs["L0_ffn_sw_gate"], dtype=np.float32)
    swu = np.ascontiguousarray(inputs["L0_ffn_sw_up"], dtype=np.float32)
    swd = np.ascontiguousarray(inputs["L0_ffn_sw_down"], dtype=np.float32)
    wg = np.asarray(inputs["L0_ffn_w_gate"], np.float32)
    wu = np.asarray(inputs["L0_ffn_w_up"], np.float32)
    wd = np.asarray(inputs["L0_ffn_w_down"], np.float32)
    wg10 = np.concatenate(
        [wg, swg[:, 0:512][None], swg[:, 512:1024][None]], axis=0)
    wu10 = np.concatenate(
        [wu, swu[:, 0:512][None], swu[:, 512:1024][None]], axis=0)
    wd10 = np.concatenate(
        [wd, swd[0:512, :][None], swd[512:1024, :][None]], axis=0)
    return {
        "gw": np.ascontiguousarray(inputs["L0_ffn_gate_w"], dtype=np.float32),
        "gb": np.ascontiguousarray(inputs["L0_ffn_gate_bias"],
                                   dtype=np.float32),
        "wg": np.ascontiguousarray(wg10),
        "wu": np.ascontiguousarray(wu10),
        "wd": np.ascontiguousarray(wd10),
    }


def _shard_x(x_full):
    out = []
    for c in range(8):
        b, h = c // 2, c % 2
        sl = x_full[b, h * T:(h + 1) * T, :]
        out.append(np.ascontiguousarray(sl.T.astype(np.float32)))
    return out


def _unshard_x(shards):
    x = np.empty((4, 4096, D), np.float32)
    for c in range(8):
        b, h = c // 2, c % 2
        x[b, h * T:(h + 1) * T, :] = shards[c].T
    return x


def _prep_l0attn_inputs(inputs):
    return {
        "wqT": np.ascontiguousarray(
            np.asarray(inputs["L0_attn_Wq"], np.float32).T),
        "wkT": np.ascontiguousarray(
            np.asarray(inputs["L0_attn_Wk"], np.float32).T),
        "wvT": np.ascontiguousarray(
            np.asarray(inputs["L0_attn_Wv"], np.float32).T),
        "woT": np.ascontiguousarray(
            np.asarray(inputs["L0_attn_Wo"], np.float32).T),
    }


def _build_wq_slabs(inputs):
    g = np.asarray(inputs["L1_ffn_w_gate"], np.float32)
    u = np.asarray(inputs["L1_ffn_w_up"], np.float32)
    d = np.asarray(inputs["L1_ffn_w_down"], np.float32)
    tiles = []
    for src, nit, nsub in ((g, 22, 8), (u, 22, 8), (d, 8, 22)):
        for it in range(nit):
            for sub in range(nsub):
                tiles.append(src[it * 128:(it + 1) * 128,
                                 sub * 128:(sub + 1) * 128])
    A = np.concatenate(tiles, axis=1)
    W = QU_PER_CORE * 128
    return [np.ascontiguousarray(A[:, c * W:(c + 1) * W]) for c in range(8)]


def _prep_fused_inputs(inputs):
    base = {}
    base.update(_prep_l0ffn_inputs(inputs))
    base.update(_prep_l0attn_inputs(inputs))
    a, ubs = _prep_l1attn_inputs(inputs)
    base["aqT"] = a["wqT"]
    base["akT"] = a["wkT"]
    base["avT"] = a["wvT"]
    base["aoT"] = a["woT"]
    for k in ("bq", "bk", "bv", "bo"):
        base[k] = a[k]
    return base, ubs


def _prep_l1attn_inputs(inputs):
    pbias = np.ascontiguousarray(inputs["L1_attn_pos_bias"], np.float32)
    w = np.arange(6144)
    ubs = []
    for half in (0, 1):
        d = w - 2047 - 2048 * half
        idx = np.clip(4095 - d, 0, 4094).astype(np.int64)
        ubs.append(np.ascontiguousarray(pbias[:, idx], np.float32))
    base = {
        "wqT": np.ascontiguousarray(
            np.asarray(inputs["L1_attn_Wq"], np.float32).T),
        "wkT": np.ascontiguousarray(
            np.asarray(inputs["L1_attn_Wk"], np.float32).T),
        "wvT": np.ascontiguousarray(
            np.asarray(inputs["L1_attn_Wv"], np.float32).T),
        "woT": np.ascontiguousarray(
            np.asarray(inputs["L1_attn_Wo"], np.float32).T),
        "bq": np.ascontiguousarray(
            np.asarray(inputs["L1_attn_bq"], np.float32).reshape(16, 64).T),
        "bk": np.ascontiguousarray(
            np.asarray(inputs["L1_attn_bk"], np.float32).reshape(16, 64).T),
        "bv": np.ascontiguousarray(
            np.asarray(inputs["L1_attn_bv"], np.float32).reshape(16, 64).T),
        "bo": np.ascontiguousarray(
            np.asarray(inputs["L1_attn_bo"], np.float32).reshape(8, 128).T),
    }
    return base, ubs


def _prep_l1ffn_inputs(inputs):
    return {
        "l1g": np.ascontiguousarray(inputs["L1_ffn_w_gate"], dtype=np.float32),
        "l1u": np.ascontiguousarray(inputs["L1_ffn_w_up"], dtype=np.float32),
        "l1d": np.ascontiguousarray(inputs["L1_ffn_w_down"], dtype=np.float32),
    }


def kernel(dbg=None, **inputs):
    if FUSED and dbg is None:
        ncf = build_fused()
        base, ubs = _prep_fused_inputs(inputs)
        slabs = _build_wq_slabs(inputs)
        shards = _shard_x(np.asarray(inputs["L0_attn_x"], np.float32))
        in_maps = [dict(base, x=shards[c],
                        hmask=np.full((128, 1), c % 2, np.float32),
                        ub=ubs[c % 2], wqsrc=slabs[c]) for c in range(8)]
        res = run_bass_kernel_spmd(ncf, in_maps, core_ids=list(range(8)))
        return _unshard_x([res.results[c]["xo"] for c in range(8)])
    attn_dbg = dbg if dbg in ("lnp", "lnp0", "lnpA", "p1", "p1a", "p1b", "p1c") else None
    if attn_dbg:
        dbg = None
    x = np.asarray(inputs["L0_attn_x"], np.float32)
    shards = _shard_x(x)
    base = _prep_l0ffn_inputs(inputs)
    baseb = _prep_l0attn_inputs(inputs)
    nca = build_l0ffn(dbg)
    ncb = None
    if dbg is None and STOP_AFTER >= 2:
        ncb = build_l0attn(attn_dbg)
    ncd = nce = None
    if dbg is None and attn_dbg is None and STOP_AFTER >= 4:
        ncd = build_l1ffn()
        based = _prep_l1ffn_inputs(inputs)
    if dbg is None and attn_dbg is None and STOP_AFTER >= 5:
        nce = build_l1attn()
        basee, ubs = _prep_l1attn_inputs(inputs)
    in_maps = [dict(base, x=shards[c]) for c in range(8)]
    res = run_bass_kernel_spmd(nca, in_maps, core_ids=list(range(8)))
    shards = [res.results[c]["xo"] for c in range(8)]
    if ncb is not None:
        in_maps = [dict(baseb, x=shards[c], xp=shards[c ^ 1],
                        hmask=np.full((128, 1), float(c % 2), np.float32))
                   for c in range(8)]
        res = run_bass_kernel_spmd(ncb, in_maps, core_ids=list(range(8)))
        shards = [res.results[c]["xo"] for c in range(8)]
    if STOP_AFTER >= 3:
        in_maps = [dict(base, x=shards[c]) for c in range(8)]
        res = run_bass_kernel_spmd(nca, in_maps, core_ids=list(range(8)))
        shards = [res.results[c]["xo"] for c in range(8)]
    if ncd is not None:
        in_maps = [dict(based, x=shards[c]) for c in range(8)]
        res = run_bass_kernel_spmd(ncd, in_maps, core_ids=list(range(8)))
        shards = [res.results[c]["xo"] for c in range(8)]
    if nce is not None:
        in_maps = [dict(basee, x=shards[c], xg0=shards[2 * (c // 2)],
                        xg1=shards[2 * (c // 2) + 1], ub=ubs[c % 2])
                   for c in range(8)]
        res = run_bass_kernel_spmd(nce, in_maps, core_ids=list(range(8)))
        shards = [res.results[c]["xo"] for c in range(8)]
    if nce is not None and STOP_AFTER >= 6:
        in_maps = [dict(based, x=shards[c]) for c in range(8)]
        res = run_bass_kernel_spmd(ncd, in_maps, core_ids=list(range(8)))
        shards = [res.results[c]["xo"] for c in range(8)]
    return _unshard_x(shards)


STOP_AFTER = 99
FUSED = True



# version 13
# speedup vs baseline: 1.1762x; 1.1762x over previous
import numpy as np
import ml_dtypes
import concourse.bass as bass
import concourse.mybir as mybir
from concourse.bass_utils import run_bass_kernel_spmd
from contextlib import ExitStack

F32 = mybir.dt.float32
BF16 = mybir.dt.bfloat16
U8 = mybir.dt.uint8
AF = mybir.ActivationFunctionType
ALU = mybir.AluOpType
AX = mybir.AxisListType.X
BF16NP = ml_dtypes.bfloat16

ENGS = ("sync", "tensor", "vector", "scalar", "gpsimd")

D = 1024
T = 2048
NH = 16
DH = 64
NE = 8
IE = 512
SHI = 1024
I1 = 2816
NCH = D // 128


def tap(x):
    return x.tensor if hasattr(x, "tensor") else x


class P:

    def __init__(self, nc):
        self.nc = nc
        self.q = {e: [] for e in ENGS}
        self.cnt = {}
        self.wm = {e: {} for e in ENGS}
        self.sems = {}
        self.semnames = []
        self.lastop = {}

    def sem(self, name):
        if name not in self.cnt:
            self.cnt[name] = 0
            self.semnames.append(name)
        return name

    BARRIER_SEMS = ("pe", "dv", "ac", "gp")

    def wait(self, eng, toks):
        for t in toks:
            if t is None:
                continue
            sname, val, peng = t
            if peng == eng and eng != "gpsimd":
                continue
            if sname in self.BARRIER_SEMS:
                val = self.cnt[sname]
                if val == 0:
                    continue
            if self.wm[eng].get(sname, 0) >= val:
                continue
            self.wm[eng][sname] = val
            self.q[eng].append(
                lambda E, s=sname, v=val: E.wait_ge(self.sems[s], v))

    SERIAL = {"vector": "dv", "scalar": "ac", "gpsimd": "gp"}

    def op(self, eng, fn, wait=(), inc=None, n=1, exact=False):
        ser = self.SERIAL.get(eng)
        if ser is not None:
            lt = self.lastop.get(eng)
            if lt is not None:
                self.wait(eng, [(lt[0], lt[1], "__self__")])
            inc = (ser + "x") if exact else ser
            self.sem(inc)
            n = 1
        self.wait(eng, wait)
        tok = None
        if inc is not None:
            self.sem(inc)
            self.cnt[inc] += n
            tok = (inc, self.cnt[inc], eng)
            self.q[eng].append(
                lambda E, f=fn, s=inc, v=n: f(E).then_inc(self.sems[s], v))
        else:
            self.q[eng].append(lambda E, f=fn: f(E))
        if ser is not None:
            self.lastop[eng] = tok
        return tok

    STREAM_ALIAS = {}

    def dma(self, eng, out, in_, wait=(), stream=None, **kw):
        stream = stream if stream is not None else eng
        s = "q_" + self.STREAM_ALIAS.get(stream, stream)
        self.sem(s)
        if self.cnt[s] > 0:
            self.wait(eng, [(s, self.cnt[s], "__dma__")])
        self.wait(eng, wait)
        self.cnt[s] += 16
        tok = (s, self.cnt[s], "__dma__")
        self.q[eng].append(
            lambda E, o=out, i=in_, sn=s, kw=kw:
            E.dma_start(o, i, **kw).then_inc(self.sems[sn], 16))
        return tok

    def barrier(self, toks):
        for eng in ENGS:
            self.wait(eng, toks)

    def finish(self, ctx):
        nc = self.nc
        for name in self.semnames:
            assert self.cnt[name] < 60000, (name, self.cnt[name])
            self.sems[name] = ctx.enter_context(nc.semaphore(name))
        block = ctx.enter_context(nc.Block())
        for eng in ENGS:
            ops = self.q[eng]
            if not ops:
                continue
            def replay(E, ops=ops):
                for f in ops:
                    f(E)
            getattr(block, eng)(replay)


def emit_consts(p, env):
    g = "gpsimd"
    p.op(g, lambda E: E.memset(env["ones_f1024"][:, :], 1.0 / 1024.0))
    p.op(g, lambda E: E.memset(env["ones_b1024"][:, :], 1.0 / 1024.0))
    p.op(g, lambda E: E.memset(env["ones_r"][:, :], 1.0))
    p.op(g, lambda E: E.memset(env["eps"][:, :], 1e-5))
    t = p.op(g, lambda E: E.memset(env["identf"][:, :], 1.0), inc="gp")
    t = p.op(g, lambda E: E.affine_select(
        env["identf"][:, :], env["identf"][:, :], [[1, 128]], ALU.is_equal, 0.0,
        channel_multiplier=-1), wait=[t], inc="gp")
    if "cidx8" in env:
        p.op(g, lambda E: E.iota(env["cidx8"][:, :], [[0, 16], [1, 8]],
                                 channel_multiplier=0,
                                 allow_small_or_imprecise_dtypes=True))
        p.op(g, lambda E: E.iota(env["cidx4"][:, :], [[0, 16], [1, 4]],
                                 channel_multiplier=0,
                                 allow_small_or_imprecise_dtypes=True))
        p.op(g, lambda E: E.iota(env["cidx8P"][:, :], [[0, 16], [1, 8]],
                                 base=90, channel_multiplier=0,
                                 allow_small_or_imprecise_dtypes=True))
        p.op(g, lambda E: E.iota(env["cidx4P"][:, :], [[0, 16], [1, 4]],
                                 base=90, channel_multiplier=0,
                                 allow_small_or_imprecise_dtypes=True))
    if "identb" in env:
        t2 = p.op(g, lambda E: E.memset(env["identb"][:, :], 1.0), inc="gp")
        p.op(g, lambda E: E.affine_select(
            env["identb"][:, :], env["identb"][:, :], [[1, 128]], ALU.is_equal,
            0.0, channel_multiplier=-1), wait=[t2], inc="gp")
    return p.op(g, lambda E: E.engine_nop(), inc="gp")


def emit_ln(p, env, src_sb, dst_sb, ntok, pbanks, wait=(), provider=None,
            dst_provider=None, provider_done=None):
    if provider is None:
        def provider(ch, ng, phase):
            c0 = ng * 512
            return src_sb[:, ch * ntok + c0: ch * ntok + c0 + 512], None
    row = env["row"]
    xsq = env["xsq"]
    mean_ps, msq_ps = pbanks["mean"], pbanks["msq"]
    bA, bB = pbanks["bA"], pbanks["bB"]
    NG = ntok // 512
    out_toks = {}
    mm_toks = {}
    row_free = {}
    for ng in range(NG):
        c0 = ng * 512
        par = 0
        mrow = row[0:1, 0:512]
        vrow = row[0:1, 512:1024]
        rrow = row[0:1, 1024:1536]
        nrow = row[0:1, 1536:2048]
        for ch in range(NCH):
            xs, xtok = provider(ch, ng, 0)
            xpar = (ng * NCH + ch) % 2
            xq = xsq[:, xpar * 512:(xpar + 1) * 512]
            sq = p.op("scalar", lambda E, i=xs, o=xq: E.activation(
                o, i, AF.Square),
                wait=list(wait) + [mm_toks.get(ng * NCH + ch - 2), xtok],
                inc="ac")
            mt = p.op("tensor", lambda E, i=xs, ch=ch: E.matmul(
                mean_ps[0:1, 0:512], env["ones_f1024"][:, 0:1], i,
                start=(ch == 0), stop=(ch == NCH - 1)),
                wait=(list(wait) + [row_free.get(("m", par)), xtok]
                      if ch == 0 else [xtok]), inc="pe")
            mm_toks[ng * NCH + ch] = p.op(
                "tensor", lambda E, xq=xq, ch=ch: E.matmul(
                    msq_ps[0:1, 0:512], env["ones_b1024"][:, 0:1], xq,
                    start=(ch == 0), stop=(ch == NCH - 1)),
                wait=([sq, row_free.get(("m", par))] if ch == 0 else [sq]),
                inc="pe")
            if provider_done is not None:
                provider_done(ch, ng, 0, [sq, mt, mm_toks[ng * NCH + ch]])
        stat_tok = mm_toks[ng * NCH + NCH - 1]
        p.op("vector", lambda E, mrow=mrow: E.tensor_copy(
            mrow, mean_ps[0:1, 0:512]),
            wait=[stat_tok, row_free.get(("bc", ng - 1))])
        p.op("vector", lambda E, mrow=mrow, vrow=vrow: E.tensor_mul(
            vrow, mrow, mrow))
        t3 = p.op("vector", lambda E, vrow=vrow: E.tensor_sub(
            vrow, msq_ps[0:1, 0:512], vrow), inc="dv")
        t4 = p.op("scalar", lambda E, vrow=vrow, rrow=rrow: E.activation(
            rrow, vrow, AF.Sqrt, bias=env["eps"][0:1, 0:1]),
            wait=[t3], inc="ac")
        t5 = p.op("vector", lambda E, rrow=rrow: E.reciprocal(rrow, rrow),
                  wait=[t4], inc="dv")
        t6 = p.op("vector", lambda E, mrow=mrow, rrow=rrow, nrow=nrow:
                  E.scalar_tensor_tensor(nrow, mrow, -1.0, rrow,
                                         ALU.mult, ALU.mult), inc="dv")
        row_free[("m", par)] = t6
        war = [out_toks.get((NCH - 1, ng - 1))]
        tb1 = p.op("tensor", lambda E, rrow=rrow: E.matmul(
            bA[:, 0:512], env["ones_r"][0:1, 0:128], rrow,
            start=True, stop=True), wait=[t5] + war, inc="pe")
        tb2 = p.op("tensor", lambda E, nrow=nrow: E.matmul(
            bB[:, 0:512], env["ones_r"][0:1, 0:128], nrow,
            start=True, stop=True), wait=[t6] + war, inc="pe")
        row_free[("bc", ng)] = tb2
        for ch in range(NCH):
            xs, xtok = provider(ch, ng, 1)
            if dst_provider is not None:
                ds, dwait, fin = dst_provider(ch, ng)
            else:
                ds = dst_sb[:, ch * ntok + c0: ch * ntok + c0 + 512]
                dwait, fin = [], None
            sc = env["lnsc"][:, 0:512]
            p.op("vector", lambda E, xs=xs, sc=sc: E.tensor_tensor(
                sc, xs, bA[:, 0:512], ALU.mult),
                wait=[tb1, tb2, xtok] + list(dwait))
            out_toks[(ch, ng)] = p.op(
                "vector", lambda E, sc=sc, ds=ds: E.tensor_tensor(
                    ds, sc, bB[:, 0:512], ALU.add), exact=True)
            if fin is not None:
                out_toks[(ch, ng)] = fin(out_toks[(ch, ng)])
            if provider_done is not None:
                provider_done(ch, ng, 1, [out_toks[(ch, ng)]])
    last = out_toks[(NCH - 1, NG - 1)]
    return (lambda ch, ng: out_toks[(ch, ng)]), last


def emit_l0_gating(p, env, xn_sb, xn_tok, pbL, pbT, dbg=None):
    cst = env.get("cst")
    """Routing weights -> wdram [NE*T] f32 (w-hat per expert, token order)."""
    sfcT = env["sfcT"]
    sg = env["sgate"]
    swT = env["swT"]
    sgate_t = tap(env["sgate"])
    W = env["sgate_w"]
    NG = T // 512
    stoks = []
    for ng in range(NG):
        c0 = ng * 512
        lt = None
        for ch in range(NCH):
            lt = p.op("tensor", lambda E, ch=ch, c0=c0: E.matmul(
                pbL[0:8, 0:512], env["sgw"][:, ch * 8:(ch + 1) * 8],
                xn_sb[:, ch * T + c0: ch * T + c0 + 512],
                start=(ch == 0), stop=(ch == NCH - 1)),
                wait=([xn_tok(ch, ng)] + ([stoks[-1]] if stoks else [])
                      if ch == 0 else [xn_tok(ch, ng)]),
                inc=("pe" if ch == NCH - 1 else None))
        stoks.append(p.op("scalar", lambda E, c0=c0: E.activation(
            sfcT[0:8, c0:c0 + 512], pbL[0:8, 0:512], AF.Sigmoid),
            wait=[lt], inc="ac"))
    tb = p.op("vector", lambda E: E.tensor_scalar_add(
        sfcT[0:8, :], sfcT[0:8, :], env["sgb"][0:8, 0:1]),
        wait=stoks, inc="dv")
    if dbg == "gate1":
        return [tb]
    ttoks = []
    for tt in range(16):
        ttoks.append(p.op("tensor", lambda E, tt=tt: E.transpose(
            pbT[:, tt * 8:(tt + 1) * 8], sfcT[0:8, tt * 128:(tt + 1) * 128],
            env["identf"][0:8, 0:8]), wait=[tb], inc="pe"))
    sfc = sg[:, 0:128]
    tcp = p.op("vector", lambda E: E.tensor_copy(sfc, pbT[:, 0:128]),
               wait=[ttoks[-1], cst])
    if dbg == "gate2":
        return [tcp]
    G_off = 128
    G = sg[:, G_off:G_off + 64]
    even = bass.AP(sgate_t, 0, [[W, 128], [8, 16], [2, 4]])
    odd = bass.AP(sgate_t, 1, [[W, 128], [8, 16], [2, 4]])
    p.op("vector", lambda E: E.tensor_tensor(G, even, odd, ALU.add))

    def top2(src_off, width, inner, cidx, cidxP, so):
        ng_ = width // inner
        m1 = sg[:, so:so + ng_]
        srcap = sg[:, src_off:src_off + width]
        src3 = bass.AP(sgate_t, src_off, [[W, 128], [inner, ng_], [1, inner]])
        m13 = bass.AP(sgate_t, so, [[W, 128], [1, ng_], [0, inner]])
        cand = sg[:, so + ng_:so + ng_ + width]
        cand3 = bass.AP(sgate_t, so + ng_, [[W, 128], [inner, ng_], [1, inner]])
        i1 = sg[:, so + ng_ + width:so + 2 * ng_ + width]
        i13 = bass.AP(sgate_t, so + ng_ + width, [[W, 128], [1, ng_], [0, inner]])
        f1_off = so + 2 * ng_ + width
        f1 = sg[:, f1_off:f1_off + width]
        t2_off = f1_off + width
        t2 = sg[:, t2_off:t2_off + width]
        t23 = bass.AP(sgate_t, t2_off, [[W, 128], [inner, ng_], [1, inner]])
        mf_off = t2_off + width
        mf = sg[:, mf_off:mf_off + width]
        p.op("vector", lambda E: E.tensor_reduce(m1, src3, AX, ALU.max))
        p.op("vector", lambda E: E.tensor_tensor(mf, srcap, m13, ALU.is_equal))
        p.op("vector", lambda E: E.scalar_tensor_tensor(
            cand, mf, -90.0, cidxP, ALU.mult, ALU.add))
        p.op("vector", lambda E: E.tensor_reduce(i1, cand3, AX, ALU.min))
        p.op("vector", lambda E: E.tensor_tensor(f1, cidx, i13, ALU.is_equal))
        p.op("vector", lambda E: E.scalar_tensor_tensor(
            t2, f1, -1e30, srcap, ALU.mult, ALU.add))
        p.op("vector", lambda E: E.tensor_reduce(m1, t23, AX, ALU.max))
        p.op("vector", lambda E: E.tensor_tensor(mf, t2, m13, ALU.is_equal))
        p.op("vector", lambda E: E.scalar_tensor_tensor(
            cand, mf, -90.0, cidxP, ALU.mult, ALU.add))
        p.op("vector", lambda E: E.tensor_reduce(i1, cand3, AX, ALU.min))
        p.op("vector", lambda E: E.tensor_tensor(t2, cidx, i13, ALU.is_equal))
        p.op("vector", lambda E: E.tensor_add(f1, f1, t2))
        return f1_off

    mG_off = top2(G_off, 64, 4, env["cidx4"][:, :], env["cidx4P"][:, :], 192)
    tmp_off = 512
    tmp = sg[:, tmp_off:tmp_off + 128]
    mGexp = bass.AP(sgate_t, mG_off, [[W, 128], [4, 16], [1, 4], [0, 2]])
    p.op("vector", lambda E: E.tensor_tensor(tmp, sfc, mGexp, ALU.mult))
    mE_off = top2(tmp_off, 128, 8, env["cidx8"][:, :], env["cidx8P"][:, :],
                  640)
    mE = sg[:, mE_off:mE_off + 128]
    ms_off = 1200
    msfc = sg[:, ms_off:ms_off + 128]
    sw_off = ms_off + 128
    sumw = sg[:, sw_off:sw_off + 16]
    sum3 = bass.AP(sgate_t, sw_off, [[W, 128], [1, 16], [0, 8]])
    msfc3 = bass.AP(sgate_t, ms_off, [[W, 128], [8, 16], [1, 8]])
    wh_off = sw_off + 16
    wh = sg[:, wh_off:wh_off + 128]
    p.op("vector", lambda E: E.tensor_mul(msfc, mE, sfc))
    p.op("vector", lambda E: E.tensor_reduce(sumw, msfc3, AX, ALU.add))
    p.op("vector", lambda E: E.tensor_scalar_add(sumw, sumw, 1e-20))
    p.op("vector", lambda E: E.reciprocal(sumw, sumw))
    twh = p.op("vector", lambda E: E.scalar_tensor_tensor(
        wh, msfc, 2.5, sum3, ALU.mult, ALU.mult), inc="dv")
    if dbg == "gate3":
        return [twh]
    dts = []
    tc_hist = {}
    whc_off = wh_off + 128
    for e in range(NE):
        whe = bass.AP(sgate_t, wh_off + e, [[W, 128], [8, 16]])
        whc = sg[:, whc_off + (e % 2) * 16: whc_off + (e % 2) * 16 + 16]
        tg2 = p.op("vector", lambda E, whe=whe, whc=whc: E.tensor_copy(
            whc, whe), wait=[twh])
        sl = e % 4
        tt = p.op("tensor", lambda E, sl=sl, whc=whc: E.transpose(
            pbT[0:16, sl * 128:(sl + 1) * 128], whc, env["identf"][:, :]),
            wait=[tg2, tc_hist.get(sl)], inc="pe")
        tc = p.op("vector", lambda E, e=e, sl=sl: E.tensor_copy(
            swT[0:16, e * 128:(e + 1) * 128],
            pbT[0:16, sl * 128:(sl + 1) * 128]), wait=[tt], inc="dv")
        tc_hist[sl] = tc
        dt = p.dma("gpsimd", env["wdram"][e * T:(e + 1) * T],
                   swT[0:16, e * 128:(e + 1) * 128],
                   wait=[tc, dts[e - 4] if e >= 4 else None],
                   stream="wdr%d" % (e % 4))
        dts.append(dt)
    return dts


def emit_l0ffn(p, env, x_sb, xn_sb, xn_tok, ln_done, first_call):
    NG = T // 512
    pb = env["pb"]
    gate_toks = emit_l0_gating(p, env, xn_sb, xn_tok, pb[2], pb[1])
    pbG = [pb[0], pb[1]]
    pbU = [pb[2], pb[3]]
    pbB = pb[4]
    pbY = [pb[5], pb[6]]
    swg, swu, swd = env["swg"], env["swu"], env["swd"]
    ssig = env["ssig"]
    shu = env["shu"]
    swrow = env["swrow"]

    ew_free = {}
    def load_w(kind, e, first_use):
        slot = e % 2
        buf = {"g": swg, "u": swu, "d": swd}[kind]
        width = {"g": 8 * 512, "u": 8 * 512, "d": 4 * 8 * 128}[kind]
        dst = buf[:, slot * width:(slot + 1) * width]
        qd = env["qdram_" + kind]
        wait = ew_free.get((kind, slot), [])
        if first_use:
            hsrc = {"g": env["hwg"], "u": env["hwu"], "d": env["hwd"]}[kind]
            if kind in ("g", "u"):
                hap = hsrc.rearrange("e (c p) i -> e p c i", p=128)[e]
                dstr = dst.rearrange("p (c i) -> p c i", c=NCH)
            else:
                hap = hsrc.rearrange("e (c p) (h m) -> e p c h m",
                                     p=128, m=128)[e]
                dstr = dst.rearrange("p (c h m) -> p c h m", c=4, h=NCH)
            t1 = p.dma("gpsimd", dstr, hap, wait=wait,
                       stream="hw%s%d" % (kind, slot))
            t2 = p.dma("gpsimd", qd[e], dst, wait=[t1],
                       stream="qs%s%d" % (kind, slot))
            env.setdefault("qstore", {})[(kind, e)] = t2
            return t1, slot
        return p.dma("sync", dst, qd[e], wait=wait,
                     stream="w%s%d" % (kind, slot)), slot

    x_upd = {}
    hu_free = {}
    acc_free = {}
    for ng in range(NG):
        c0 = ng * 512
        xn_ng = [xn_tok(ch, ng) for ch in range(NCH)]
        first_use = first_call and ng == 0
        wt = {}
        acc_toks = {}
        for e in range(10):
            slot = e % 2
            if e == 0:
                wt[0] = (load_w("g", 0, first_use), load_w("u", 0, first_use),
                         load_w("d", 0, first_use))
            if e + 1 < 10:
                wt[e + 1] = (load_w("g", e + 1, first_use),
                             load_w("u", e + 1, first_use),
                             load_w("d", e + 1, first_use))
            routed = e < NE
            if routed:
                rw = p.dma("sync", swrow[0:1, slot * 512:(slot + 1) * 512],
                           env["wdram"][e * T + c0: e * T + c0 + 512],
                           wait=[gate_toks[e], ew_free.get(("row", slot))],
                           stream="row%d" % slot)
                tB = p.op("tensor", lambda E, slot=slot: E.matmul(
                    pbB[:, 0:512], env["ones_r"][0:1, 0:128],
                    swrow[0:1, slot * 512:(slot + 1) * 512], start=True,
                    stop=True),
                    wait=[rw, ew_free.get(("B",)), ln_done,
                          gate_toks[-1] if e == 0 else None],
                    inc="pe")
                ew_free[("row", slot)] = tB
            hu_toks = {}
            tg = tu = None
            for ich in range(4):
                gb = pbG[ich % 2]
                ub = pbU[ich % 2]
                sgslot = ich % 2
                sig = ssig[:, sgslot * 512:(sgslot + 1) * 512]
                for ch in range(NCH):
                    o = slot * 8 * 512 + ch * 512 + ich * 128
                    rhs = xn_sb[:, ch * T + c0: ch * T + c0 + 512]
                    w0 = ([wt[e][0][0], wt[e][1][0], ln_done,
                           ew_free.get(("gu", ich % 2))] + xn_ng
                          if ch == 0 else [])
                    tg = p.op("tensor", lambda E, o=o, r=rhs, ch=ch, gb=gb:
                              E.matmul(gb[:, 0:512], swg[:, o:o + 128], r,
                                       start=(ch == 0), stop=(ch == NCH - 1)),
                              wait=w0, inc=("pw" if ch == NCH - 1 else None))
                    tu = p.op("tensor", lambda E, o=o, r=rhs, ch=ch, ub=ub:
                              E.matmul(ub[:, 0:512], swu[:, o:o + 128], r,
                                       start=(ch == 0), stop=(ch == NCH - 1)),
                              inc=("pw" if ch == NCH - 1 else None))
                ts = p.op("scalar", lambda E, gb=gb, sig=sig: E.activation(
                    sig, gb[:, 0:512], AF.Silu),
                    wait=[tg, ew_free.get(("sig", sgslot))], exact=True)
                hu = shu[:, ich * 512:(ich + 1) * 512]
                if routed:
                    tm = p.op("vector", lambda E, ub=ub, sig=sig:
                              E.tensor_tensor(env["stmp"][:, 0:512],
                                              ub[:, 0:512], sig, ALU.mult),
                              wait=[ts, tu, hu_free.get(ich)])
                    th = p.op("vector", lambda E, hu=hu: E.tensor_tensor(
                        hu, env["stmp"][:, 0:512], pbB[:, 0:512], ALU.mult),
                        wait=[tB], exact=True)
                else:
                    th = p.op("vector", lambda E, hu=hu, ub=ub, sig=sig:
                              E.tensor_tensor(hu, ub[:, 0:512], sig, ALU.mult),
                              wait=[ts, tu, hu_free.get(ich)], exact=True)
                hu_toks[ich] = th
                ew_free[("gu", ich % 2)] = th
                ew_free[("sig", sgslot)] = th
                if routed:
                    ew_free[("B",)] = th
            dslot = wt[e][2][1]
            tlast = None
            for H in range(NCH):
                yb = pbY[H % 2]
                for ich in range(4):
                    o = dslot * 4 * 8 * 128 + (ich * 8 + H) * 128
                    hu = shu[:, ich * 512:(ich + 1) * 512]
                    w0 = ([wt[e][2][0], acc_free.get(H % 2)] +
                          list(hu_toks.values()) if ich == 0 else [])
                    tlast = p.op(
                        "tensor",
                        lambda E, o=o, hu=hu, f=(ich == 0), s=(ich == 3),
                        yb=yb: E.matmul(yb[:, 0:512], swd[:, o:o + 128], hu,
                                        start=f, stop=s),
                        wait=w0, inc=("pw" if ich == 3 else None))
                xs = x_sb[:, H * T + c0: H * T + c0 + 512]
                tv = p.op("vector", lambda E, yb=yb, xs=xs:
                          E.scalar_tensor_tensor(
                              xs, yb[:, 0:512], 0.5, xs, ALU.mult, ALU.add),
                          wait=[tlast], exact=True)
                if e == 9:
                    x_upd[(H, ng)] = tv
                acc_free[H % 2] = tv
            for ich in range(4):
                hu_free[ich] = tlast
            ew_free[("g", slot)] = [tg, env.get("qstore", {}).get(("g", e))]
            ew_free[("u", slot)] = [tu, env.get("qstore", {}).get(("u", e))]
            ew_free[("d", dslot)] = [tlast,
                                     env.get("qstore", {}).get(("d", e))]
    return x_upd


def emit_l0attn(p, env, x_sb, xn_sb, xn_tok, xnp_dram, xnp_toks, ln_done,
                npairs=8, nphase=4):
    pb = env["pb"]
    pbS, pbA, pbO = pb[0], [pb[1], pb[2]], [pb[3], pb[4]]
    pbD, pbB, pbP = pb[5], pb[6], [pb[6], pb[7]]
    sQ, sKT = env["sQ"], env["sKT"]
    sKtok, sV = env["sKtok"], env["sV"]
    sxp = env["sxp"]
    sS = env["sS"]
    sAm = env["sAm"]
    srcp = env["srcp"]
    sOutE = env["sOutE"]
    sAtt = env["sAtt"]
    escr = env["escr"]
    sw = env["swpair"]
    WSLOT = 5 * 8 * 128
    NT = T // 128
    prev = {}
    x_att = {}

    def elu1_evac(psum_ap, dst_ap, par, wait, width, rows=128):
        s1 = escr[0:rows, par * 512:par * 512 + width]
        t1 = p.op("vector", lambda E: E.tensor_scalar(
            s1, psum_ap, 0.0, None, ALU.min), wait=wait)
        t2 = p.op("scalar", lambda E: E.activation(s1, s1, AF.Exp), wait=[t1])
        t3 = p.op("vector", lambda E: E.scalar_tensor_tensor(
            dst_ap, psum_ap, 0.0, s1, ALU.max, ALU.add), wait=[t2], inc="dv")
        return t3

    for pr in range(npairs):
        par = pr % 2
        wofs = par * WSLOT
        wtok = {}
        for wi, hname in enumerate(("hwqT", "hwkT", "hwvT")):
            dst = sw[:, wofs + wi * 1024: wofs + wi * 1024 + 1024]
            wtok[wi] = p.dma(
                "gpsimd", dst.rearrange("p (c m) -> p c m", c=NCH),
                env[hname].rearrange("(c p) o -> p c o", p=128)[
                    :, :, 128 * pr:128 * pr + 128],
                wait=[prev.get(("wfree", par))],
                stream="wat%s%d" % (hname[2], par))
        wo_dst = sw[:, wofs + 3 * 1024: wofs + 3 * 1024 + 1024]
        wtok[3] = p.dma(
            "gpsimd",
            bass.AP(tap(sw), wofs + 3 * 1024,
                    [[env["sw_w"], 64], [1024, 2], [1, 1024]]),
            env["hwoT"].rearrange("(r hh p) o -> p hh r o", p=64, hh=2)[
                :, :, pr], wait=[prev.get(("wfree", par))],
            stream="wao%d" % par)
        wq = lambda ch: sw[:, wofs + ch * 128: wofs + ch * 128 + 128]
        wk = lambda ch: sw[:, wofs + 1024 + ch * 128: wofs + 1024 + ch * 128 + 128]
        wv = lambda ch: sw[:, wofs + 2048 + ch * 128: wofs + 2048 + ch * 128 + 128]
        wo = lambda hh, H: sw[0:64, wofs + 3 * 1024 + hh * 1024 + H * 128:
                              wofs + 3 * 1024 + hh * 1024 + H * 128 + 128]

        sinit_done = None
        for tt in range(NT):
            xpar = tt % 2
            xp = sxp[:, xpar * 1024:(xpar + 1) * 1024]
            tld = p.dma("gpsimd", xp.rearrange("p (c m) -> p c m", c=NCH),
                        xnp_dram.rearrange("(c p) t -> p c t", p=128)[
                            :, :, tt * 128:(tt + 1) * 128],
                        wait=list(xnp_toks) + [prev.get(("xp", xpar))],
                        stream="xp%d" % xpar)
            tk = tv = None
            for ch in range(NCH):
                w0 = [tld, wtok[1], wtok[2], ln_done,
                      prev.get(("pp",))] if ch == 0 else [tld]
                tk = p.op("tensor", lambda E, ch=ch, xp=xp, w=wk(ch): E.matmul(
                    pbP[0][:, 0:128], xp[:, ch * 128:(ch + 1) * 128],
                    w, start=(ch == 0), stop=(ch == NCH - 1)),
                    wait=w0, inc=("pe" if ch == NCH - 1 else None))
                tv = p.op("tensor", lambda E, ch=ch, xp=xp, w=wv(ch): E.matmul(
                    pbP[1][:, 0:128], xp[:, ch * 128:(ch + 1) * 128],
                    w, start=(ch == 0), stop=(ch == NCH - 1)),
                    inc=("pe" if ch == NCH - 1 else None))
            prev[("xp", xpar)] = tv
            ek = elu1_evac(pbP[0][:, 0:128],
                           sKtok[:, tt * 128:(tt + 1) * 128], 0,
                           [tk, prev.get(("ktok", tt))], 128)
            vdst = bass.AP(tap(sV), tt * 130,
                           [[env["sV_w"], 128], [65, 2], [1, 64]])
            ev = p.op("vector", lambda E, vdst=vdst: E.tensor_copy(
                vdst, pbP[1][:, 0:128]),
                wait=[tv, prev.get(("vtok", tt))], inc="dv")
            prev[("pp",)] = ev
            kap = sKtok[:, tt * 128:(tt + 1) * 128]
            vap = sV[:, tt * 130:(tt + 1) * 130]
            sinit_done = p.op(
                "tensor", lambda E, kap=kap, vap=vap, tt=tt:
                E.matmul(pbS[:, 0:130], kap, vap, start=(tt == 0),
                         stop=(tt == NT - 1)),
                wait=([ek, ev, prev.get(("st",)), env.get("v1_tok")]
                      if tt == 0 else [ek, ev]),
                inc=("pe" if tt == NT - 1 else None))
        hm = p.op("vector", lambda E: E.tensor_scalar(
            env["sSf"][:, 0:130], pbS[:, 0:130], env["hmask"][:, 0:1], None,
            ALU.mult), wait=[sinit_done, prev.get(("sSread",))], inc="dv")
        hm = p.op("vector", lambda E: E.tensor_copy(
            sS[:, 0:130], env["sSf"][:, 0:130]), inc="dv")
        if nphase <= 1:
            x_att[(0, 0, pr)] = hm
            return {k: v for k, v in x_att.items() if k[2] == npairs - 1}

        qk_done = []
        for ng in range(4):
            c0 = ng * 512
            for hh in range(2):
                for dst, wfn, wt, pp in ((sQ, wq, wtok[0], 0),
                                         (sKT, wk, wtok[1], 1)):
                    tq = None
                    for ch in range(NCH):
                        w0 = ([wt, ln_done, hm, prev.get(("proj", pp))]
                              if ch == 0 else [])
                        w_ = wfn(ch)[:, hh * 64:(hh + 1) * 64]
                        tq = p.op("tensor", lambda E, ch=ch, c0=c0, pp=pp,
                                  w=w_: E.matmul(
                                      pbP[pp][0:64, 0:512], w,
                                      xn_sb[:, ch * T + c0:ch * T + c0 + 512],
                                      start=(ch == 0), stop=(ch == NCH - 1)),
                                  wait=w0,
                                  inc=("pe" if ch == NCH - 1 else None))
                    te = elu1_evac(
                        pbP[pp][0:64, 0:512],
                        dst[0:64, hh * T + c0: hh * T + c0 + 512], pp,
                        [tq], 512, rows=64)
                    prev[("proj", pp)] = te
                    qk_done.append(te)
        kv_done = []
        ek = ev = None
        for tt in range(NT):
            tk = tv = None
            for ch in range(NCH):
                xo = xn_sb[:, ch * T + tt * 128: ch * T + tt * 128 + 128]
                w0 = ([prev.get(("proj", 0)), prev.get(("proj", 1)), ek, ev]
                      if ch == 0 else [])
                tk = p.op("tensor", lambda E, ch=ch, xo=xo, w=wk(ch):
                          E.matmul(pbP[0][:, 0:128], xo, w, start=(ch == 0),
                                   stop=(ch == NCH - 1)), wait=w0,
                          inc=("pe" if ch == NCH - 1 else None))
                tv = p.op("tensor", lambda E, ch=ch, xo=xo, w=wv(ch):
                          E.matmul(pbP[1][:, 0:128], xo, w, start=(ch == 0),
                                   stop=(ch == NCH - 1)),
                          inc=("pe" if ch == NCH - 1 else None))
            ek = elu1_evac(pbP[0][:, 0:128],
                           sKtok[:, tt * 128:(tt + 1) * 128], 0, [tk], 128)
            vdst = bass.AP(tap(sV), tt * 130,
                           [[env["sV_w"], 128], [65, 2], [1, 64]])
            ev = p.op("vector", lambda E, vdst=vdst: E.tensor_copy(
                vdst, pbP[1][:, 0:128]), wait=[tv], inc="dv")
            prev[("ktok", tt)] = ek
            prev[("vtok", tt)] = ev
            kv_done.append(ek)
            kv_done.append(ev)

        if nphase <= 2:
            x_att[(0, 0, pr)] = kv_done[-1]
            return {k: v for k, v in x_att.items() if k[2] == npairs - 1}
        att_done = []
        st_cast = hm
        for c in range(NT):
            cpar = c % 2
            st_copy = st_cast
            ta = None
            for hh in range(2):
                ta = p.op("tensor", lambda E, hh=hh, c=c: E.matmul(
                    pbA[cpar][:, hh * 128:(hh + 1) * 128],
                    sKT[:, hh * T + c * 128: hh * T + (c + 1) * 128],
                    sQ[:, hh * T + c * 128: hh * T + (c + 1) * 128],
                    start=True, stop=True),
                    wait=([kv_done[-1], kv_done[-2], qk_done[-1],
                           prev.get(("Aread", cpar))] if hh == 0 else []),
                    inc=("pe" if hh == 1 else None))
            tm = p.op("vector", lambda E: E.tensor_tensor(
                sAm[:, 0:256], pbA[cpar][:, 0:256],
                bass.AP(tap(env["sUT"]), 0, [[128, 128], [0, 2], [1, 128]]),
                ALU.mult), wait=[ta], inc="dv")
            prev[("Aread", cpar)] = tm
            for hh in range(2):
                p.op("tensor", lambda E, hh=hh, c=c: E.matmul(
                    pbO[cpar][0:64, hh * 128:(hh + 1) * 128],
                    sS[:, hh * 65: hh * 65 + 64],
                    sQ[:, hh * T + c * 128: hh * T + (c + 1) * 128],
                    start=True, stop=False),
                    wait=([st_copy, tm, prev.get(("Oread", cpar))]
                          if hh == 0 else []),
                    inc="pe")
                p.op("tensor", lambda E, hh=hh, c=c: E.matmul(
                    pbO[cpar][0:64, hh * 128:(hh + 1) * 128],
                    sV[:, c * 130 + hh * 65: c * 130 + hh * 65 + 64],
                    sAm[:, hh * 128:(hh + 1) * 128],
                    start=False, stop=True), inc="pe")
            to = None
            for hh in range(2):
                p.op("tensor", lambda E, hh=hh, c=c: E.matmul(
                    pbD[0:1, hh * 128:(hh + 1) * 128],
                    sS[:, hh * 65 + 64: hh * 65 + 65],
                    sQ[:, hh * T + c * 128: hh * T + (c + 1) * 128],
                    start=True, stop=False),
                    wait=([prev.get(("Dread",))] if hh == 0 else []),
                    inc="pe")
                to = p.op("tensor", lambda E, hh=hh, c=c: E.matmul(
                    pbD[0:1, hh * 128:(hh + 1) * 128],
                    sV[:, c * 130 + hh * 65 + 64: c * 130 + hh * 65 + 65],
                    sAm[:, hh * 128:(hh + 1) * 128],
                    start=False, stop=True), inc="pe")
            stm = p.op("tensor", lambda E, c=c: E.matmul(
                pbS[:, 0:130], sKtok[:, c * 128:(c + 1) * 128],
                sV[:, c * 130:(c + 1) * 130], start=True, stop=True),
                wait=[prev.get(("sadd",)), hm], inc="pe")
            tadd = p.op("vector", lambda E: E.tensor_add(
                env["sSf"][:, 0:130], pbS[:, 0:130], env["sSf"][:, 0:130]),
                wait=[stm], inc="dv")
            prev[("sadd",)] = tadd
            st_cast = p.op("vector", lambda E: E.tensor_copy(
                sS[:, 0:130], env["sSf"][:, 0:130]), wait=[to], inc="dv")
            prev[("stmm", c)] = stm
            trm = p.op("vector", lambda E: E.tensor_scalar_max(
                srcp[0:1, 0:256], pbD[0:1, 0:256], 1e-6),
                wait=[to], inc="dv")
            prev[("Dread",)] = trm
            tr = p.op("vector", lambda E: E.reciprocal(
                srcp[0:1, 0:256], srcp[0:1, 0:256]), inc="dv")
            tbb = None
            for hh in range(2):
                tbb = p.op("tensor", lambda E, hh=hh: E.matmul(
                    pbB[0:64, hh * 128:(hh + 1) * 128],
                    env["ones_r"][0:1, 0:64],
                    srcp[0:1, hh * 128:(hh + 1) * 128],
                    start=True, stop=True),
                    wait=([tr, prev.get(("Bread",))] if hh == 0 else []),
                    inc=("pe" if hh == 1 else None))
            tev = p.op("scalar", lambda E: E.activation(
                sOutE[0:64, 0:256], pbO[cpar][0:64, 0:256], AF.Identity),
                wait=[to], inc="ac")
            prev[("Oread", cpar)] = tev
            tdv = None
            for hh in range(2):
                tdv = p.op("vector", lambda E, hh=hh, c=c: E.tensor_tensor(
                    sAtt[0:64, hh * T + c * 128: hh * T + (c + 1) * 128],
                    sOutE[0:64, hh * 128:(hh + 1) * 128],
                    pbB[0:64, hh * 128:(hh + 1) * 128], ALU.mult),
                    wait=[tbb, tev, prev.get(("attw",))], inc="dv")
            prev[("Bread",)] = tdv
            att_done.append(tdv)
        prev[("sSread",)] = att_done[-1]
        prev[("sadd",)] = prev.get(("sadd",))

        if nphase <= 3:
            x_att[(0, 0, pr)] = att_done[-1]
            return {k: v for k, v in x_att.items() if k[2] == npairs - 1}
        for ng in range(4):
            c0 = ng * 512
            for H in range(NCH):
                ypp = pbP[H % 2]
                tw_ = None
                for hh in range(2):
                    tw_ = p.op("tensor", lambda E, hh=hh, H=H, c0=c0,
                               ypp=ypp, w=wo(hh, H): E.matmul(
                                   ypp[:, 0:512], w,
                                   sAtt[0:64, hh * T + c0: hh * T + c0 + 512],
                                   start=(hh == 0), stop=(hh == 1)),
                               wait=([att_done[-1], wtok[3],
                                      prev.get(("yp", H % 2))]
                                     if hh == 0 else []),
                               inc=("pe" if hh == 1 else None))
                xs = x_sb[:, H * T + c0: H * T + c0 + 512]
                tx_ = p.op("vector", lambda E, xs=xs, ypp=ypp: E.tensor_add(
                    xs, ypp[:, 0:512], xs), wait=[tw_], inc="dv")
                prev[("yp", H % 2)] = tx_
                x_att[(H, ng, pr)] = tx_
        prev[("attw",)] = tx_
        prev[("wfree", par)] = tx_
        prev[("st",)] = tx_
        prev[("pp",)] = tx_
    return {k: v for k, v in x_att.items() if k[2] == npairs - 1}


MAGIC = 12582912.0

def emit_quant(p, env, src, dst, W, wait=(), src_f32=True, dst_wait=(),
               eng="vector", scr="", exact=False):
    nb = W // 16
    z, qa = env["qz" + scr][:, 0:W], env["qa" + scr][:, 0:W]
    qs = env["qs" + scr][:, 0:W]
    bm, sc = env["qbm" + scr][:, 0:nb], env["qsc" + scr][:, 0:nb]
    st = tap(src)
    src3 = bass.AP(st, src.offset, list(src.ap[:-1]) + [[16, nb], [1, 16]])
    bmt = tap(env["qbm" + scr])
    bm3 = bass.AP(bmt, 0, [[env["qbm_w" + scr], 128], [1, nb], [0, 16]])
    sct = tap(env["qsc" + scr])
    sc3 = bass.AP(sct, 0, [[env["qsc_w" + scr], 128], [1, nb], [0, 16]])
    qk = ("qprev", eng, scr)
    tr_ = p.op("vector", lambda E: E.tensor_reduce(
        bm, src3, AX, ALU.max, apply_absolute_value=True),
        wait=list(wait) + [env.get(("qbmw", eng, scr))], inc="dv")
    tsc = p.op(eng, lambda E: E.tensor_scalar(
        sc, bm, 1.0 / 6.0, 1e-12, ALU.mult, ALU.max), wait=[tr_])
    trc = p.op("vector", lambda E: E.reciprocal(bm, sc), wait=[tsc],
               inc="dv")
    t0 = p.op("scalar", lambda E: E.activation(qs, src, AF.Sign),
              wait=list(wait) + [env.get(qk)], inc="ac")
    t1 = p.op("scalar", lambda E: E.activation(z, src, AF.Abs),
              inc="ac")
    tzm = p.op(eng, lambda E: E.tensor_tensor(z, z, bm3, ALU.mult),
               wait=[t1, trc])
    env[("qbmw", eng, scr)] = tzm if eng != "vector" else None
    p.op(eng, lambda E: E.tensor_scalar(
        qa, z, 2.0, MAGIC, ALU.mult, ALU.add))
    p.op(eng, lambda E: E.tensor_scalar(
        qa, qa, MAGIC + 4.0, None, ALU.min))
    p.op(eng, lambda E: E.tensor_scalar(
        qa, qa, -MAGIC, 0.5, ALU.add, ALU.mult))
    p.op(eng, lambda E: E.tensor_tensor(qs, qs, sc3, ALU.mult),
         wait=[t0])
    qb = env["qb" + scr][:, 0:W]
    p.op(eng, lambda E: E.tensor_scalar(
        qb, z, MAGIC, MAGIC + 4.0, ALU.add, ALU.min))
    p.op(eng, lambda E: E.tensor_scalar(
        qb, qb, MAGIC + 2.0, -(MAGIC + 2.0), ALU.max, ALU.add))
    p.op(eng, lambda E: E.tensor_add(qa, qa, qb))
    p.op(eng, lambda E: E.tensor_scalar(
        qb, z, 5.0, 2.0, ALU.is_gt, ALU.mult))
    p.op(eng, lambda E: E.tensor_add(qa, qa, qb))
    tf = p.op(eng, lambda E: E.tensor_tensor(dst, qa, qs, ALU.mult),
              wait=list(dst_wait), exact=exact)
    env[qk] = tf
    return tf


QU_G0, QU_U0, QU_D0 = 0, 176, 352
QU_PER_CORE = 66


def _q_unit(kind, it, sub):
    if kind == "g":
        return QU_G0 + it * 8 + sub
    if kind == "u":
        return QU_U0 + it * 8 + sub
    return QU_D0 + it * 22 + sub


def emit_wquant_dist(p, env):
    qwbuf, qwout, souT = env["qwbuf"], env["qwout"], env["souT"]
    pbq = env["pbq"]
    wqsrc = env["wqsrc"]
    qown = env["qown"]
    strips = [(k * 4, 512) for k in range(16)] + [(64, 256)]
    tf = {}
    trp = {}
    cp = {}
    st = {}
    for k, (u0, W) in enumerate(strips):
        tl = p.dma("sync", qwbuf[:, 0:W], wqsrc[:, u0 * 128: u0 * 128 + W],
                   wait=[tf.get(k - 1)], stream="qwl0")
        tf[k] = emit_quant(p, env, qwbuf[:, 0:W], qwout[:, 0:W], W,
                           wait=[tl], dst_wait=[trp.get(k - 1)],
                           eng="vector")
        t = None
        for j in range(W // 128):
            t = p.op("tensor", lambda E, j=j: E.transpose(
                pbq[:, j * 128: j * 128 + 128],
                qwout[:, j * 128: j * 128 + 128],
                env["identb"][:, :]),
                wait=([tf[k], cp.get(k - 1), env["cst"]] if j == 0 else []),
                inc=("pe" if j == W // 128 - 1 else None))
        trp[k] = t
        cp[k] = p.op("vector", lambda E, W=W: E.tensor_copy(
            souT[:, 0:W], pbq[:, 0:W]),
            wait=[t, st.get(k - 1)], inc="dv")
        dst = bass.AP(tap(qown), u0 * 16384,
                      [[128, 128], [16384, W // 128], [1, 128]])
        src = souT[:, 0:W].rearrange("p (n m) -> p n m", m=128)
        st[k] = p.dma("sync", dst, src, wait=[cp[k]], stream="qws0")
    return list(st.values())


def emit_l1ffn(p, env, x_sb, xn_sb, xn_tok, ln_done, first_call, wq_toks):
    pb = env["pb"]
    pbt = env["pbt"]
    pbG = [pb[0], pb[1]]
    pbU = [pb[2], pb[3]]
    pbY = [pb[4], pb[5]]
    sxq, shq = env["sxq"], env["shq"]
    sxt = env["sxt"]
    ssil = env["ssil"]
    shts = env["shts"]
    shtq = env["shtq"]
    swg1, swu1, swd1 = env["swg1"], env["swu1"], env["swd1"]
    igroups = [(i * 512, 512) for i in range(5)] + [(2560, 256)]
    prev = {}
    x_upd = {}
    last_gu = {}
    sfx = "A" if first_call else "B"
    qdone = {}

    def xfwd(ng):
        c0 = ng * 512
        xn_ng = [xn_tok(ch, ng) for ch in range(NCH)]
        for tt in range(4):
            ttr = None
            for ch in range(NCH):
                ttr = p.op("tensor", lambda E, ch=ch, tt=tt, c0=c0: E.transpose(
                    pbt[:, ch * 128:(ch + 1) * 128],
                    xn_sb[:, ch * T + c0 + tt * 128: ch * T + c0 + tt * 128 + 128],
                    env["identb"][:, :]),
                    wait=(xn_ng + [ln_done, prev.get(("pbt",))]
                          if ch == 0 else []),
                    inc=("pe" if ch == NCH - 1 else None))
            sl = sxt[:, tt * 1024:(tt + 1) * 1024]
            tc = p.op("vector", lambda E, sl=sl: E.tensor_copy(
                sl, pbt[:, 0:1024]),
                wait=[ttr, prev.get(("sxt", tt))], exact=True)
            prev[("pbt",)] = tc
            q1 = emit_quant(p, env, sl[:, 0:512], sl[:, 0:512], 512,
                            wait=[tc], src_f32=False, exact=True)
            q2 = emit_quant(p, env, sl[:, 512:1024], sl[:, 512:1024], 512,
                            wait=[tc], src_f32=False, exact=True)
            qdone[tt] = (q1, q2)

    def xback(ng):
        for tt in range(4):
            sl = sxt[:, tt * 1024:(tt + 1) * 1024]
            ttr = None
            for ch in range(NCH):
                ttr = p.op("tensor", lambda E, sl=sl, ch=ch: E.transpose(
                    pbt[:, ch * 128:(ch + 1) * 128],
                    sl[:, ch * 128:(ch + 1) * 128], env["identb"][:, :]),
                    wait=(list(qdone[tt]) + [prev.get(("pbt",))]
                          if ch == 0 else []),
                    inc=("pe" if ch == NCH - 1 else None))
            prev[("sxt", tt)] = ttr
            sxq_dst = bass.AP(tap(sxq), tt * 128,
                              [[8 * 512, 128], [512, 8], [1, 128]])
            tc = p.op("vector", lambda E, d=sxq_dst: E.tensor_copy(
                d, pbt[:, 0:1024]),
                wait=[ttr, prev.get(("sxq", tt))], exact=True)
            prev[("pbt",)] = tc
            prev[("xq", tt)] = tc

    xfwd(0)
    xback(0)
    for ng in range(4):
        c0 = ng * 512
        for gi, (i0, w) in enumerate(igroups):
            wslot = gi % 2
            wg_tok = {}
            for kind, buf in (("g", swg1), ("u", swu1)):
                for ch in range(NCH):
                    t0 = i0 // 128
                    nt = w // 128
                    col0 = wslot * 8 * 512 + ch * 512
                    src = bass.AP(tap(env["qall"]),
                                  _q_unit(kind, t0, ch) * 16384,
                                  [[128, 128], [8 * 16384, nt], [1, 128]])
                    wg_tok[(kind, ch)] = p.dma(
                        "sync",
                        buf[:, col0:col0 + w].rearrange(
                            "p (n m) -> p n m", m=128),
                        src,
                        wait=[prev.get(("gw", kind, wslot, ch)),
                              env.get("qall_tok")],
                        stream="l1w%s%d%d" % (kind, wslot, ch))
            for tt in range(4):
                gb = pbG[tt % 2]
                ub = pbU[tt % 2]
                tg = tu = None
                for ch in range(NCH):
                    lh = bass.AP(tap(sxq), ch * 512 + tt * 128,
                                 [[8 * 512, 128], [1, 128]])
                    w0 = ([prev.get(("xq", tt)), wg_tok[("g", ch)],
                           wg_tok[("u", ch)], prev.get(("gub", tt % 2))]
                          if ch == 0
                          else ([wg_tok[("g", ch)], wg_tok[("u", ch)]]
                                if tt == 0 else []))
                    tg = p.op("tensor", lambda E, lh=lh, ch=ch, w=w, gb=gb,
                              o=wslot * 8 * 512 + ch * 512: E.matmul(
                                  gb[:, 0:w], lh, swg1[:, o:o + w],
                                  start=(ch == 0), stop=(ch == NCH - 1)),
                              wait=w0, inc=("pw" if ch == NCH - 1 else None))
                    tu = p.op("tensor", lambda E, lh=lh, ch=ch, w=w, ub=ub,
                              o=wslot * 8 * 512 + ch * 512: E.matmul(
                                  ub[:, 0:w], lh, swu1[:, o:o + w],
                                  start=(ch == 0), stop=(ch == NCH - 1)),
                              inc=("pw" if ch == NCH - 1 else None))
                if gi == 5:
                    last_gu[tt] = tu
                ts_ = p.op("scalar", lambda E, gb=gb, w=w: E.activation(
                    ssil[:, 0:w], gb[:, 0:w], AF.Silu), wait=[tg],
                    exact=True)
                tm = p.op("vector", lambda E, ub=ub, w=w: E.tensor_tensor(
                    shts[:, 0:w], ub[:, 0:w], ssil[:, 0:w], ALU.mult),
                    wait=[ts_, tu], exact=True)
                prev[("gub", tt % 2)] = tm
                for ch in range(NCH):
                    prev[("gw", "g", wslot, ch)] = tg
                    prev[("gw", "u", wslot, ch)] = tu
                qt = emit_quant(p, env, shts[:, 0:w], shtq[:, 0:w], w,
                                wait=[tm], exact=True)
                ttr = None
                for j in range(w // 128):
                    ttr = p.op("tensor", lambda E, j=j: E.transpose(
                        pbt[:, j * 128:(j + 1) * 128],
                        shtq[:, j * 128:(j + 1) * 128], env["identb"][:, :]),
                        wait=([qt, prev.get(("pbt",))] if j == 0 else []),
                        inc=("pe" if j == w // 128 - 1 else None))
                hq_dst = bass.AP(tap(shq), (i0 // 128) * 512 + tt * 128,
                                 [[22 * 512, 128], [512, w // 128], [1, 128]])
                tc = p.op("vector", lambda E, d=hq_dst, w=w: E.tensor_copy(
                    d, pbt[:, 0:w]),
                    wait=[ttr, prev.get(("hq", gi, tt))], exact=True)
                prev[("pbt",)] = tc
                prev[("hq", gi, tt)] = tc
        cur_hq = {gi: [prev[("hq", gi, tt)] for tt in range(4)]
                  for gi in range(6)}
        if ng + 1 < 4:
            xfwd(ng + 1)
        for H in range(NCH):
            yb = pbY[H % 2]
            wslot = H % 2
            tlast = None
            for ich in range(22):
                wt = p.dma("sync",
                           swd1[:, wslot * 1024 + (ich % 8) * 128:
                                wslot * 1024 + (ich % 8) * 128 + 128],
                           bass.AP(tap(env["qall"]),
                                   _q_unit("d", H, ich) * 16384,
                                   [[128, 128], [1, 128]]),
                           wait=[prev.get(("dw", wslot, ich % 8)),
                                 env.get("qall_tok")],
                           stream="l1wd%d%d" % (wslot, ich % 2))
                w0 = [wt, prev.get(("yb", H % 2))] if ich == 0 else [wt]
                if ich % 4 == 0:
                    w0 += cur_hq[ich // 4]
                tlast = p.op("tensor", lambda E, ich=ich, H=H, yb=yb,
                             o=wslot * 1024 + (ich % 8) * 128: E.matmul(
                                 yb[:, 0:512], swd1[:, o:o + 128],
                                 shq[:, ich * 512:(ich + 1) * 512],
                                 start=(ich == 0), stop=(ich == 21)),
                             wait=w0, inc="pw")
                prev[("dw", wslot, ich % 8)] = tlast
            xs = x_sb[:, H * T + c0: H * T + c0 + 512]
            tv = p.op("vector", lambda E, xs=xs, yb=yb:
                      E.scalar_tensor_tensor(xs, yb[:, 0:512], 0.5, xs,
                                             ALU.mult, ALU.add),
                      wait=[tlast], exact=True)
            prev[("yb", H % 2)] = tv
            x_upd[(H, ng)] = tv
        for gi in range(6):
            for tt in range(4):
                prev[("hq", gi, tt)] = tlast
        for tt in range(4):
            prev[("sxq", tt)] = last_gu[tt]
        if ng + 1 < 4:
            xback(ng + 1)
    return x_upd


KT = 4096


def emit_l1attn_projA(p, env, dbg=None):
    pb = env["pb"]
    sxn, sxg = env["sxn"], env["sxg"]
    swq, swk, swv = env["swq"], env["swk"], env["swv"]
    qsb, ksb, sv = env["qsb"], env["ksb"], env["sv"]
    sq2, rnorm = env["sq2"], env["rnorm"]
    negrow, mrow = env["negrow"], env["mrow"]
    ln_last = env["lnlast"]
    prev = {}
    t_v1 = p.op("gpsimd", lambda E: E.memset(
        bass.AP(tap(sv), 64, [[2080, 128], [65, 32]]), 1.0), inc="gp")
    tub = p.dma("gpsimd", env["subf"][0:16, 0:6144], env["ub_in"],
                stream="ub")
    tue = p.op("scalar", lambda E: E.activation(
        env["subb"][0:16, 0:6144], env["subf"][0:16, 0:6144], AF.Exp),
        wait=[tub], inc="ac")
    tus = p.dma("gpsimd", env["ubexp"], env["subb"][0:16, 0:6144],
                wait=[tue], stream="ub")
    tbs = p.op("vector", lambda E: E.tensor_scalar(
        env["sbqs"][0:64, 0:16], env["sbq"][0:64, 0:16], 0.125, None,
        ALU.mult), wait=[env["bq_tok"]], inc="dv")
    sstore = {}
    for h in range(16):
        sl = h % 2
        tq = p.dma("gpsimd",
                   swq[:, sl * 512:(sl + 1) * 512].rearrange(
                       "p (c m) -> p c m", c=8),
                   env["hwqT"][:, h * 64:(h + 1) * 64].rearrange(
                       "(c p) m -> p c m", p=128),
                   wait=[prev.get(("wq", sl)), env["cst"]],
                   stream="wq%d" % sl)
        tk = p.dma("gpsimd",
                   swk[:, sl * 512:(sl + 1) * 512].rearrange(
                       "p (c m) -> p c m", c=8),
                   env["hwkT"][:, h * 64:(h + 1) * 64].rearrange(
                       "(c p) m -> p c m", p=128),
                   wait=[prev.get(("wk", sl))], stream="wk%d" % sl)
        tv = p.dma("gpsimd",
                   swv[:, sl * 512:(sl + 1) * 512].rearrange(
                       "p (c m) -> p c m", c=8),
                   env["hwvT"][:, h * 64:(h + 1) * 64].rearrange(
                       "(c p) m -> p c m", p=128),
                   wait=[prev.get(("wv", sl))], stream="wv%d" % sl)
        nrcol = 0
        tnr = None
        for qg in range(4):
            bank = pb[qg % 2]
            tm = None
            for ch in range(NCH):
                tm = p.op("tensor", lambda E, ch=ch, qg=qg, sl=sl, bank=bank:
                          E.matmul(bank[0:64, 0:512],
                                   swq[:, sl * 512 + ch * 64:
                                       sl * 512 + (ch + 1) * 64],
                                   sxn[:, ch * T + qg * 512:
                                       ch * T + qg * 512 + 512],
                                   start=(ch == 0), stop=(ch == NCH - 1)),
                          wait=([tq, prev.get(("pj", qg % 2)), ln_last, tbs]
                                if ch == 0 else []),
                          inc=("pe" if ch == NCH - 1 else None))
            te = p.op("scalar", lambda E, qg=qg, h=h, bank=bank: E.activation(
                qsb[0:64, qg * 512:(qg + 1) * 512], bank[0:64, 0:512],
                AF.Identity, bias=env["sbqs"][0:64, h:h + 1], scale=0.125),
                wait=[tm, sstore.get(("q", h - 1))], inc="ac")
            prev[("pj", qg % 2)] = te
            tsq = p.op("scalar", lambda E, qg=qg: E.activation(
                sq2[0:64, 0:512], qsb[0:64, qg * 512:(qg + 1) * 512],
                AF.Square), wait=[prev.get(("sqw",))], inc="ac")
            tnm = p.op("tensor", lambda E: E.matmul(
                pb[6][0:1, 0:512], env["onesc"][0:64, 0:1],
                sq2[0:64, 0:512], start=True, stop=True),
                wait=[tsq, prev.get(("nr",))], inc="pe")
            prev[("sqw",)] = tnm
            tnr = p.op("vector", lambda E, col=nrcol: E.tensor_reduce(
                rnorm[0:1, col:col + 1], pb[6][0:1, 0:512], AX, ALU.max),
                wait=[tnm], inc="dv")
            prev[("nr",)] = tnr
            nrcol += 1
        prev[("wq", sl)] = tm
        for kg in range(8):
            bank = pb[kg % 2]
            tm = None
            for ch in range(NCH):
                tm = p.op("tensor", lambda E, ch=ch, kg=kg, sl=sl, bank=bank:
                          E.matmul(bank[0:64, 0:512],
                                   swk[:, sl * 512 + ch * 64:
                                       sl * 512 + (ch + 1) * 64],
                                   sxg[:, ch * KT + kg * 512:
                                       ch * KT + kg * 512 + 512],
                                   start=(ch == 0), stop=(ch == NCH - 1)),
                          wait=([tk, prev.get(("pj", kg % 2))]
                                if ch == 0 else []),
                          inc=("pe" if ch == NCH - 1 else None))
            te = p.op("scalar", lambda E, kg=kg, h=h, bank=bank: E.activation(
                ksb[0:64, kg * 512:(kg + 1) * 512], bank[0:64, 0:512],
                AF.Identity, bias=env["sbk"][0:64, h:h + 1]),
                wait=[tm, env["bk_tok"], sstore.get(("k", h - 1))], inc="ac")
            prev[("pj", kg % 2)] = te
            tsq = p.op("scalar", lambda E, kg=kg: E.activation(
                sq2[0:64, 0:512], ksb[0:64, kg * 512:(kg + 1) * 512],
                AF.Square), wait=[prev.get(("sqw",))], inc="ac")
            tnm = p.op("tensor", lambda E: E.matmul(
                pb[6][0:1, 0:512], env["onesc"][0:64, 0:1],
                sq2[0:64, 0:512], start=True, stop=True),
                wait=[tsq, prev.get(("nr",))], inc="pe")
            prev[("sqw",)] = tnm
            tnr = p.op("vector", lambda E, col=nrcol: E.tensor_reduce(
                rnorm[0:1, col:col + 1], pb[6][0:1, 0:512], AX, ALU.max),
                wait=[tnm], inc="dv")
            prev[("nr",)] = tnr
            nrcol += 1
        prev[("wk", sl)] = tm
        for vb in range(8):
            bank = pb[vb % 2]
            tm4 = None
            for s4 in range(4):
                kc = vb * 4 + s4
                tm = None
                for ch in range(NCH):
                    tm = p.op("tensor",
                              lambda E, ch=ch, kc=kc, s4=s4, sl=sl, bank=bank:
                              E.matmul(bank[:, s4 * 64:(s4 + 1) * 64],
                                       sxg[:, ch * KT + kc * 128:
                                           ch * KT + kc * 128 + 128],
                                       swv[:, sl * 512 + ch * 64:
                                           sl * 512 + (ch + 1) * 64],
                                       start=(ch == 0), stop=(ch == NCH - 1)),
                              wait=([tv, prev.get(("pj", vb % 2)), t_v1]
                                    if (ch == 0 and s4 == 0) else []),
                              inc=("pe" if (ch == NCH - 1 and s4 == 3)
                                   else None))
                tm4 = tm
            te = p.op("vector", lambda E, vb=vb, bank=bank: E.tensor_copy(
                bass.AP(tap(sv), vb * 4 * 65, [[2080, 128], [65, 4], [1, 64]]),
                bass.AP(tap(bank), 0, [[512, 128], [64, 4], [1, 64]])),
                wait=[tm4, sstore.get(("v", h - 1))], inc="dv")
            prev[("pj", vb % 2)] = te
        prev[("wv", sl)] = tm4
        t1 = p.op("vector", lambda E: E.tensor_reduce(
            mrow[0:1, 0:1], rnorm[0:1, 0:4], AX, ALU.max), wait=[tnr],
            inc="dv")
        p.op("vector", lambda E: E.tensor_reduce(
            mrow[0:1, 1:2], rnorm[0:1, 4:12], AX, ALU.max))
        t3 = p.op("vector", lambda E: E.tensor_mul(
            mrow[0:1, 2:3], mrow[0:1, 0:1], mrow[0:1, 1:2]), inc="dv")
        t4 = p.op("scalar", lambda E: E.activation(
            mrow[0:1, 3:4], mrow[0:1, 2:3], AF.Sqrt,
            bias=env["szero"][0:1, 0:1]), wait=[t3, env["cst"]], inc="ac")
        tneg = p.op("vector", lambda E, h=h: E.tensor_scalar(
            negrow[0:1, h:h + 1], mrow[0:1, 3:4], -1.0, None, ALU.mult),
            wait=[t4], inc="dv")
        sstore[("q", h)] = p.dma(
            "gpsimd", bass.AP(tap(env["qsid"]), h * 64 * 2048,
                              [[2048, 64], [1, 2048]]),
            qsb[0:64, 0:2048],
            wait=[prev[("pj", 0)], prev[("pj", 1)], sstore.get(("q", h - 1))],
            stream="qs")
        sstore[("k", h)] = p.dma(
            "gpsimd", bass.AP(tap(env["ksid"]), h * 64 * 4096,
                              [[4096, 64], [1, 4096]]),
            ksb[0:64, 0:4096],
            wait=[prev[("pj", 0)], prev[("pj", 1)], sstore.get(("k", h - 1))],
            stream="ks")
        sstore[("v", h)] = p.dma(
            "gpsimd", bass.AP(tap(env["vsid"]), h * 128 * 2080,
                              [[2080, 128], [1, 2080]]),
            sv[:, 0:2080],
            wait=[prev[("pj", 0)], prev[("pj", 1)], sstore.get(("v", h - 1))],
            stream="vs")
    tbm = p.op("tensor", lambda E: E.matmul(
        pb[7][:, 0:16], env["ones_r"][0:1, 0:128], negrow[0:1, 0:16],
        start=True, stop=True), wait=[tneg], inc="pe")
    tbc = p.op("vector", lambda E: E.tensor_copy(
        env["snegm"][:, 0:16], pb[7][:, 0:16]), wait=[tbm], inc="dv")
    return [sstore[("q", 15)], sstore[("k", 15)], sstore[("v", 15)],
            tbc, tus]


def emit_l1attn_scores(p, env, pa_done, dbg=None):
    pb = env["pb"]
    qsb2, ksb2, sv2 = env["qsb2"], env["ksb2"], env["sv2"]
    sub, Eb, escr = env["sub"], env["Eb"], env["escr"]
    onum, of32, osb, srcp = env["onum"], env["of32"], env["osb"], env["srcp"]
    prev = {}
    ost = {}
    for h in range(16):
        w0 = pa_done if h == 0 else []
        tql = p.dma("sync", qsb2[0:64, 0:2048],
                    bass.AP(tap(env["qsid"]), h * 64 * 2048,
                            [[2048, 64], [1, 2048]]),
                    wait=[prev.get(("qfree",))] + w0, stream="qL")
        tkl = p.dma("sync", ksb2[0:64, 0:4096],
                    bass.AP(tap(env["ksid"]), h * 64 * 4096,
                            [[4096, 64], [1, 4096]]),
                    wait=[prev.get(("kfree",))] + w0, stream="kL")
        tvl = p.dma("sync", sv2[:, 0:2080],
                    bass.AP(tap(env["vsid"]), h * 128 * 2080,
                            [[2080, 128], [1, 2080]]),
                    wait=[prev.get(("vfree",))] + w0, stream="vL")
        tsl = p.dma("sync", sub[:, 0:6016],
                    bass.AP(tap(env["ubexp"]), h * 6144,
                            [[1, 128], [1, 6016]]),
                    wait=[prev.get(("subfree",))] + w0, stream="sL")
        for qg in range(4):
            etoks = []
            for c in range(32):
                bank = pb[c % 2]
                tm = p.op("tensor", lambda E, c=c, qg=qg, bank=bank: E.matmul(
                    bank[:, 0:512], ksb2[0:64, c * 128:(c + 1) * 128],
                    qsb2[0:64, qg * 512:(qg + 1) * 512],
                    start=True, stop=True),
                    wait=([tql, tkl, prev.get(("sc", c % 2))]
                          if c < 2 else [prev.get(("sc", c % 2))]),
                    inc="pe")
                tex = p.op("scalar", lambda E, bank=bank, h=h, sl=c % 2:
                           E.activation(escr[:, sl * 512:(sl + 1) * 512],
                                        bank[:, 0:512], AF.Exp,
                                        bias=env["snegm"][:, h:h + 1]),
                           wait=[tm] + ([etoks[c - 2]] if c >= 2
                                        else [prev.get(("esc", c % 2))]),
                           inc="ac")
                prev[("sc", c % 2)] = tex
                tmu = p.op("vector", lambda E, c=c, qg=qg, sl=c % 2:
                           E.tensor_tensor(
                               Eb[:, c * 512:(c + 1) * 512],
                               escr[:, sl * 512:(sl + 1) * 512],
                               bass.AP(tap(sub), c * 128 - qg * 512 + 2047,
                                       [[6016, 128], [-1, 512]]),
                               ALU.mult),
                           wait=([tex, tsl, prev.get(("Ewar",))]
                                 if c == 0 else [tex]), exact=True)
                etoks.append(tmu)
            avb = pb[2 + (qg % 2)]
            tav = None
            for c in range(32):
                tav = p.op("tensor", lambda E, c=c, avb=avb: E.matmul(
                    avb[0:65, 0:512], sv2[:, c * 65:(c + 1) * 65],
                    Eb[:, c * 512:(c + 1) * 512],
                    start=(c == 0), stop=(c == 31)),
                    wait=([tvl, etoks[c], prev.get(("av", qg % 2))]
                          if c == 0 else [etoks[c]]),
                    inc=("pw" if c == 31 else None))
            prev[("Ewar",)] = tav
            prev[("esc", 0)] = etoks[-2]
            prev[("esc", 1)] = etoks[-1]
            trc = p.op("vector", lambda E, avb=avb: E.tensor_scalar_max(
                srcp[0:1, 0:512], avb[64:65, 0:512], 1e-37),
                wait=[tav], inc="dv")
            trc2 = p.op("vector", lambda E: E.reciprocal(
                srcp[0:1, 0:512], srcp[0:1, 0:512]), inc="dv")
            tbc2 = p.op("tensor", lambda E: E.matmul(
                pb[4][0:64, 0:512], env["ones_r"][0:1, 0:64],
                srcp[0:1, 0:512], start=True, stop=True),
                wait=[trc2, prev.get(("bb",))], inc="pe")
            tnum = p.op("scalar", lambda E, avb=avb: E.activation(
                onum[0:64, 0:512], avb[0:64, 0:512], AF.Identity),
                wait=[tav], inc="ac")
            tdv = p.op("vector", lambda E: E.tensor_tensor(
                of32[0:64, 0:512], onum[0:64, 0:512], pb[4][0:64, 0:512],
                ALU.mult), wait=[tbc2, tnum], exact=True)
            prev[("bb",)] = tdv
            prev[("av", qg % 2)] = tdv
            tos = p.op("scalar", lambda E, h=h: E.activation(
                osb[0:64, 0:512], of32[0:64, 0:512], AF.Identity,
                bias=env["sbv"][0:64, h:h + 1]),
                wait=[tdv, env["bv_tok"], ost.get(("w",))], exact=True)
            ost[("w",)] = p.dma(
                "gpsimd", bass.AP(tap(env["osid"]),
                                  h * 64 * 2048 + qg * 512,
                                  [[2048, 64], [1, 512]]),
                osb[0:64, 0:512], wait=[tos, ost.get(("w",))], stream="os")
        prev[("qfree",)] = tav
        prev[("kfree",)] = tav
        prev[("vfree",)] = tav
        prev[("subfree",)] = etoks[-1]
    return [ost[("w",)]]


def emit_l1attn_wo(p, env, ps_done):
    pb = env["pb"]
    swo, so_r = env["swo"], env["so_r"]
    sxi, sxo = env["sxi"], env["sxo"]
    prev = {}
    wo_toks = {}
    for h in range(16):
        wo_toks[h] = p.dma(
            "gpsimd", swo[0:64, h * 1024:(h + 1) * 1024],
            env["hwoT"][h * 64:(h + 1) * 64, 0:1024],
            wait=(ps_done if h < 2 else [wo_toks[h - 2]]),
            stream="wo%d" % (h % 2))
    xst = []
    for qg in range(4):
        so_toks = {}
        for h in range(16):
            so_toks[h] = p.dma(
                "sync", so_r[0:64, h * 512:(h + 1) * 512],
                bass.AP(tap(env["osid"]), h * 64 * 2048 + qg * 512,
                        [[2048, 64], [1, 512]]),
                wait=[prev.get(("sor", h % 4))] +
                     (ps_done if (qg == 0 and h < 4) else []),
                stream="sor%d" % (h % 4))
        for H in range(NCH):
            bank = pb[5 + (H % 2)]
            sl = H % 2
            txl = p.dma("sync", sxi[:, sl * 512:(sl + 1) * 512],
                        env["x_in"][H * 128:(H + 1) * 128,
                                    qg * 512:(qg + 1) * 512],
                        wait=[prev.get(("xi", sl))], stream="xi%d" % sl)
            tmm = None
            for hh in range(16):
                tmm = p.op("tensor", lambda E, hh=hh, H=H, bank=bank:
                           E.matmul(bank[:, 0:512],
                                    swo[0:64, hh * 1024 + H * 128:
                                        hh * 1024 + H * 128 + 128],
                                    so_r[0:64, hh * 512:(hh + 1) * 512],
                                    start=(hh == 0), stop=(hh == 15)),
                           wait=([wo_toks[14], wo_toks[15], so_toks[12],
                                  so_toks[13], so_toks[14], so_toks[15],
                                  prev.get(("wb", H % 2))]
                                 if hh == 0 else []),
                           inc=("pe" if hh == 15 else None))
            tev = p.op("scalar", lambda E, bank=bank, H=H, sl=sl:
                       E.activation(sxo[:, sl * 512:(sl + 1) * 512],
                                    bank[:, 0:512], AF.Identity,
                                    bias=env["sbo"][:, H:H + 1]),
                       wait=[tmm, env["bo_tok"], prev.get(("xo", sl))],
                       inc="ac")
            tad = p.op("vector", lambda E, sl=sl: E.tensor_add(
                sxo[:, sl * 512:(sl + 1) * 512],
                sxo[:, sl * 512:(sl + 1) * 512],
                sxi[:, sl * 512:(sl + 1) * 512]),
                wait=[tev, txl], inc="dv")
            tst = p.dma("gpsimd", env["xo"][H * 128:(H + 1) * 128,
                                            qg * 512:(qg + 1) * 512],
                        sxo[:, sl * 512:(sl + 1) * 512],
                        wait=[tad], stream="xst%d" % sl)
            prev[("xo", sl)] = tst
            prev[("xi", sl)] = tad
            prev[("wb", sl)] = tev
            xst.append(tst)
        for hh in range(4):
            prev[("sor", hh)] = tmm
    return xst


def emit_l1attn_projA_fused(p, env):
    pb = env["pb"]
    sxn = env["sxn"]
    swq, swk, swv = env["swq"], env["swk"], env["swv"]
    qsb, ksb, sv = env["qsb"], env["ksb"], env["sv"]
    sq2, rnorm = env["sq2"], env["rnorm"]
    prev = {}
    t_v1 = p.op("gpsimd", lambda E: E.memset(
        bass.AP(tap(sv), 64, [[1040, 128], [65, 16]]), 1.0), inc="gp")
    tus = []
    for s in range(2):
        tub = p.dma("gpsimd", env["subf"][0:16, 0:3072],
                    env["ub_in"][0:16, s * 3072:(s + 1) * 3072],
                    wait=[prev.get(("ubb",))], stream="ub")
        tue = p.op("scalar", lambda E: E.activation(
            env["subb"][0:16, 0:3072], env["subf"][0:16, 0:3072], AF.Exp),
            wait=[tub, prev.get(("ubs",))], inc="ac")
        prev[("ubb",)] = tue
        t = p.dma("gpsimd", env["ubexp"][0:16, s * 3072:(s + 1) * 3072],
                  env["subb"][0:16, 0:3072], wait=[tue], stream="ub")
        prev[("ubs",)] = t
        tus.append(t)
    tbs = p.op("vector", lambda E: E.tensor_scalar(
        env["sbqs"][0:64, 0:16], env["sbq"][0:64, 0:16], 0.125, None,
        ALU.mult), wait=[env["bq_tok"]], inc="dv")
    ln_last = env["lnlast"]
    sstore = {}
    tqn = None
    for h in range(16):
        sl = h % 2
        tq = p.dma("gpsimd",
                   swq[:, sl * 512:(sl + 1) * 512].rearrange(
                       "p (c m) -> p c m", c=8),
                   env["hwq1"][:, h * 64:(h + 1) * 64].rearrange(
                       "(c p) m -> p c m", p=128),
                   wait=[prev.get(("wq", sl)), env["cst"]],
                   stream="wq%d" % sl)
        tk = p.dma("gpsimd",
                   swk[:, sl * 512:(sl + 1) * 512].rearrange(
                       "p (c m) -> p c m", c=8),
                   env["hwk1"][:, h * 64:(h + 1) * 64].rearrange(
                       "(c p) m -> p c m", p=128),
                   wait=[prev.get(("wk", sl))], stream="wk%d" % sl)
        tv = p.dma("gpsimd",
                   swv[:, sl * 512:(sl + 1) * 512].rearrange(
                       "p (c m) -> p c m", c=8),
                   env["hwv1"][:, h * 64:(h + 1) * 64].rearrange(
                       "(c p) m -> p c m", p=128),
                   wait=[prev.get(("wv", sl))], stream="wv%d" % sl)
        tnr = None
        for qg in range(4):
            bank = pb[qg % 2]
            tm = None
            for ch in range(NCH):
                tm = p.op("tensor", lambda E, ch=ch, qg=qg, sl=sl, bank=bank:
                          E.matmul(bank[0:64, 0:512],
                                   swq[:, sl * 512 + ch * 64:
                                       sl * 512 + (ch + 1) * 64],
                                   sxn[:, ch * T + qg * 512:
                                       ch * T + qg * 512 + 512],
                                   start=(ch == 0), stop=(ch == NCH - 1)),
                          wait=([tq, prev.get(("pj", qg % 2)), ln_last, tbs]
                                if ch == 0 else []),
                          inc=("pe" if ch == NCH - 1 else None))
            te = p.op("scalar", lambda E, qg=qg, h=h, bank=bank: E.activation(
                qsb[0:64, qg * 512:(qg + 1) * 512], bank[0:64, 0:512],
                AF.Identity, bias=env["sbqs"][0:64, h:h + 1], scale=0.125),
                wait=[tm, sstore.get(("q", h - 1))], inc="ac")
            prev[("pj", qg % 2)] = te
            tsq = p.op("scalar", lambda E, qg=qg: E.activation(
                sq2[0:64, 0:512], qsb[0:64, qg * 512:(qg + 1) * 512],
                AF.Square), wait=[prev.get(("sqw",))], inc="ac")
            tnm = p.op("tensor", lambda E: E.matmul(
                pb[6][0:1, 0:512], env["onesc"][0:64, 0:1],
                sq2[0:64, 0:512], start=True, stop=True),
                wait=[tsq, prev.get(("nr",))], inc="pe")
            prev[("sqw",)] = tnm
            tnr = p.op("vector", lambda E, col=qg: E.tensor_reduce(
                rnorm[0:1, col:col + 1], pb[6][0:1, 0:512], AX, ALU.max),
                wait=[tnm], inc="dv")
            prev[("nr",)] = tnr
        prev[("wq", sl)] = tm
        tqn = p.op("vector", lambda E, h=h: E.tensor_reduce(
            env["qnorm"][0:1, h:h + 1], rnorm[0:1, 0:4], AX, ALU.max),
            wait=[tnr], inc="dv")
        for kg in range(4):
            bank = pb[kg % 2]
            tm = None
            for ch in range(NCH):
                tm = p.op("tensor", lambda E, ch=ch, kg=kg, sl=sl, bank=bank:
                          E.matmul(bank[0:64, 0:512],
                                   swk[:, sl * 512 + ch * 64:
                                       sl * 512 + (ch + 1) * 64],
                                   sxn[:, ch * T + kg * 512:
                                       ch * T + kg * 512 + 512],
                                   start=(ch == 0), stop=(ch == NCH - 1)),
                          wait=([tk, prev.get(("pj", kg % 2))]
                                if ch == 0 else []),
                          inc=("pe" if ch == NCH - 1 else None))
            te = p.op("scalar", lambda E, kg=kg, h=h, bank=bank: E.activation(
                ksb[0:64, kg * 512:(kg + 1) * 512], bank[0:64, 0:512],
                AF.Identity, bias=env["sbk"][0:64, h:h + 1]),
                wait=[tm, env["bk_tok"], sstore.get(("k", h - 1))], inc="ac")
            prev[("pj", kg % 2)] = te
        prev[("wk", sl)] = tm
        for vb in range(4):
            bank = pb[vb % 2]
            tm4 = None
            for s4 in range(4):
                kc = vb * 4 + s4
                tm = None
                for ch in range(NCH):
                    tm = p.op("tensor",
                              lambda E, ch=ch, kc=kc, s4=s4, sl=sl, bank=bank:
                              E.matmul(bank[:, s4 * 64:(s4 + 1) * 64],
                                       sxn[:, ch * T + kc * 128:
                                           ch * T + kc * 128 + 128],
                                       swv[:, sl * 512 + ch * 64:
                                           sl * 512 + (ch + 1) * 64],
                                       start=(ch == 0), stop=(ch == NCH - 1)),
                              wait=([tv, prev.get(("pj", vb % 2)), t_v1]
                                    if (ch == 0 and s4 == 0) else []),
                              inc=("pe" if (ch == NCH - 1 and s4 == 3)
                                   else None))
                tm4 = tm
            te = p.op("vector", lambda E, vb=vb, bank=bank: E.tensor_copy(
                bass.AP(tap(sv), vb * 4 * 65, [[1040, 128], [65, 4], [1, 64]]),
                bass.AP(tap(bank), 0, [[512, 128], [64, 4], [1, 64]])),
                wait=[tm4, sstore.get(("v", h - 1))], inc="dv")
            prev[("pj", vb % 2)] = te
        prev[("wv", sl)] = tm4
        sstore[("q", h)] = p.dma(
            "gpsimd", bass.AP(tap(env["qsid"]), h * 64 * 2048,
                              [[2048, 64], [1, 2048]]),
            qsb[0:64, 0:2048],
            wait=[prev[("pj", 0)], prev[("pj", 1)]], stream="qs")
        sstore[("k", h)] = p.dma(
            "gpsimd", bass.AP(tap(env["ksid_own"]), h * 64 * 2048,
                              [[2048, 64], [1, 2048]]),
            ksb[0:64, 0:2048],
            wait=[prev[("pj", 0)], prev[("pj", 1)]], stream="ks")
        sstore[("v", h)] = p.dma(
            "gpsimd", bass.AP(tap(env["vsid_own"]), h * 128 * 1040,
                              [[1040, 128], [1, 1040]]),
            sv[:, 0:1040],
            wait=[prev[("pj", 0)], prev[("pj", 1)]], stream="vs")
    return [sstore[("q", 15)], sstore[("k", 15)], sstore[("v", 15)],
            tqn] + tus


def emit_l1attn_scores_fused(p, env, pa_done, ag_toks):
    pb = env["pb"]
    qsb2, ksb2, sv2 = env["qsb2"], env["ksb2"], env["sv2"]
    sub, Eb, escr = env["sub"], env["Eb"], env["escr"]
    onum, of32, osb, srcp = env["onum"], env["of32"], env["osb"], env["srcp"]
    sq2b, rnk, mr2 = env["sq2b"], env["rnk"], env["mr2"]
    snegm = env["snegm"]
    prev = {}
    ost = {}
    for h in range(16):
        w0 = (pa_done + ag_toks[:1]) if h == 0 else []
        wv0 = (pa_done + ag_toks) if h == 0 else []
        tql = p.dma("sync", qsb2[0:64, 0:2048],
                    bass.AP(tap(env["qsid"]), h * 64 * 2048,
                            [[2048, 64], [1, 2048]]),
                    wait=[prev.get(("qfree",))] + w0, stream="qL")
        tkl = None
        for hf in range(2):
            tkl = p.dma("sync", ksb2[0:64, hf * 2048:(hf + 1) * 2048],
                        bass.AP(tap(env["ksid_all"]),
                                (hf * 16 + h) * 64 * 2048,
                                [[2048, 64], [1, 2048]]),
                        wait=[prev.get(("kfree",))] + w0 if hf == 0 else [],
                        stream="kL")
        tvl = None
        for hf in range(2):
            tvl = p.dma("sync", sv2[:, hf * 1040:(hf + 1) * 1040],
                        bass.AP(tap(env["vsid_all"]),
                                (hf * 16 + h) * 128 * 1040,
                                [[1040, 128], [1, 1040]]),
                        wait=[prev.get(("vfree",))] + wv0 if hf == 0 else [],
                        stream="vL")
        tsl = p.dma("sync", sub[:, 0:6016],
                    bass.AP(tap(env["ubexp"]), h * 6144,
                            [[1, 128], [1, 6016]]),
                    wait=[prev.get(("subfree",))] + w0, stream="sL")
        tnr = None
        for kg in range(8):
            tsq = p.op("scalar", lambda E, kg=kg: E.activation(
                sq2b[0:64, 0:512], ksb2[0:64, kg * 512:(kg + 1) * 512],
                AF.Square),
                wait=([tkl] if kg == 0 else []) + [prev.get(("sqw",))],
                inc="ac")
            tnm = p.op("tensor", lambda E: E.matmul(
                pb[6][0:1, 0:512], env["onesc"][0:64, 0:1],
                sq2b[0:64, 0:512], start=True, stop=True),
                wait=[tsq, prev.get(("nr",))], inc="pe")
            prev[("sqw",)] = tnm
            tnr = p.op("vector", lambda E, col=kg: E.tensor_reduce(
                rnk[0:1, col:col + 1], pb[6][0:1, 0:512], AX, ALU.max),
                wait=[tnm], inc="dv")
            prev[("nr",)] = tnr
        p.op("vector", lambda E: E.tensor_reduce(
            mr2[0:1, 0:1], rnk[0:1, 0:8], AX, ALU.max), wait=[tnr])
        t2 = p.op("vector", lambda E, h=h: E.tensor_mul(
            mr2[0:1, 1:2], mr2[0:1, 0:1], env["qnorm"][0:1, h:h + 1]),
            inc="dv")
        t3 = p.op("scalar", lambda E: E.activation(
            mr2[0:1, 2:3], mr2[0:1, 1:2], AF.Sqrt,
            bias=env["szero"][0:1, 0:1]), wait=[t2], inc="ac")
        t4 = p.op("vector", lambda E: E.tensor_scalar(
            mr2[0:1, 3:4], mr2[0:1, 2:3], -1.0, None, ALU.mult),
            wait=[t3], inc="dv")
        t5 = p.op("tensor", lambda E: E.matmul(
            pb[5][:, 0:1], env["ones_r"][0:1, 0:128], mr2[0:1, 3:4],
            start=True, stop=True), wait=[t4, prev.get(("negb",))], inc="pe")
        t6 = p.op("vector", lambda E: E.tensor_copy(
            snegm[:, 0:1], pb[5][:, 0:1]),
            wait=[t5, prev.get(("negw",))], exact=True)
        prev[("negb",)] = t6
        for qg in range(4):
            etoks = []
            for c in range(32):
                bank = pb[c % 2]
                tm = p.op("tensor", lambda E, c=c, qg=qg, bank=bank: E.matmul(
                    bank[:, 0:512], ksb2[0:64, c * 128:(c + 1) * 128],
                    qsb2[0:64, qg * 512:(qg + 1) * 512],
                    start=True, stop=True),
                    wait=([tql, tkl, prev.get(("sc", c % 2))]
                          if c < 2 else [prev.get(("sc", c % 2))]),
                    inc="pe")
                tex = p.op("scalar", lambda E, bank=bank, sl=c % 2:
                           E.activation(escr[:, sl * 512:(sl + 1) * 512],
                                        bank[:, 0:512], AF.Exp,
                                        bias=snegm[:, 0:1]),
                           wait=[tm, t6] + ([etoks[c - 2]] if c >= 2
                                            else [prev.get(("esc", c % 2))]),
                           exact=True)
                prev[("sc", c % 2)] = tex
                tmu = p.op("vector", lambda E, c=c, qg=qg, sl=c % 2:
                           E.tensor_tensor(
                               Eb[:, c * 512:(c + 1) * 512],
                               escr[:, sl * 512:(sl + 1) * 512],
                               bass.AP(tap(sub), c * 128 - qg * 512 + 2047,
                                       [[6016, 128], [-1, 512]]),
                               ALU.mult),
                           wait=([tex, tsl, prev.get(("Ewar",))]
                                 if c == 0 else [tex]), exact=True)
                etoks.append(tmu)
            avb = pb[2 + (qg % 2)]
            tav = None
            for c in range(32):
                tav = p.op("tensor", lambda E, c=c, avb=avb: E.matmul(
                    avb[0:65, 0:512], sv2[:, c * 65:(c + 1) * 65],
                    Eb[:, c * 512:(c + 1) * 512],
                    start=(c == 0), stop=(c == 31)),
                    wait=([tvl, etoks[c], prev.get(("av", qg % 2))]
                          if c == 0 else [etoks[c]]),
                    inc=("pw" if c == 31 else None))
            prev[("Ewar",)] = tav
            prev[("esc", 0)] = etoks[-2]
            prev[("esc", 1)] = etoks[-1]
            trc = p.op("vector", lambda E, avb=avb: E.tensor_scalar_max(
                srcp[0:1, 0:512], avb[64:65, 0:512], 1e-37),
                wait=[tav], inc="dv")
            trc2 = p.op("vector", lambda E: E.reciprocal(
                srcp[0:1, 0:512], srcp[0:1, 0:512]), inc="dv")
            tbc2 = p.op("tensor", lambda E: E.matmul(
                pb[4][0:64, 0:512], env["ones_r"][0:1, 0:64],
                srcp[0:1, 0:512], start=True, stop=True),
                wait=[trc2, prev.get(("bb",))], inc="pe")
            tnum = p.op("scalar", lambda E, avb=avb: E.activation(
                onum[0:64, 0:512], avb[0:64, 0:512], AF.Identity),
                wait=[tav], inc="ac")
            tdv = p.op("vector", lambda E: E.tensor_tensor(
                of32[0:64, 0:512], onum[0:64, 0:512], pb[4][0:64, 0:512],
                ALU.mult), wait=[tbc2, tnum], exact=True)
            prev[("bb",)] = tdv
            prev[("av", qg % 2)] = tdv
            tos = p.op("scalar", lambda E, h=h: E.activation(
                osb[0:64, 0:512], of32[0:64, 0:512], AF.Identity,
                bias=env["sbv"][0:64, h:h + 1]),
                wait=[tdv, env["bv_tok"], ost.get(("w",))], exact=True)
            ost[("w",)] = p.dma(
                "gpsimd", bass.AP(tap(env["osid"]),
                                  h * 64 * 2048 + qg * 512,
                                  [[2048, 64], [1, 512]]),
                osb[0:64, 0:512], wait=[tos], stream="os")
        prev[("qfree",)] = tav
        prev[("kfree",)] = tav
        prev[("vfree",)] = tav
        prev[("subfree",)] = etoks[-1]
        prev[("negw",)] = prev[("sc", 1)]
    return [ost[("w",)]]


def emit_l1attn_wo_fused(p, env, ps_done, x_sb):
    pb = env["pb"]
    swo, so_r, sxo = env["swo"], env["so_r"], env["sxo"]
    prev = {}
    wo_toks = {}
    x_upd = {}
    for h in range(16):
        wo_toks[h] = p.dma(
            "gpsimd", swo[0:64, h * 1024:(h + 1) * 1024],
            env["hwo1"][h * 64:(h + 1) * 64, 0:1024],
            wait=(ps_done if h < 2 else []), stream="wo%d" % (h % 2))
    for qg in range(4):
        so_toks = {}
        for h in range(16):
            so_toks[h] = p.dma(
                "sync", so_r[0:64, h * 512:(h + 1) * 512],
                bass.AP(tap(env["osid"]), h * 64 * 2048 + qg * 512,
                        [[2048, 64], [1, 512]]),
                wait=[prev.get(("sor", h % 4))] +
                     (ps_done if (qg == 0 and h < 4) else []),
                stream="sor%d" % (h % 4))
        for H in range(NCH):
            bank = pb[5 + (H % 2)]
            sl = H % 2
            tmm = None
            for hh in range(16):
                tmm = p.op("tensor", lambda E, hh=hh, H=H, bank=bank:
                           E.matmul(bank[:, 0:512],
                                    swo[0:64, hh * 1024 + H * 128:
                                        hh * 1024 + H * 128 + 128],
                                    so_r[0:64, hh * 512:(hh + 1) * 512],
                                    start=(hh == 0), stop=(hh == 15)),
                           wait=([wo_toks[14], wo_toks[15], so_toks[12],
                                  so_toks[13], so_toks[14], so_toks[15],
                                  prev.get(("wb", H % 2))]
                                 if hh == 0 else []),
                           inc=("pe" if hh == 15 else None))
            tev = p.op("scalar", lambda E, bank=bank, H=H, sl=sl:
                       E.activation(sxo[:, sl * 512:(sl + 1) * 512],
                                    bank[:, 0:512], AF.Identity,
                                    bias=env["sbo"][:, H:H + 1]),
                       wait=[tmm, env["bo_tok"], prev.get(("xo", sl))],
                       inc="ac")
            xs = x_sb[:, H * T + qg * 512: H * T + qg * 512 + 512]
            tad = p.op("vector", lambda E, xs=xs, sl=sl: E.tensor_add(
                xs, sxo[:, sl * 512:(sl + 1) * 512], xs),
                wait=[tev], inc="dv")
            prev[("xo", sl)] = tad
            prev[("wb", sl)] = tev
            x_upd[(H, qg)] = tad
        for hh in range(4):
            prev[("sor", hh)] = tmm
    return x_upd


def build_fused(dbg=None, nstages=6):
    P.STREAM_ALIAS = {}
    for kind in "gu":
        for sl in range(2):
            for ch in range(8):
                P.STREAM_ALIAS["l1w%s%d%d" % (kind, sl, ch)] = \
                    "l1w%s%d" % (kind, sl)
    for xh, yh in (("watq", "wq"), ("watk", "wk"), ("watv", "wv"),
                   ("wao", "wo")):
        for i in range(2):
            P.STREAM_ALIAS["%s%d" % (xh, i)] = "%s%d" % (yh, i)
    nc = bass.Bass("TRN2", target_bir_lowering=False)
    x_in = nc.dram_tensor("x", [D, T], F32, kind="ExternalInput").ap()
    x_out = nc.dram_tensor("xo", [D, T], F32, kind="ExternalOutput").ap()
    hgw = nc.dram_tensor("gw", [D, NE], F32, kind="ExternalInput").ap()
    hgb = nc.dram_tensor("gb", [NE], F32, kind="ExternalInput").ap()
    hwg = nc.dram_tensor("wg", [10, D, IE], F32, kind="ExternalInput").ap()
    hwu = nc.dram_tensor("wu", [10, D, IE], F32, kind="ExternalInput").ap()
    hwd = nc.dram_tensor("wd", [10, IE, D], F32, kind="ExternalInput").ap()
    hwqT = nc.dram_tensor("wqT", [D, D], F32, kind="ExternalInput").ap()
    hwkT = nc.dram_tensor("wkT", [D, D], F32, kind="ExternalInput").ap()
    hwvT = nc.dram_tensor("wvT", [D, D], F32, kind="ExternalInput").ap()
    hwoT = nc.dram_tensor("woT", [D, D], F32, kind="ExternalInput").ap()
    hmask_in = nc.dram_tensor("hmask", [128, 1], F32,
                              kind="ExternalInput").ap()
    wqsrc = nc.dram_tensor("wqsrc", [128, QU_PER_CORE * 128], F32,
                           kind="ExternalInput").ap()
    hwq1 = nc.dram_tensor("aqT", [D, D], F32, kind="ExternalInput").ap()
    hwk1 = nc.dram_tensor("akT", [D, D], F32, kind="ExternalInput").ap()
    hwv1 = nc.dram_tensor("avT", [D, D], F32, kind="ExternalInput").ap()
    hwo1 = nc.dram_tensor("aoT", [D, D], F32, kind="ExternalInput").ap()
    ub_in = nc.dram_tensor("ub", [16, 6144], F32, kind="ExternalInput").ap()
    bq_in = nc.dram_tensor("bq", [64, 16], F32, kind="ExternalInput").ap()
    bk_in = nc.dram_tensor("bk", [64, 16], F32, kind="ExternalInput").ap()
    bv_in = nc.dram_tensor("bv", [64, 16], F32, kind="ExternalInput").ap()
    bo_in = nc.dram_tensor("bo", [128, 8], F32, kind="ExternalInput").ap()
    wdram = nc.dram_tensor("wdram", [NE * T], F32, kind="Internal").ap()
    qg = nc.dram_tensor("qg", [10, 128, 8 * 512], BF16, kind="Internal").ap()
    qu = nc.dram_tensor("qu", [10, 128, 8 * 512], BF16, kind="Internal").ap()
    qd = nc.dram_tensor("qd", [10, 128, 4 * 8 * 128], BF16,
                        kind="Internal").ap()
    cc1_in = nc.dram_tensor("ccxi", [D, T], BF16, kind="Internal").ap()
    cc1_out = nc.dram_tensor("ccxo", [2 * D, T], BF16, kind="Internal").ap()
    qown = nc.dram_tensor("qown", [QU_PER_CORE * 16384], BF16,
                          kind="Internal").ap()
    qall = nc.dram_tensor("qall", [8 * QU_PER_CORE * 16384], BF16,
                          kind="Internal", addr_space="Shared").ap()
    ubexp = nc.dram_tensor("ubexp", [16, 6144], BF16, kind="Internal").ap()
    qsid = nc.dram_tensor("qsid", [16, 64, 2048], BF16, kind="Internal").ap()
    ksid_own = nc.dram_tensor("ksid_own", [16, 64, 2048], BF16,
                              kind="Internal").ap()
    vsid_own = nc.dram_tensor("vsid_own", [16, 128, 1040], BF16,
                              kind="Internal").ap()
    ksid_all = nc.dram_tensor("ksid_all", [2, 16, 64, 2048], BF16,
                              kind="Internal").ap()
    vsid_all = nc.dram_tensor("vsid_all", [2, 16, 128, 1040], BF16,
                              kind="Internal").ap()
    osid = nc.dram_tensor("osid", [16, 64, 2048], BF16, kind="Internal").ap()

    env = {"hwg": hwg, "hwu": hwu, "hwd": hwd, "wdram": wdram,
           "qdram_g": qg, "qdram_u": qu, "qdram_d": qd,
           "hwqT": hwqT, "hwkT": hwkT, "hwvT": hwvT, "hwoT": hwoT,
           "wqsrc": wqsrc, "qown": qown, "qall": qall,
           "hwq1": hwq1, "hwk1": hwk1, "hwv1": hwv1, "hwo1": hwo1,
           "ub_in": ub_in, "ubexp": ubexp, "qsid": qsid,
           "ksid_own": ksid_own, "vsid_own": vsid_own,
           "ksid_all": ksid_all, "vsid_all": vsid_all, "osid": osid}
    RG = [[0, 1], [2, 3], [4, 5], [6, 7]]
    with ExitStack() as ctx:
        e = ctx.enter_context
        env["ones_f1024"] = e(nc.sbuf_tensor("ones_f1024", [128, 1], F32))
        env["ones_b1024"] = e(nc.sbuf_tensor("ones_b1024", [128, 1], BF16))
        env["ones_r"] = e(nc.sbuf_tensor("ones_r", [1, 128], F32))
        env["eps"] = e(nc.sbuf_tensor("eps", [1, 1], F32))
        env["identf"] = e(nc.sbuf_tensor("identf", [128, 128], F32))
        env["identb"] = e(nc.sbuf_tensor("identb", [128, 128], BF16))
        env["cidx8"] = e(nc.sbuf_tensor("cidx8", [128, 128], F32))
        env["cidx4"] = e(nc.sbuf_tensor("cidx4", [128, 64], F32))
        env["cidx8P"] = e(nc.sbuf_tensor("cidx8P", [128, 128], F32))
        env["cidx4P"] = e(nc.sbuf_tensor("cidx4P", [128, 64], F32))
        env["ones_r64"] = e(nc.sbuf_tensor("ones_r64", [128, 64], F32))
        env["sUT"] = e(nc.sbuf_tensor("sUT", [128, 128], BF16))
        env["hmask"] = e(nc.sbuf_tensor("hmaskb", [128, 1], F32))
        env["onesc"] = e(nc.sbuf_tensor("onesc", [64, 1], BF16))
        env["szero"] = e(nc.sbuf_tensor("szero", [1, 1], F32))
        env["sbq"] = e(nc.sbuf_tensor("sbq", [64, 16], F32))
        env["sbqs"] = e(nc.sbuf_tensor("sbqs", [64, 16], F32))
        env["sbk"] = e(nc.sbuf_tensor("sbk", [64, 16], F32))
        env["sbv"] = e(nc.sbuf_tensor("sbv", [64, 16], F32))
        env["sbo"] = e(nc.sbuf_tensor("sbo", [128, 8], F32))
        env["qnorm"] = e(nc.sbuf_tensor("qnorm", [1, 16], F32))
        env["row"] = e(nc.sbuf_tensor("row", [1, 2048], F32))
        env["xsq"] = e(nc.sbuf_tensor("xsq", [128, 1024], BF16))
        env["lnsc"] = e(nc.sbuf_tensor("lnsc", [128, 512], F32))
        env["qz"] = e(nc.sbuf_tensor("qz", [128, 512], F32))
        env["qa"] = e(nc.sbuf_tensor("qa", [128, 512], F32))
        env["qb"] = e(nc.sbuf_tensor("qb", [128, 512], F32))
        env["qs"] = e(nc.sbuf_tensor("qs", [128, 512], F32))
        env["qbm_w"] = 32
        env["qbm"] = e(nc.sbuf_tensor("qbm", [128, 32], F32))
        env["qsc_w"] = 32
        env["qsc"] = e(nc.sbuf_tensor("qsc", [128, 32], F32))
        sx = e(nc.sbuf_tensor("sx", [128, NCH * T], F32))
        sxn = e(nc.sbuf_tensor("sxn", [128, NCH * T], BF16))
        env["sxn"] = sxn
        pbb = [e(nc.psum_tensor(f"pb{i}", [128, 512], F32)) for i in range(7)]

        p = P(nc)
        for s in ("q_sync", "q_gpsimd", "pe", "dv", "ac", "gp"):
            p.sem(s)
        env["pb"] = pbb + pbb[0:1]
        benv = env
        cst = emit_consts(p, env)
        p.op("gpsimd", lambda E: E.memset(env["ones_r64"][:, :], 1.0))
        t_ut = p.op("gpsimd", lambda E: E.memset(env["sUT"][:, :], 1.0),
                    inc="gp")
        t_ut = p.op("gpsimd", lambda E: E.affine_select(
            env["sUT"][:, :], env["sUT"][:, :], [[1, 128]], ALU.is_ge, 0.0,
            channel_multiplier=-1), wait=[t_ut], inc="gp")
        p.op("gpsimd", lambda E: E.memset(env["onesc"][:, :], 1.0))
        cst = p.op("gpsimd", lambda E: E.memset(env["szero"][:, :], 0.0),
                   inc="gp")
        env["cst"] = cst
        thm = p.dma("gpsimd", env["hmask"][:, :], hmask_in, stream="hm")
        env["bq_tok"] = p.dma("gpsimd", env["sbq"][:, :], bq_in, stream="bq")
        env["bk_tok"] = p.dma("gpsimd", env["sbk"][:, :], bk_in, stream="bk")
        env["bv_tok"] = p.dma("gpsimd", env["sbv"][:, :], bv_in, stream="bv")
        env["bo_tok"] = p.dma("gpsimd", env["sbo"][:, :], bo_in, stream="bo")
        tx = p.dma("gpsimd", sx[:, :].rearrange("p (c t) -> p c t", c=NCH),
                   x_in.rearrange("(c p) t -> p c t", p=128), stream="x")

        sQ0 = ExitStack()
        q0 = sQ0.enter_context
        env["qwbuf"] = q0(nc.sbuf_tensor("qwbuf", [128, 512], F32))
        env["qwout"] = q0(nc.sbuf_tensor("qwout", [128, 512], BF16))
        env["souT"] = q0(nc.sbuf_tensor("souT", [128, 512], BF16))
        env["pbq"] = q0(nc.psum_tensor("pbq", [128, 1024], BF16))
        wq_store = emit_wquant_dist(p, env)

        lnb_ff0 = {"mean": pbb[6], "msq": pbb[5], "bA": pbb[4], "bB": pbb[3]}
        lnb_ff1 = {"mean": pbb[2], "msq": pbb[3], "bA": pbb[4], "bB": pbb[5]}

        def l0ffn_stage(first, gate_w, sfx):
            env = dict(benv)
            sA = ExitStack()
            a = sA.enter_context
            env["sgw"] = a(nc.sbuf_tensor("sgw" + sfx, [128, 64], BF16))
            env["sgb"] = a(nc.sbuf_tensor("sgb" + sfx, [8, 1], F32))
            env["sfcT"] = a(nc.sbuf_tensor("sfcT" + sfx, [8, T], F32))
            env["sgate_w"] = 1536
            env["sgate"] = a(nc.sbuf_tensor("sgate" + sfx, [128, 1536], F32))
            env["swT"] = a(nc.sbuf_tensor("swT" + sfx, [16, NE * 128], F32))
            env["swrow"] = a(nc.sbuf_tensor("swrow" + sfx, [1, 1024], F32))
            env["swg"] = a(nc.sbuf_tensor("swgb" + sfx,
                                          [128, 2 * 8 * 512], BF16))
            env["swu"] = a(nc.sbuf_tensor("swub" + sfx,
                                          [128, 2 * 8 * 512], BF16))
            env["swd"] = a(nc.sbuf_tensor("swdb" + sfx,
                                          [128, 2 * 4 * 8 * 128], BF16))
            env["ssig"] = a(nc.sbuf_tensor("ssig" + sfx, [128, 1024], BF16))
            env["stmp"] = a(nc.sbuf_tensor("stmp" + sfx, [128, 512], BF16))
            env["shu"] = a(nc.sbuf_tensor("shu" + sfx, [128, 4 * 512], BF16))
            env["pb"] = pbb + pbb[0:1]
            tgw = p.dma("gpsimd",
                        env["sgw"][:, :].rearrange("p (c e) -> p c e", c=NCH),
                        hgw.rearrange("(c p) e -> p c e", p=128), stream="gw")
            tgb = p.dma("gpsimd", env["sgb"][:, :], hgb[:, None], stream="gb")
            xn_tok, ln_done = emit_ln(p, env, sx, sxn, T, lnb_ff0,
                                      wait=gate_w + [tx, cst, tgw, tgb])
            x_upd = emit_l0ffn(p, env, sx, sxn, xn_tok, ln_done, first)
            fin = list(x_upd.values())
            if first:
                fin += list(env["qstore"].values())
            sA.close()
            p.barrier(fin)
            return fin

        fin = fin1 = l0ffn_stage(True, [], "")

        qall_tok = p.op("gpsimd", lambda E: E.collective_compute(
            "AllGather", ALU.bypass, replica_groups=[list(range(8))],
            ins=[env["qown"]], outs=[env["qall"]]),
            wait=wq_store, inc="gp")
        benv["qall_tok"] = qall_tok
        p.barrier(fin1 + [qall_tok])
        sQ0.close()

        env = dict(benv)
        sA = ExitStack()
        a = sA.enter_context
        env["sQ"] = a(nc.sbuf_tensor("sQ", [128, 2 * T], BF16))
        env["sKT"] = a(nc.sbuf_tensor("sKT", [128, 2 * T], BF16))
        env["sKtok"] = a(nc.sbuf_tensor("sKtok", [128, 16 * 128], BF16))
        env["sV_w"] = 16 * 130
        env["sV"] = a(nc.sbuf_tensor("sV", [128, 16 * 130], BF16))
        env["sxp"] = a(nc.sbuf_tensor("sxp", [128, 2 * 1024], BF16))
        env["sS"] = a(nc.sbuf_tensor("sS", [128, 130], BF16))
        env["sSf"] = a(nc.sbuf_tensor("sSf", [128, 130], F32))
        env["sAm"] = a(nc.sbuf_tensor("sAm", [128, 256], BF16))
        env["srcp"] = a(nc.sbuf_tensor("srcp", [1, 256], F32))
        env["sOutE"] = a(nc.sbuf_tensor("sOutE", [64, 256], BF16))
        env["sAtt"] = a(nc.sbuf_tensor("sAtt", [64, 2 * T], BF16))
        env["escr"] = a(nc.sbuf_tensor("escr", [128, 1024], F32))
        env["sw_w"] = 2 * 5 * 8 * 128
        env["swpair"] = a(nc.sbuf_tensor("swpair", [128, env["sw_w"]], BF16))
        pb7a = a(nc.psum_tensor("pb7a", [128, 512], F32))
        env["pb"] = pbb + [pb7a]
        t_v1 = p.op("gpsimd", lambda E, t=env["sV"]: E.memset(t[:, :], 1.0),
                    wait=fin1, inc="gp")
        p.op("gpsimd", lambda E, t=env["sQ"]: E.memset(t[64:128, :], 0.0))
        t_v1 = p.op("gpsimd", lambda E, t=env["sKT"]: E.memset(
            t[64:128, :], 0.0), inc="gp")
        env["v1_tok"] = t_v1
        xn_tok2, ln_done2 = emit_ln(p, env, sx, sxn, T, lnb_ff0,
                                    wait=fin1 + [cst, t_ut, thm])
        tccs = p.dma("gpsimd", cc1_in.rearrange("(c p) t -> p c t", p=128),
                     sxn[:, :].rearrange("p (c t) -> p c t", c=NCH),
                     wait=[ln_done2], stream="ccs")
        tcc = p.op("gpsimd", lambda E: E.collective_compute(
            "AllGather", ALU.bypass, replica_groups=RG,
            ins=[cc1_in], outs=[cc1_out]), wait=[tccs], inc="gp")
        if nstages >= 2:
            x_att = emit_l0attn(p, env, sx, sxn, xn_tok2, cc1_out[0:D, :],
                                [tcc], ln_done2, npairs=8, nphase=4)
            fin = fin2 = list(x_att.values())
        sA.close()
        p.barrier(fin)

        if nstages >= 3:
            fin = fin3 = l0ffn_stage(False, fin2, "b")

        def l1ffn_stage(first, gate_w, sfx):
            env = dict(benv)
            sA = ExitStack()
            a = sA.enter_context
            env["pbt"] = a(nc.psum_tensor("pbt" + sfx, [128, 1024], BF16))
            env["pb"] = pbb
            env["sxq"] = a(nc.sbuf_tensor("sxq" + sfx, [128, 8 * 512], BF16))
            env["shq"] = a(nc.sbuf_tensor("shq" + sfx, [128, 22 * 512], BF16))
            env["sxt"] = a(nc.sbuf_tensor("sxt" + sfx, [128, 4096], BF16))
            env["ssil"] = a(nc.sbuf_tensor("ssil" + sfx, [128, 512], F32))
            env["shts"] = a(nc.sbuf_tensor("shts" + sfx, [128, 512], F32))
            env["shtq"] = a(nc.sbuf_tensor("shtq" + sfx, [128, 512], BF16))
            env["swg1"] = a(nc.sbuf_tensor("swg1" + sfx,
                                           [128, 2 * 8 * 512], BF16))
            env["swu1"] = a(nc.sbuf_tensor("swu1" + sfx,
                                           [128, 2 * 8 * 512], BF16))
            env["swd1"] = a(nc.sbuf_tensor("swd1" + sfx, [128, 2048], BF16))
            env["qzP"] = a(nc.sbuf_tensor("qzP" + sfx, [128, 512], F32))
            env["qaP"] = a(nc.sbuf_tensor("qaP" + sfx, [128, 512], F32))
            env["qbP"] = a(nc.sbuf_tensor("qbP" + sfx, [128, 512], F32))
            env["qsP"] = a(nc.sbuf_tensor("qsP" + sfx, [128, 512], F32))
            env["qbm_wP"] = 32
            env["qbmP"] = a(nc.sbuf_tensor("qbmP" + sfx, [128, 32], F32))
            env["qsc_wP"] = 32
            env["qscP"] = a(nc.sbuf_tensor("qscP" + sfx, [128, 32], F32))
            xn_tok, ln_done = emit_ln(p, env, sx, sxn, T, lnb_ff1,
                                      wait=gate_w + [cst])
            x_upd = emit_l1ffn(p, env, sx, sxn, xn_tok, ln_done, first, {})
            fin = list(x_upd.values())
            sA.close()
            p.barrier(fin)
            return fin

        if nstages >= 4:
            fin = fin4 = l1ffn_stage(True, fin3, "")

        if nstages < 5:
            st = p.dma("gpsimd",
                       x_out.rearrange("(c p) t -> p c t", p=128),
                       sx[:, :].rearrange("p (c t) -> p c t", c=NCH),
                       wait=fin, stream="st")
            p.op("gpsimd", lambda E: E.engine_nop(), wait=[st])
            p.finish(ctx)
            P.STREAM_ALIAS = {}
            return nc

        env = dict(benv)
        sA = ExitStack()
        a = sA.enter_context
        env["swq"] = a(nc.sbuf_tensor("aswq", [128, 1024], BF16))
        env["swk"] = a(nc.sbuf_tensor("aswk", [128, 1024], BF16))
        env["swv"] = a(nc.sbuf_tensor("aswv", [128, 1024], BF16))
        env["qsb"] = a(nc.sbuf_tensor("qsb", [64, 2048], BF16))
        env["ksb"] = a(nc.sbuf_tensor("ksb", [64, 2048], BF16))
        env["sv"] = a(nc.sbuf_tensor("asv", [128, 1040], BF16))
        env["sq2"] = a(nc.sbuf_tensor("sq2", [64, 512], BF16))
        env["rnorm"] = a(nc.sbuf_tensor("rnorm", [1, 4], F32))
        env["subf"] = a(nc.sbuf_tensor("subf", [16, 3072], F32))
        env["subb"] = a(nc.sbuf_tensor("subb", [16, 3072], BF16))
        env["pb"] = pbb + pbb[0:1]
        xn_tok5, ln_done5 = emit_ln(p, env, sx, sxn, T, lnb_ff1,
                                    wait=fin4 + [cst])
        env["lnlast"] = ln_done5
        pa = emit_l1attn_projA_fused(p, env)
        tagk = p.op("gpsimd", lambda E: E.collective_compute(
            "AllGather", ALU.bypass, replica_groups=RG,
            ins=[ksid_own], outs=[ksid_all]), wait=pa, inc="gp")
        tagv = p.op("gpsimd", lambda E: E.collective_compute(
            "AllGather", ALU.bypass, replica_groups=RG,
            ins=[vsid_own], outs=[vsid_all]), wait=[tagk], inc="gp")
        sA.close()
        p.barrier(pa + [tagk, tagv])
        env = dict(env)
        sB = ExitStack()
        b_ = sB.enter_context
        env["qsb2"] = b_(nc.sbuf_tensor("qsb2", [64, 2048], BF16))
        env["ksb2"] = b_(nc.sbuf_tensor("ksb2", [64, 4096], BF16))
        env["sv2"] = b_(nc.sbuf_tensor("sv2", [128, 2080], BF16))
        env["sub"] = b_(nc.sbuf_tensor("sub", [128, 6016], BF16))
        env["Eb"] = b_(nc.sbuf_tensor("Eb", [128, 32 * 512], BF16))
        env["escr"] = b_(nc.sbuf_tensor("escr2", [128, 1024], BF16))
        env["onum"] = b_(nc.sbuf_tensor("onum", [64, 512], F32))
        env["of32"] = b_(nc.sbuf_tensor("of32", [64, 512], F32))
        env["osb"] = b_(nc.sbuf_tensor("osb", [64, 512], BF16))
        env["srcp"] = b_(nc.sbuf_tensor("srcp2", [1, 512], F32))
        env["sq2b"] = b_(nc.sbuf_tensor("sq2b", [64, 512], BF16))
        env["rnk"] = b_(nc.sbuf_tensor("rnk", [1, 8], F32))
        env["mr2"] = b_(nc.sbuf_tensor("mr2", [1, 4], F32))
        env["snegm"] = b_(nc.sbuf_tensor("snegm", [128, 1], F32))
        ps = emit_l1attn_scores_fused(p, env, pa, [tagk, tagv])
        sB.close()
        p.barrier(ps)
        env = dict(env)
        sC = ExitStack()
        c_ = sC.enter_context
        env["swo"] = c_(nc.sbuf_tensor("swo", [64, 16 * 1024], BF16))
        env["so_r"] = c_(nc.sbuf_tensor("so_r", [64, 16 * 512], BF16))
        env["sxo"] = c_(nc.sbuf_tensor("sxo", [128, 1024], F32))
        x_upd5 = emit_l1attn_wo_fused(p, env, ps, sx)
        fin = fin5 = list(x_upd5.values())
        sC.close()
        p.barrier(fin5)

        if nstages >= 6:
            fin = fin6 = l1ffn_stage(False, fin5, "z6")

        st = p.dma("gpsimd", x_out.rearrange("(c p) t -> p c t", p=128),
                   sx[:, :].rearrange("p (c t) -> p c t", c=NCH),
                   wait=fin, stream="st")
        p.op("gpsimd", lambda E: E.engine_nop(), wait=[st])
        p.finish(ctx)
    P.STREAM_ALIAS = {}
    return nc


def _env_common(nc, ctx, env):
    e = ctx.enter_context
    env["ones_f1024"] = e(nc.sbuf_tensor("ones_f1024", [128, 1], F32))
    env["ones_b1024"] = e(nc.sbuf_tensor("ones_b1024", [128, 1], BF16))
    env["ones_r"] = e(nc.sbuf_tensor("ones_r", [1, 128], F32))
    env["eps"] = e(nc.sbuf_tensor("eps", [1, 1], F32))
    env["identf"] = e(nc.sbuf_tensor("identf", [128, 128], F32))
    env["row"] = e(nc.sbuf_tensor("row", [1, 2048], F32))
    env["xsq"] = e(nc.sbuf_tensor("xsq", [128, 1024], BF16))
    env["lnsc"] = e(nc.sbuf_tensor("lnsc", [128, 512], F32))
    env["pb"] = [e(nc.psum_tensor(f"pb{i}", [128, 512], F32)) for i in range(8)]


def build_l0ffn(dbg=None):
    nc = bass.Bass("TRN2", target_bir_lowering=False)
    x_in = nc.dram_tensor("x", [D, T], F32, kind="ExternalInput").ap()
    x_out = nc.dram_tensor("xo", [D, T], F32, kind="ExternalOutput").ap()
    hgw = nc.dram_tensor("gw", [D, NE], F32, kind="ExternalInput").ap()
    hgb = nc.dram_tensor("gb", [NE], F32, kind="ExternalInput").ap()
    hwg = nc.dram_tensor("wg", [10, D, IE], F32, kind="ExternalInput").ap()
    hwu = nc.dram_tensor("wu", [10, D, IE], F32, kind="ExternalInput").ap()
    hwd = nc.dram_tensor("wd", [10, IE, D], F32, kind="ExternalInput").ap()
    wdram = nc.dram_tensor("wdram", [NE * T], F32, kind="Internal").ap()
    qg = nc.dram_tensor("qg", [10, 128, 8 * 512], BF16, kind="Internal").ap()
    qu = nc.dram_tensor("qu", [10, 128, 8 * 512], BF16, kind="Internal").ap()
    qd = nc.dram_tensor("qd", [10, 128, 4 * 8 * 128], BF16, kind="Internal").ap()

    env = {"hwg": hwg, "hwu": hwu, "hwd": hwd, "wdram": wdram,
           "qdram_g": qg, "qdram_u": qu, "qdram_d": qd}
    with ExitStack() as ctx:
        e = ctx.enter_context
        _env_common(nc, ctx, env)
        env["cidx8"] = e(nc.sbuf_tensor("cidx8", [128, 128], F32))
        env["cidx4"] = e(nc.sbuf_tensor("cidx4", [128, 64], F32))
        env["sgw"] = e(nc.sbuf_tensor("sgw", [128, 64], BF16))
        env["sgb"] = e(nc.sbuf_tensor("sgb", [8, 1], F32))
        env["sfcT"] = e(nc.sbuf_tensor("sfcT", [8, T], F32))
        env["sgate_w"] = 1536
        env["sgate"] = e(nc.sbuf_tensor("sgate", [128, 1536], F32))
        env["cidx8P"] = e(nc.sbuf_tensor("cidx8P", [128, 128], F32))
        env["cidx4P"] = e(nc.sbuf_tensor("cidx4P", [128, 64], F32))
        env["swT"] = e(nc.sbuf_tensor("swT", [16, NE * 128], F32))
        env["swrow"] = e(nc.sbuf_tensor("swrow", [1, 1024], F32))
        sx = e(nc.sbuf_tensor("sx", [128, NCH * T], F32))
        sxn = e(nc.sbuf_tensor("sxn", [128, NCH * T], BF16))
        env["swg"] = e(nc.sbuf_tensor("swgb", [128, 2 * 8 * 512], BF16))
        env["swu"] = e(nc.sbuf_tensor("swub", [128, 2 * 8 * 512], BF16))
        env["swd"] = e(nc.sbuf_tensor("swdb", [128, 2 * 4 * 8 * 128], BF16))
        env["ssig"] = e(nc.sbuf_tensor("ssig", [128, 1024], BF16))
        env["stmp"] = e(nc.sbuf_tensor("stmp", [128, 512], BF16))
        env["shu"] = e(nc.sbuf_tensor("shu", [128, 4 * 512], BF16))

        p = P(nc)
        for s in ("q_sync", "q_gpsimd", "pe", "dv", "ac", "gp"):
            p.sem(s)
        cst = emit_consts(p, env)
        env["cst"] = cst
        tx = p.dma("gpsimd", sx[:, :].rearrange("p (c t) -> p c t", c=NCH),
                   x_in.rearrange("(c p) t -> p c t", p=128), stream="x")
        tgw = p.dma("gpsimd",
                    env["sgw"][:, :].rearrange("p (c e) -> p c e", c=NCH),
                    hgw.rearrange("(c p) e -> p c e", p=128), stream="gw")
        tgb = p.dma("gpsimd", env["sgb"][:, :], hgb[:, None], stream="gb")
        xn_tok, ln_done = emit_ln(
            p, env, sx, sxn, T,
            {"mean": env["pb"][6], "msq": env["pb"][5],
             "bA": env["pb"][4], "bB": env["pb"][3]},
            wait=[tx, cst, tgw, tgb])
        if dbg == "ln":
            st = p.dma("gpsimd", x_out.rearrange("(c p) t -> p c t", p=128),
                       sxn[:, :].rearrange("p (c t) -> p c t", c=NCH),
                       wait=[ln_done], stream="st")
        elif dbg in ("gate", "gate1", "gate2", "gate3"):
            gts = emit_l0_gating(p, env, sxn, xn_tok, env["pb"][2],
                                 env["pb"][1], dbg=dbg)
            if dbg == "gate1":
                st = p.dma("gpsimd", x_out[0:8, :], env["sfcT"][0:8, :],
                           wait=gts, stream="st")
                p.op("gpsimd", lambda E: E.engine_nop(), wait=[st])
                p.finish(ctx)
                return nc
            if dbg in ("gate2", "gate3"):
                st = p.dma("gpsimd", x_out[0:128, 0:1536],
                           env["sgate"][:, :], wait=gts, stream="st")
                p.op("gpsimd", lambda E: E.engine_nop(), wait=[st])
                p.finish(ctx)
                return nc
            st = None
            for e_ in range(8):
                tld = p.dma("gpsimd", env["sfcT"][0:1, :],
                            env["wdram"][e_ * T:(e_ + 1) * T],
                            wait=gts + [st], stream="dbg")
                st = p.dma("gpsimd", x_out[e_:e_ + 1, :], env["sfcT"][0:1, :],
                           wait=[tld], stream="st")
        else:
            x_upd = emit_l0ffn(p, env, sx, sxn, xn_tok, ln_done, True)
            st = p.dma("gpsimd", x_out.rearrange("(c p) t -> p c t", p=128),
                       sx[:, :].rearrange("p (c t) -> p c t", c=NCH),
                       wait=list(x_upd.values()), stream="st")
        p.op("gpsimd", lambda E: E.engine_nop(), wait=[st])
        p.finish(ctx)
    return nc


def build_l0attn(dbg=None):
    nc = bass.Bass("TRN2", target_bir_lowering=False)
    x_in = nc.dram_tensor("x", [D, T], F32, kind="ExternalInput").ap()
    xp_in = nc.dram_tensor("xp", [D, T], F32, kind="ExternalInput").ap()
    x_out = nc.dram_tensor("xo", [D, T], F32, kind="ExternalOutput").ap()
    hwqT = nc.dram_tensor("wqT", [D, D], F32, kind="ExternalInput").ap()
    hwkT = nc.dram_tensor("wkT", [D, D], F32, kind="ExternalInput").ap()
    hwvT = nc.dram_tensor("wvT", [D, D], F32, kind="ExternalInput").ap()
    hwoT = nc.dram_tensor("woT", [D, D], F32, kind="ExternalInput").ap()
    hmask_in = nc.dram_tensor("hmask", [128, 1], F32, kind="ExternalInput").ap()
    xnp_dram = nc.dram_tensor("xnp", [D, T], BF16, kind="Internal").ap()

    env = {"hwqT": hwqT, "hwkT": hwkT, "hwvT": hwvT, "hwoT": hwoT}
    with ExitStack() as ctx:
        e = ctx.enter_context
        _env_common(nc, ctx, env)
        env["ones_r64"] = e(nc.sbuf_tensor("ones_r64", [128, 64], F32))
        env["sUT"] = e(nc.sbuf_tensor("sUT", [128, 128], BF16))
        env["hmask"] = e(nc.sbuf_tensor("hmaskb", [128, 1], F32))
        sx = e(nc.sbuf_tensor("sx", [128, NCH * T], F32))
        sxn = e(nc.sbuf_tensor("sxn", [128, NCH * T], BF16))
        sxpf = e(nc.sbuf_tensor("sxpf", [128, 4 * 512], F32))
        sstg = e(nc.sbuf_tensor("sstg", [128, 2 * 512], BF16))
        env["sQ"] = e(nc.sbuf_tensor("sQ", [128, 2 * T], BF16))
        env["sKT"] = e(nc.sbuf_tensor("sKT", [128, 2 * T], BF16))
        env["sKtok"] = e(nc.sbuf_tensor("sKtok", [128, 16 * 128], BF16))
        env["sV_w"] = 16 * 130
        env["sV"] = e(nc.sbuf_tensor("sV", [128, 16 * 130], BF16))
        env["sxp"] = e(nc.sbuf_tensor("sxp", [128, 2 * 1024], BF16))
        env["sS"] = e(nc.sbuf_tensor("sS", [128, 130], BF16))
        env["sSf"] = e(nc.sbuf_tensor("sSf", [128, 130], F32))
        env["sAm"] = e(nc.sbuf_tensor("sAm", [128, 256], BF16))
        env["srcp"] = e(nc.sbuf_tensor("srcp", [1, 256], F32))
        env["sOutE"] = e(nc.sbuf_tensor("sOutE", [64, 256], BF16))
        env["sAtt"] = e(nc.sbuf_tensor("sAtt", [64, 2 * T], BF16))
        env["escr"] = e(nc.sbuf_tensor("escr", [128, 1024], F32))
        env["sw_w"] = 2 * 5 * 8 * 128
        env["swpair"] = e(nc.sbuf_tensor("swpair", [128, env["sw_w"]], BF16))

        p = P(nc)
        for s in ("q_sync", "q_gpsimd", "pe", "dv", "ac", "gp"):
            p.sem(s)
        cst = emit_consts(p, env)
        env["cst"] = cst
        if dbg != "lnpA":
            p.op("gpsimd", lambda E: E.memset(env["ones_r64"][:, :], 1.0))
            t_ut = p.op("gpsimd", lambda E: E.memset(env["sUT"][:, :], 1.0),
                        inc="gp")
            t_ut = p.op("gpsimd", lambda E: E.affine_select(
                env["sUT"][:, :], env["sUT"][:, :], [[1, 128]], ALU.is_ge, 0.0,
                channel_multiplier=-1), wait=[t_ut], inc="gp")
            t_v1 = p.op("gpsimd", lambda E: E.memset(
                env["sV"][:, :], 1.0), inc="gp")
            p.op("gpsimd", lambda E: E.memset(env["sQ"][64:128, :], 0.0))
            t_v1 = p.op("gpsimd", lambda E: E.memset(
                env["sKT"][64:128, :], 0.0), inc="gp")
            thm = p.dma("gpsimd", env["hmask"][:, :], hmask_in, stream="hm")
        else:
            t_ut = t_v1 = thm = None
        tx = p.dma("gpsimd", sx[:, :].rearrange("p (c t) -> p c t", c=NCH),
                   x_in.rearrange("(c p) t -> p c t", p=128), stream="x")
        xn_tok, ln_done = emit_ln(
            p, env, sx, sxn, T,
            {"mean": env["pb"][6], "msq": env["pb"][5],
             "bA": env["pb"][4], "bB": env["pb"][3]},
            wait=[tx, cst, t_ut, thm])
        slot_busy = {}
        slot_seq = [0]
        slot_of = {}
        slot_last = {}
        def part_provider(ch, ng, phase):
            slot = slot_seq[0] % 4
            slot_seq[0] += 1
            slot_of[(ch, ng, phase)] = slot
            buf = sxpf[:, slot * 512:(slot + 1) * 512]
            tok = p.dma("sync", buf,
                        xp_in[ch * 128:(ch + 1) * 128,
                              ng * 512:(ng + 1) * 512],
                        wait=list(slot_busy.get(slot, [])) +
                        [slot_last.get(slot)], stream="xpf%d" % slot)
            slot_last[slot] = tok
            return buf, tok
        def part_done(ch, ng, phase, tok):
            slot_busy[slot_of[(ch, ng, phase)]] = tok
        dst_dma = {}
        def part_dst(ch, ng):
            slot = ch % 2
            ds = sstg[:, slot * 512:(slot + 1) * 512]
            dwait = [dst_dma.get(slot)]
            def fin(tok):
                dt = p.dma("gpsimd",
                           xnp_dram[ch * 128:(ch + 1) * 128,
                                    ng * 512:(ng + 1) * 512], ds,
                           wait=[tok], stream="xnp%d" % slot)
                dst_dma[slot] = dt
                return dt
            return ds, dwait, fin
        _, lnp_done = emit_ln(
            p, env, None, None, T,
            {"mean": env["pb"][6], "msq": env["pb"][5],
             "bA": env["pb"][4], "bB": env["pb"][3]},
            wait=[cst, ln_done], provider=part_provider,
            dst_provider=part_dst, provider_done=part_done)
        env["v1_tok"] = t_v1
        if dbg == "lnpA":
            st = p.dma("gpsimd", x_out.rearrange("(c p) t -> p c t", p=128),
                       sxn[:, :].rearrange("p (c t) -> p c t", c=NCH),
                       wait=[ln_done], stream="st")
            p.op("gpsimd", lambda E: E.engine_nop(), wait=[st])
            p.finish(ctx)
            return nc
        if dbg == "lnp0":
            tw9 = p.dma("gpsimd", xnp_dram.rearrange("(c p) t -> p c t", p=128),
                        sxn[:, :].rearrange("p (c t) -> p c t", c=NCH),
                        wait=[ln_done], stream="xnp0")
            st = None
            for ch in range(NCH):
                tl = p.dma("gpsimd", sstg[:, 0:256],
                           xnp_dram[ch * 128:(ch + 1) * 128, 0:256],
                           wait=[tw9, st], stream="dbg")
                st = p.dma("gpsimd",
                           x_out[ch * 128:(ch + 1) * 128, 0:256],
                           sstg[:, 0:256], wait=[tl], stream="st")
            p.op("gpsimd", lambda E: E.engine_nop(), wait=[st])
            p.finish(ctx)
            return nc
        if dbg == "lnp":
            st = None
            for ch in range(NCH):
                tl = p.dma("gpsimd", sstg[:, 0:256],
                           xnp_dram[ch * 128:(ch + 1) * 128, 0:256],
                           wait=[dst_dma.get(0), dst_dma.get(1), lnp_done, st],
                           stream="dbg")
                st = p.dma("gpsimd",
                           x_out[ch * 128:(ch + 1) * 128, 0:256],
                           sstg[:, 0:256], wait=[tl], stream="st")
            p.op("gpsimd", lambda E: E.engine_nop(), wait=[st])
            p.finish(ctx)
            return nc
        nph = {"p1a": 1, "p1b": 2, "p1c": 3}.get(dbg, 4)
        x_att = emit_l0attn(p, env, sx, sxn, xn_tok, xnp_dram,
                            [dst_dma.get(0), dst_dma.get(1), lnp_done],
                            ln_done,
                            npairs=(1 if (dbg or "").startswith("p1") else 8),
                            nphase=nph)
        st = p.dma("gpsimd", x_out.rearrange("(c p) t -> p c t", p=128),
                   sx[:, :].rearrange("p (c t) -> p c t", c=NCH),
                   wait=list(x_att.values()), stream="st")
        p.op("gpsimd", lambda E: E.engine_nop(), wait=[st])
        p.finish(ctx)
    return nc


def build_l1ffn(first_call=True):
    nc = bass.Bass("TRN2", target_bir_lowering=False)
    x_in = nc.dram_tensor("x", [D, T], F32, kind="ExternalInput").ap()
    x_out = nc.dram_tensor("xo", [D, T], F32, kind="ExternalOutput").ap()
    hl1g = nc.dram_tensor("l1g", [I1, D], F32, kind="ExternalInput").ap()
    hl1u = nc.dram_tensor("l1u", [I1, D], F32, kind="ExternalInput").ap()
    hl1d = nc.dram_tensor("l1d", [D, I1], F32, kind="ExternalInput").ap()
    ql1g = nc.dram_tensor("ql1g", [I1, D], BF16, kind="Internal").ap()
    ql1u = nc.dram_tensor("ql1u", [I1, D], BF16, kind="Internal").ap()
    ql1d = nc.dram_tensor("ql1d", [D, I1], BF16, kind="Internal").ap()
    env = {"hl1g": hl1g, "hl1u": hl1u, "hl1d": hl1d,
           "ql1g": ql1g, "ql1u": ql1u, "ql1d": ql1d}
    with ExitStack() as ctx:
        e = ctx.enter_context
        env["ones_f1024"] = e(nc.sbuf_tensor("ones_f1024", [128, 1], F32))
        env["ones_b1024"] = e(nc.sbuf_tensor("ones_b1024", [128, 1], BF16))
        env["ones_r"] = e(nc.sbuf_tensor("ones_r", [1, 128], F32))
        env["eps"] = e(nc.sbuf_tensor("eps", [1, 1], F32))
        env["identf"] = e(nc.sbuf_tensor("identf", [128, 128], F32))
        env["identb"] = e(nc.sbuf_tensor("identb", [128, 128], BF16))
        env["row"] = e(nc.sbuf_tensor("row", [1, 2048], F32))
        env["xsq"] = e(nc.sbuf_tensor("xsq", [128, 1024], BF16))
        env["lnsc"] = e(nc.sbuf_tensor("lnsc", [128, 512], F32))
        env["pb"] = [e(nc.psum_tensor(f"pb{i}", [128, 512], F32))
                     for i in range(7)]
        env["pbt"] = e(nc.psum_tensor("pbt", [128, 1024], BF16))
        sx = e(nc.sbuf_tensor("sx", [128, NCH * T], F32))
        sxn = e(nc.sbuf_tensor("sxn", [128, NCH * T], BF16))
        env["sxq"] = e(nc.sbuf_tensor("sxq", [128, 8 * 512], BF16))
        env["shq"] = e(nc.sbuf_tensor("shq", [128, 22 * 512], BF16))
        env["sxt"] = e(nc.sbuf_tensor("sxt", [128, 1024], BF16))
        env["sxtq"] = e(nc.sbuf_tensor("sxtq", [128, 1024], BF16))
        env["ssil"] = e(nc.sbuf_tensor("ssil", [128, 512], F32))
        env["shts"] = e(nc.sbuf_tensor("shts", [128, 512], F32))
        env["shtq"] = e(nc.sbuf_tensor("shtq", [128, 512], BF16))
        env["swg1"] = e(nc.sbuf_tensor("swg1", [128, 2 * 8 * 512], BF16))
        env["swu1"] = e(nc.sbuf_tensor("swu1", [128, 2 * 8 * 512], BF16))
        env["swd1"] = e(nc.sbuf_tensor("swd1", [128, 512], BF16))
        env["qwbuf"] = e(nc.sbuf_tensor("qwbuf", [128, 1024], F32))
        env["qwout"] = e(nc.sbuf_tensor("qwout", [128, 1024], BF16))
        env["qz"] = e(nc.sbuf_tensor("qz", [128, 512], F32))
        env["qa"] = e(nc.sbuf_tensor("qa", [128, 512], F32))
        env["qb"] = e(nc.sbuf_tensor("qb", [128, 512], F32))
        env["qs"] = e(nc.sbuf_tensor("qs", [128, 512], F32))
        env["qbm_w"] = 32
        env["qbm"] = e(nc.sbuf_tensor("qbm", [128, 32], F32))
        env["qsc_w"] = 32
        env["qsc"] = e(nc.sbuf_tensor("qsc", [128, 32], F32))

        p = P(nc)
        for s in ("q_sync", "q_gpsimd", "pe", "dv", "ac", "gp"):
            p.sem(s)
        cst = emit_consts(p, env)
        env["cst"] = cst
        tx = p.dma("gpsimd", sx[:, :].rearrange("p (c t) -> p c t", c=NCH),
                   x_in.rearrange("(c p) t -> p c t", p=128), stream="x")
        wq_toks = emit_l1_wquant(p, env, None)
        xn_tok, ln_done = emit_ln(
            p, env, sx, sxn, T,
            {"mean": env["pb"][2], "msq": env["pb"][3],
             "bA": env["pb"][4], "bB": env["pb"][5]},
            wait=[tx, cst])
        x_upd = emit_l1ffn(p, env, sx, sxn, xn_tok, ln_done, True, wq_toks)
        st = p.dma("gpsimd", x_out.rearrange("(c p) t -> p c t", p=128),
                   sx[:, :].rearrange("p (c t) -> p c t", c=NCH),
                   wait=list(x_upd.values()), stream="st")
        p.op("gpsimd", lambda E: E.engine_nop(), wait=[st])
        p.finish(ctx)
    return nc


def build_l1attn(dbg=None):
    nc = bass.Bass("TRN2", target_bir_lowering=False)
    x_in = nc.dram_tensor("x", [D, T], F32, kind="ExternalInput").ap()
    xg0_in = nc.dram_tensor("xg0", [D, T], F32, kind="ExternalInput").ap()
    xg1_in = nc.dram_tensor("xg1", [D, T], F32, kind="ExternalInput").ap()
    x_out = nc.dram_tensor("xo", [D, T], F32, kind="ExternalOutput").ap()
    hwqT = nc.dram_tensor("wqT", [D, D], F32, kind="ExternalInput").ap()
    hwkT = nc.dram_tensor("wkT", [D, D], F32, kind="ExternalInput").ap()
    hwvT = nc.dram_tensor("wvT", [D, D], F32, kind="ExternalInput").ap()
    hwoT = nc.dram_tensor("woT", [D, D], F32, kind="ExternalInput").ap()
    ub_in = nc.dram_tensor("ub", [16, 6144], F32, kind="ExternalInput").ap()
    bq_in = nc.dram_tensor("bq", [64, 16], F32, kind="ExternalInput").ap()
    bk_in = nc.dram_tensor("bk", [64, 16], F32, kind="ExternalInput").ap()
    bv_in = nc.dram_tensor("bv", [64, 16], F32, kind="ExternalInput").ap()
    bo_in = nc.dram_tensor("bo", [128, 8], F32, kind="ExternalInput").ap()
    ubexp = nc.dram_tensor("ubexp", [16, 6144], BF16, kind="Internal").ap()
    qsid = nc.dram_tensor("qsid", [16, 64, 2048], BF16, kind="Internal").ap()
    ksid = nc.dram_tensor("ksid", [16, 64, 4096], BF16, kind="Internal").ap()
    vsid = nc.dram_tensor("vsid", [16, 128, 2080], BF16, kind="Internal").ap()
    osid = nc.dram_tensor("osid", [16, 64, 2048], BF16, kind="Internal").ap()
    env = {"hwqT": hwqT, "hwkT": hwkT, "hwvT": hwvT, "hwoT": hwoT,
           "ub_in": ub_in, "ubexp": ubexp, "qsid": qsid, "ksid": ksid,
           "vsid": vsid, "osid": osid, "x_in": x_in, "xo": x_out}
    with ExitStack() as ctx:
        e = ctx.enter_context
        env["ones_f1024"] = e(nc.sbuf_tensor("ones_f1024", [128, 1], F32))
        env["ones_b1024"] = e(nc.sbuf_tensor("ones_b1024", [128, 1], BF16))
        env["ones_r"] = e(nc.sbuf_tensor("ones_r", [1, 128], F32))
        env["eps"] = e(nc.sbuf_tensor("eps", [1, 1], F32))
        env["identf"] = e(nc.sbuf_tensor("identf", [128, 128], F32))
        env["onesc"] = e(nc.sbuf_tensor("onesc", [64, 1], BF16))
        env["szero"] = e(nc.sbuf_tensor("szero", [1, 1], F32))
        env["snegm"] = e(nc.sbuf_tensor("snegm", [128, 16], F32))
        env["sbq"] = e(nc.sbuf_tensor("sbq", [64, 16], F32))
        env["sbqs"] = e(nc.sbuf_tensor("sbqs", [64, 16], F32))
        env["sbk"] = e(nc.sbuf_tensor("sbk", [64, 16], F32))
        env["sbv"] = e(nc.sbuf_tensor("sbv", [64, 16], F32))
        env["sbo"] = e(nc.sbuf_tensor("sbo", [128, 8], F32))
        env["pb"] = [e(nc.psum_tensor(f"pb{i}", [128, 512], F32))
                     for i in range(8)]
        p = P(nc)
        for s in ("q_sync", "q_gpsimd", "pe", "dv", "ac", "gp"):
            p.sem(s)

        stackA = ExitStack()
        eA = stackA.enter_context
        env["row"] = eA(nc.sbuf_tensor("row", [1, 2048], F32))
        env["xsq"] = eA(nc.sbuf_tensor("xsq", [128, 1024], BF16))
        env["lnsc"] = eA(nc.sbuf_tensor("lnsc", [128, 512], F32))
        sxpf = eA(nc.sbuf_tensor("sxpf", [128, 4 * 512], F32))
        env["sxn"] = eA(nc.sbuf_tensor("sxn", [128, NCH * T], BF16))
        env["sxg"] = eA(nc.sbuf_tensor("sxg", [128, NCH * KT], BF16))
        env["swq"] = eA(nc.sbuf_tensor("swq", [128, 1024], BF16))
        env["swk"] = eA(nc.sbuf_tensor("swk", [128, 1024], BF16))
        env["swv"] = eA(nc.sbuf_tensor("swv", [128, 1024], BF16))
        env["qsb"] = eA(nc.sbuf_tensor("qsb", [64, 2048], BF16))
        env["ksb"] = eA(nc.sbuf_tensor("ksb", [64, 4096], BF16))
        env["sv"] = eA(nc.sbuf_tensor("sv", [128, 2080], BF16))
        env["sq2"] = eA(nc.sbuf_tensor("sq2", [64, 512], BF16))
        env["rnorm"] = eA(nc.sbuf_tensor("rnorm", [1, 12], F32))
        env["negrow"] = eA(nc.sbuf_tensor("negrow", [1, 16], F32))
        env["mrow"] = eA(nc.sbuf_tensor("mrow", [1, 4], F32))
        env["subf"] = eA(nc.sbuf_tensor("subf", [16, 6144], F32))
        env["subb"] = eA(nc.sbuf_tensor("subb", [16, 6144], BF16))

        cst = emit_consts(p, env)
        p.op("gpsimd", lambda E: E.memset(env["onesc"][:, :], 1.0))
        cst = p.op("gpsimd", lambda E: E.memset(env["szero"][:, :], 0.0),
                   inc="gp")
        env["cst"] = cst
        env["bq_tok"] = p.dma("gpsimd", env["sbq"][:, :], bq_in, stream="bq")
        env["bk_tok"] = p.dma("gpsimd", env["sbk"][:, :], bk_in, stream="bk")
        env["bv_tok"] = p.dma("gpsimd", env["sbv"][:, :], bv_in, stream="bv")
        env["bo_tok"] = p.dma("gpsimd", env["sbo"][:, :], bo_in, stream="bo")

        slot_busy, slot_seq, slot_of, slot_last = {}, [0], {}, {}

        def mk_prov(srcd, tag):
            def prov(ch, ng, phase):
                slot = slot_seq[0] % 4
                slot_seq[0] += 1
                slot_of[(tag, ch, ng, phase)] = slot
                buf = sxpf[:, slot * 512:(slot + 1) * 512]
                tok = p.dma("sync", buf,
                            srcd[ch * 128:(ch + 1) * 128,
                                 ng * 512:(ng + 1) * 512],
                            wait=list(slot_busy.get(slot, [])) +
                            [slot_last.get(slot)], stream="xpf%d" % slot)
                slot_last[slot] = tok
                return buf, tok

            def done(ch, ng, phase, toks):
                slot_busy[slot_of[(tag, ch, ng, phase)]] = toks
            return prov, done

        def mk_dst(off):
            def dstp(ch, ng):
                c0 = ch * KT + off + ng * 512
                return env["sxg"][:, c0:c0 + 512], [], None
            return dstp

        lnb = {"mean": env["pb"][2], "msq": env["pb"][3],
               "bA": env["pb"][4], "bB": env["pb"][5]}
        pv, pd = mk_prov(x_in, 0)
        _, ln0 = emit_ln(p, env, None, env["sxn"], T, lnb,
                         wait=[cst], provider=pv, provider_done=pd)
        pv, pd = mk_prov(xg0_in, 1)
        _, ln1 = emit_ln(p, env, None, None, T, lnb, wait=[cst, ln0],
                         provider=pv, dst_provider=mk_dst(0),
                         provider_done=pd)
        pv, pd = mk_prov(xg1_in, 2)
        _, ln2 = emit_ln(p, env, None, None, T, lnb, wait=[cst, ln1],
                         provider=pv, dst_provider=mk_dst(2048),
                         provider_done=pd)
        env["lnlast"] = ln2
        if dbg == "ln":
            st = p.dma("gpsimd", x_out.rearrange("(c p) t -> p c t", p=128),
                       env["sxn"][:, :].rearrange("p (c t) -> p c t", c=NCH),
                       wait=[ln2], stream="st")
            p.op("gpsimd", lambda E: E.engine_nop(), wait=[st])
            p.finish(ctx)
            stackA.close()
            return nc
        pa = emit_l1attn_projA(p, env)
        if dbg in ("projq", "projk"):
            srcb = env["qsb"] if dbg == "projq" else env["ksb"]
            w = 2048 if dbg == "projq" else 4096
            st = p.dma("gpsimd", bass.AP(tap(x_out), 0, [[w, 64], [1, w]]),
                       srcb[0:64, 0:w], wait=pa, stream="st")
            p.op("gpsimd", lambda E: E.engine_nop(), wait=[st])
            p.finish(ctx)
            stackA.close()
            return nc
        stackA.close()

        stackB = ExitStack()
        eB = stackB.enter_context
        env["qsb2"] = eB(nc.sbuf_tensor("qsb2", [64, 2048], BF16))
        env["ksb2"] = eB(nc.sbuf_tensor("ksb2", [64, 4096], BF16))
        env["sv2"] = eB(nc.sbuf_tensor("sv2", [128, 2080], BF16))
        env["sub"] = eB(nc.sbuf_tensor("sub", [128, 6016], BF16))
        env["Eb"] = eB(nc.sbuf_tensor("Eb", [128, 32 * 512], BF16))
        env["escr"] = eB(nc.sbuf_tensor("escr", [128, 1024], BF16))
        env["onum"] = eB(nc.sbuf_tensor("onum", [64, 512], F32))
        env["of32"] = eB(nc.sbuf_tensor("of32", [64, 512], F32))
        env["osb"] = eB(nc.sbuf_tensor("osb", [64, 512], BF16))
        env["srcp"] = eB(nc.sbuf_tensor("srcp", [1, 512], F32))
        ps = emit_l1attn_scores(p, env, pa)
        stackB.close()
        if dbg == "o":
            st = p.dma("gpsimd", bass.AP(tap(x_out), 0,
                                         [[2048, 1024], [1, 2048]]),
                       bass.AP(tap(osid), 0, [[2048, 1024], [1, 2048]]),
                       wait=ps, stream="st")
            p.op("gpsimd", lambda E: E.engine_nop(), wait=[st])
            p.finish(ctx)
            return nc

        env["swo"] = e(nc.sbuf_tensor("swo", [64, 16 * 1024], BF16))
        env["so_r"] = e(nc.sbuf_tensor("so_r", [64, 16 * 512], BF16))
        env["sxi"] = e(nc.sbuf_tensor("sxi", [128, 1024], F32))
        env["sxo"] = e(nc.sbuf_tensor("sxo", [128, 1024], F32))
        xst = emit_l1attn_wo(p, env, ps)
        p.op("gpsimd", lambda E: E.engine_nop(), wait=xst)
        p.finish(ctx)
    return nc





def _prep_l0ffn_inputs(inputs):
    swg = np.ascontiguousarray(inputs["L0_ffn_sw_gate"], dtype=np.float32)
    swu = np.ascontiguousarray(inputs["L0_ffn_sw_up"], dtype=np.float32)
    swd = np.ascontiguousarray(inputs["L0_ffn_sw_down"], dtype=np.float32)
    wg = np.asarray(inputs["L0_ffn_w_gate"], np.float32)
    wu = np.asarray(inputs["L0_ffn_w_up"], np.float32)
    wd = np.asarray(inputs["L0_ffn_w_down"], np.float32)
    wg10 = np.concatenate(
        [wg, swg[:, 0:512][None], swg[:, 512:1024][None]], axis=0)
    wu10 = np.concatenate(
        [wu, swu[:, 0:512][None], swu[:, 512:1024][None]], axis=0)
    wd10 = np.concatenate(
        [wd, swd[0:512, :][None], swd[512:1024, :][None]], axis=0)
    return {
        "gw": np.ascontiguousarray(inputs["L0_ffn_gate_w"], dtype=np.float32),
        "gb": np.ascontiguousarray(inputs["L0_ffn_gate_bias"],
                                   dtype=np.float32),
        "wg": np.ascontiguousarray(wg10),
        "wu": np.ascontiguousarray(wu10),
        "wd": np.ascontiguousarray(wd10),
    }


def _shard_x(x_full):
    out = []
    for c in range(8):
        b, h = c // 2, c % 2
        sl = x_full[b, h * T:(h + 1) * T, :]
        out.append(np.ascontiguousarray(sl.T.astype(np.float32)))
    return out


def _unshard_x(shards):
    x = np.empty((4, 4096, D), np.float32)
    for c in range(8):
        b, h = c // 2, c % 2
        x[b, h * T:(h + 1) * T, :] = shards[c].T
    return x


def _prep_l0attn_inputs(inputs):
    return {
        "wqT": np.ascontiguousarray(
            np.asarray(inputs["L0_attn_Wq"], np.float32).T),
        "wkT": np.ascontiguousarray(
            np.asarray(inputs["L0_attn_Wk"], np.float32).T),
        "wvT": np.ascontiguousarray(
            np.asarray(inputs["L0_attn_Wv"], np.float32).T),
        "woT": np.ascontiguousarray(
            np.asarray(inputs["L0_attn_Wo"], np.float32).T),
    }


def _build_wq_slabs(inputs):
    g = np.asarray(inputs["L1_ffn_w_gate"], np.float32)
    u = np.asarray(inputs["L1_ffn_w_up"], np.float32)
    d = np.asarray(inputs["L1_ffn_w_down"], np.float32)
    tiles = []
    for src, nit, nsub in ((g, 22, 8), (u, 22, 8), (d, 8, 22)):
        for it in range(nit):
            for sub in range(nsub):
                tiles.append(src[it * 128:(it + 1) * 128,
                                 sub * 128:(sub + 1) * 128])
    A = np.concatenate(tiles, axis=1)
    W = QU_PER_CORE * 128
    return [np.ascontiguousarray(A[:, c * W:(c + 1) * W]) for c in range(8)]


def _prep_fused_inputs(inputs):
    base = {}
    base.update(_prep_l0ffn_inputs(inputs))
    base.update(_prep_l0attn_inputs(inputs))
    a, ubs = _prep_l1attn_inputs(inputs)
    base["aqT"] = a["wqT"]
    base["akT"] = a["wkT"]
    base["avT"] = a["wvT"]
    base["aoT"] = a["woT"]
    for k in ("bq", "bk", "bv", "bo"):
        base[k] = a[k]
    return base, ubs


def _prep_l1attn_inputs(inputs):
    pbias = np.ascontiguousarray(inputs["L1_attn_pos_bias"], np.float32)
    w = np.arange(6144)
    ubs = []
    for half in (0, 1):
        d = w - 2047 - 2048 * half
        idx = np.clip(4095 - d, 0, 4094).astype(np.int64)
        ubs.append(np.ascontiguousarray(pbias[:, idx], np.float32))
    base = {
        "wqT": np.ascontiguousarray(
            np.asarray(inputs["L1_attn_Wq"], np.float32).T),
        "wkT": np.ascontiguousarray(
            np.asarray(inputs["L1_attn_Wk"], np.float32).T),
        "wvT": np.ascontiguousarray(
            np.asarray(inputs["L1_attn_Wv"], np.float32).T),
        "woT": np.ascontiguousarray(
            np.asarray(inputs["L1_attn_Wo"], np.float32).T),
        "bq": np.ascontiguousarray(
            np.asarray(inputs["L1_attn_bq"], np.float32).reshape(16, 64).T),
        "bk": np.ascontiguousarray(
            np.asarray(inputs["L1_attn_bk"], np.float32).reshape(16, 64).T),
        "bv": np.ascontiguousarray(
            np.asarray(inputs["L1_attn_bv"], np.float32).reshape(16, 64).T),
        "bo": np.ascontiguousarray(
            np.asarray(inputs["L1_attn_bo"], np.float32).reshape(8, 128).T),
    }
    return base, ubs


def _prep_l1ffn_inputs(inputs):
    return {
        "l1g": np.ascontiguousarray(inputs["L1_ffn_w_gate"], dtype=np.float32),
        "l1u": np.ascontiguousarray(inputs["L1_ffn_w_up"], dtype=np.float32),
        "l1d": np.ascontiguousarray(inputs["L1_ffn_w_down"], dtype=np.float32),
    }


def kernel(dbg=None, **inputs):
    if FUSED and dbg is None:
        ncf = build_fused()
        base, ubs = _prep_fused_inputs(inputs)
        slabs = _build_wq_slabs(inputs)
        shards = _shard_x(np.asarray(inputs["L0_attn_x"], np.float32))
        in_maps = [dict(base, x=shards[c],
                        hmask=np.full((128, 1), c % 2, np.float32),
                        ub=ubs[c % 2], wqsrc=slabs[c]) for c in range(8)]
        res = run_bass_kernel_spmd(ncf, in_maps, core_ids=list(range(8)))
        return _unshard_x([res.results[c]["xo"] for c in range(8)])
    attn_dbg = dbg if dbg in ("lnp", "lnp0", "lnpA", "p1", "p1a", "p1b", "p1c") else None
    if attn_dbg:
        dbg = None
    x = np.asarray(inputs["L0_attn_x"], np.float32)
    shards = _shard_x(x)
    base = _prep_l0ffn_inputs(inputs)
    baseb = _prep_l0attn_inputs(inputs)
    nca = build_l0ffn(dbg)
    ncb = None
    if dbg is None and STOP_AFTER >= 2:
        ncb = build_l0attn(attn_dbg)
    ncd = nce = None
    if dbg is None and attn_dbg is None and STOP_AFTER >= 4:
        ncd = build_l1ffn()
        based = _prep_l1ffn_inputs(inputs)
    if dbg is None and attn_dbg is None and STOP_AFTER >= 5:
        nce = build_l1attn()
        basee, ubs = _prep_l1attn_inputs(inputs)
    in_maps = [dict(base, x=shards[c]) for c in range(8)]
    res = run_bass_kernel_spmd(nca, in_maps, core_ids=list(range(8)))
    shards = [res.results[c]["xo"] for c in range(8)]
    if ncb is not None:
        in_maps = [dict(baseb, x=shards[c], xp=shards[c ^ 1],
                        hmask=np.full((128, 1), float(c % 2), np.float32))
                   for c in range(8)]
        res = run_bass_kernel_spmd(ncb, in_maps, core_ids=list(range(8)))
        shards = [res.results[c]["xo"] for c in range(8)]
    if STOP_AFTER >= 3:
        in_maps = [dict(base, x=shards[c]) for c in range(8)]
        res = run_bass_kernel_spmd(nca, in_maps, core_ids=list(range(8)))
        shards = [res.results[c]["xo"] for c in range(8)]
    if ncd is not None:
        in_maps = [dict(based, x=shards[c]) for c in range(8)]
        res = run_bass_kernel_spmd(ncd, in_maps, core_ids=list(range(8)))
        shards = [res.results[c]["xo"] for c in range(8)]
    if nce is not None:
        in_maps = [dict(basee, x=shards[c], xg0=shards[2 * (c // 2)],
                        xg1=shards[2 * (c // 2) + 1], ub=ubs[c % 2])
                   for c in range(8)]
        res = run_bass_kernel_spmd(nce, in_maps, core_ids=list(range(8)))
        shards = [res.results[c]["xo"] for c in range(8)]
    if nce is not None and STOP_AFTER >= 6:
        in_maps = [dict(based, x=shards[c]) for c in range(8)]
        res = run_bass_kernel_spmd(ncd, in_maps, core_ids=list(range(8)))
        shards = [res.results[c]["xo"] for c in range(8)]
    return _unshard_x(shards)


STOP_AFTER = 99
FUSED = True

